# version 02
# baseline (speedup 1.0000x reference)
import numpy as np
import ml_dtypes
from contextlib import ExitStack

import concourse.bass as bass
import concourse.mybir as mybir
from concourse.bass_utils import run_bass_kernel_spmd

BF16 = mybir.dt.bfloat16
F32 = mybir.dt.float32
AF = mybir.ActivationFunctionType
ALU = mybir.AluOpType
AX = mybir.AxisListType

N_CORES = 8
B = 16
BC = B // N_CORES
S = 2048
D = 2048
DC = D // 128
TT_ = BC * S
NE = 8
F_MOE = 1408
FC_MOE = F_MOE // 128
I_FFN = 8192
LN_EPS = 1e-5


ENGINES = ("sync", "tensor", "vector", "scalar", "gpsimd")


class Sem:
    def __init__(self, k, name):
        self.h = k.ctx.enter_context(k.nc.semaphore(name))
        self.n = 0

    def bump(self, amount):
        self.n += amount
        return self.n


class K:

    def __init__(self, nc, ctx):
        self.nc = nc
        self.ctx = ctx
        self.q = {e: [] for e in ENGINES}
        self._sems = {}

    def sem(self, name):
        if name not in self._sems:
            self._sems[name] = Sem(self, name)
        return self._sems[name]

    def emit(self, eng, fn):
        self.q[eng].append(fn)

    def wait(self, eng, sem, val):
        if sem is None or val is None or val <= 0:
            return
        assert val <= sem.n, f"wait_ge({val}) but sem only reaches {sem.n}"
        self.emit(eng, lambda e, h=sem.h, v=val: e.wait_ge(h, v))

    def waits(self, eng, deps):
        for sem, val in deps:
            self.wait(eng, sem, val)

    def op(self, eng, fn, sem=None, amount=1):
        if sem is None:
            self.emit(eng, fn)
            return None
        tgt = sem.bump(amount)
        self.emit(eng, lambda e, fn=fn, h=sem.h, a=amount: fn(e).then_inc(h, a))
        return tgt

    def chain(self, eng, fn, sem):
        t = self.op(eng, fn, sem)
        self.wait(eng, sem, t)
        return t

    def dma(self, eng, out, in_, sem, **kw):
        return self.op(eng, lambda e, o=out, i=in_, kw=kw: e.dma_start(out=o, in_=i, **kw),
                       sem, 16)

    def _uniq(self, name):
        self._n = getattr(self, "_n", 0) + 1
        return f"{name}__{self._n}"

    def sbuf(self, name, shape, dtype):
        return self.ctx.enter_context(self.nc.sbuf_tensor(self._uniq(name), list(shape), dtype))

    def psum(self, name, shape, dtype=F32):
        return self.ctx.enter_context(self.nc.psum_tensor(self._uniq(name), list(shape), dtype))

    def fence_targets(self):
        return [(s, s.n) for s in self._sems.values() if s.n > 0]

    def playback(self, block):
        for eng in ENGINES:
            fns = self.q[eng]
            if fns:
                getattr(block, eng)(lambda e, fns=fns: [fn(e) for fn in fns])


class Slot:

    __slots__ = ("buf", "sem", "tgt", "ready_sem", "ready_tgt")

    def __init__(self, buf):
        self.buf = buf
        self.sem = None
        self.tgt = 0
        self.ready_sem = None
        self.ready_tgt = 0


class Rot:
    def __init__(self, bufs):
        self.slots = [Slot(b) for b in bufs]
        self.i = 0

    def next(self):
        s = self.slots[self.i % len(self.slots)]
        self.i += 1
        return s



def _c(a):
    return np.ascontiguousarray(a)


def host_constants():
    ident = np.eye(128, dtype=np.float32)
    tri = np.where(np.arange(128)[None, :] <= np.arange(128)[:, None], 0.0, -1e9).astype(np.float32)
    oh = np.zeros((8, 8 * 128), dtype=np.float32)
    for e in range(8):
        oh[e, e * 128:(e + 1) * 128] = 1.0
    ones_col = np.ones((128, 1), dtype=np.float32)
    half = 32
    inv_freq = 1.0 / 10000.0 ** (np.arange(0, half, dtype=np.float32) / half)
    t = np.arange(S, dtype=np.float32)
    freqs = np.einsum("i,j->ij", t, inv_freq)
    emb = np.concatenate([freqs, freqs], axis=-1)
    cos_t = _c(np.cos(emb).T.astype(np.float32))
    sin_t = _c(np.sin(emb).T.astype(np.float32))
    return dict(ident=ident, tri=tri, onehots=oh, ones_col=ones_col,
                cos_t=cos_t, sin_t=sin_t)


def host_layout(inputs):
    g = {}
    for nm, src in (("wq0", "L0_attn_Wq"), ("wk0", "L0_attn_Wk"),
                    ("wv0", "L0_attn_Wv"), ("wo0", "L0_attn_Wo"),
                    ("wq1", "L1_attn_Wq"), ("wk1", "L1_attn_Wk"),
                    ("wv1", "L1_attn_Wv"), ("wo1", "L1_attn_Wo")):
        g[nm] = _c(np.asarray(inputs[src]).T)
    g["rw"] = _c(np.asarray(inputs["L0_ffn_router_w"]).T)
    g["gw"] = _c(np.asarray(inputs["L0_ffn_gate_w"]).transpose(0, 2, 1))
    g["uw"] = _c(np.asarray(inputs["L0_ffn_up_w"]).transpose(0, 2, 1))
    g["dw"] = _c(np.asarray(inputs["L0_ffn_down_w"]).transpose(0, 2, 1))
    for nm, src in (("bq1", "L1_attn_bq"), ("bk1", "L1_attn_bk"),
                    ("bv1", "L1_attn_bv"), ("bo1", "L1_attn_bo")):
        g[nm] = _c(np.asarray(inputs[src]).reshape(DC, 128).T)
    g["wg1"] = _c(np.asarray(inputs["L1_ffn_w_gate"]))
    g["wu1"] = _c(np.asarray(inputs["L1_ffn_w_up"]))
    g["wd1"] = _c(np.asarray(inputs["L1_ffn_w_down"]))
    g.update(host_constants())

    x = np.asarray(inputs["L0_attn_x"])
    per_core = []
    for c in range(N_CORES):
        m = dict(g)
        m["x_t"] = _c(x[c * BC:(c + 1) * BC].transpose(0, 2, 1))
        per_core.append(m)
    return per_core



class Frame:
    pass


def setup_frame(k, dr):
    f = Frame()
    f.cT = k.sem("cT")
    f.cV = k.sem("cV")
    f.cS = k.sem("cS")
    f.cG = k.sem("cG")
    f.dL = k.sem("dL")
    f.dW = k.sem("dW")
    f.dS = k.sem("dS")
    f.dA = k.sem("dA")

    f.identb = k.sbuf("identb", [128, 128], BF16)
    f.identf = k.sbuf("identf", [128, 128], F32)
    f.oh = k.sbuf("oh_sb", [8, 1024], F32)
    f.ones_col = k.sbuf("ones_col_sb", [128, 1], BF16)
    f.tri = k.sbuf("tri_sb", [128, 128], F32)
    f.cos_t = k.sbuf("cos_sb", [64, S], F32)
    f.sin_t = k.sbuf("sin_sb", [64, S], F32)

    k.dma("gpsimd", f.identb[:], dr["ident"][:], f.dW)
    k.dma("gpsimd", f.ones_col[:], dr["ones_col"][:], f.dW)
    k.dma("sync", f.identf[:], dr["ident"][:], f.dL)
    k.dma("sync", f.oh[:], dr["onehots"][:], f.dL)
    k.dma("sync", f.tri[:], dr["tri"][:], f.dL)
    k.dma("sync", f.cos_t[:], dr["cos_t"][:], f.dL)
    k.dma("sync", f.sin_t[:], dr["sin_t"][:], f.dL)
    f.consts_ready = [(f.dW, f.dW.n), (f.dL, f.dL.n)]
    return f



def emit_ln_seq(k, f, ln, x_src, xn_buf, seq, extra=None):
    TLN = LN_TILE
    ntile = S // TLN
    last = None
    tile_ready = []
    for tt in range(ntile):
        c0 = tt * TLN
        xf = ln.xf.next()
        xb = ln.xb.next()
        src = x_src[seq].rearrange("(c p) t -> p c t", p=128)[:, :, c0:c0 + TLN]
        k.waits("sync", [(xf.sem, xf.tgt)])
        xf.ready_sem, xf.ready_tgt = f.dL, k.dma("sync", xf.buf[:], src, f.dL)
        k.waits("gpsimd", [(xb.sem, xb.tgt)])
        xb.ready_sem, xb.ready_tgt = f.dW, k.dma("gpsimd", xb.buf[:], src, f.dW)

        k.waits("scalar", [(xf.ready_sem, xf.ready_tgt), (ln.xsq.slots[0].sem, ln.xsq.slots[0].tgt)])
        xsq = ln.xsq.next()
        t_sq = k.op("scalar", lambda s, a=xsq.buf, b=xf.buf: s.activation(
            a[:], b[:].rearrange("p c t -> p (c t)"), AF.Square), f.cS)

        ps_s, ps_q = ln.ps_s, ln.ps_q
        k.waits("tensor", [(xb.ready_sem, xb.ready_tgt), (ps_s.sem, ps_s.tgt), (ps_q.sem, ps_q.tgt),
                           (f.cS, t_sq)])
        def mk_sums(t, xbb=xb.buf, xqq=xsq.buf, pss=ps_s.buf, psq=ps_q.buf):
            for c in range(DC):
                t.matmul(pss[:], ln.ones_col[:], xbb[:, c, :], start=(c == 0), stop=(c == DC - 1))
            for c in range(DC):
                inst = t.matmul(psq[:], ln.ones_col[:], xqq[:, c, :], start=(c == 0), stop=(c == DC - 1))
            return inst
        t_mm = k.op("tensor", mk_sums, f.cT)

        k.wait("vector", f.cT, t_mm)
        k.wait("vector", f.cS, ln.prev_rstd)
        k.chain("vector", lambda v: v.tensor_scalar(ln.mu[:], ps_s.buf[:], 1.0 / D, None, ALU.mult), f.cV)
        k.chain("vector", lambda v: v.tensor_scalar(ln.msq[:], ps_q.buf[:], 1.0 / D, None, ALU.mult), f.cV)
        k.chain("vector", lambda v: v.tensor_tensor(ln.var[:], ln.mu[:], ln.mu[:], ALU.mult), f.cV)
        t_var = k.chain("vector", lambda v: v.scalar_tensor_tensor(
            ln.var[:], ln.msq[:], float(LN_EPS), ln.var[:], ALU.add, ALU.subtract), f.cV)
        ps_s.sem, ps_s.tgt = f.cV, t_var
        ps_q.sem, ps_q.tgt = f.cV, t_var
        k.wait("scalar", f.cV, t_var)
        k.wait("scalar", f.cT, ln.prev_bc)
        k.chain("scalar", lambda s: s.activation(ln.lnv[:], ln.var[:], AF.Ln), f.cS)
        t_rstd = k.chain("scalar", lambda s: s.activation(ln.a_sb[:], ln.lnv[:], AF.Exp, scale=-0.5), f.cS)
        ln.prev_rstd = t_rstd
        k.wait("vector", f.cS, t_rstd)
        t_b = k.chain("vector", lambda v: v.scalar_tensor_tensor(
            ln.b_sb[:], ln.mu[:], -1.0, ln.a_sb[:], ALU.mult, ALU.mult), f.cV)

        psA, psB = ln.psA, ln.psB
        k.waits("tensor", [(f.cV, t_b), (psA.sem, psA.tgt), (psB.sem, psB.tgt)])
        def mk_bc(t):
            t.matmul(psA.buf[:], f.oh[0:1, 0:128], ln.a_sb[:], start=True, stop=True)
            return t.matmul(psB.buf[:], f.oh[0:1, 0:128], ln.b_sb[:], start=True, stop=True)
        t_bc = k.op("tensor", mk_bc, f.cT)
        ln.prev_bc = t_bc

        k.waits("vector", [(f.cT, t_bc)])
        for c in range(DC):
            k.op("vector", lambda v, c=c, xff=xf.buf: v.tensor_tensor(
                ln.t1[:], xff[:, c, :], psA.buf[:], ALU.mult))
            lastv = k.op("vector", lambda v, c=c, c0=c0: v.tensor_tensor(
                xn_buf[:, c, c0:c0 + TLN], ln.t1[:], psB.buf[:], ALU.add), f.cV)
        psA.sem, psA.tgt = f.cV, lastv
        psB.sem, psB.tgt = f.cV, lastv
        xf.sem, xf.tgt = f.cV, lastv
        xb.sem, xb.tgt = f.cT, t_mm
        xsq.sem, xsq.tgt = f.cT, t_mm
        last = lastv
        tile_ready.append(lastv)
        if extra is not None:
            extra(tt, c0, TLN, [(f.cV, lastv)])
    return f.cV, last, tile_ready


LN_TILE = 256


def make_ln(k, f, name="ln"):
    ln = Frame()
    TLN = LN_TILE
    ln.xf = Rot([k.sbuf(f"{name}_xf{i}", [128, DC, TLN], F32) for i in range(2)])
    ln.xb = Rot([k.sbuf(f"{name}_xb{i}", [128, DC, TLN], BF16) for i in range(2)])
    ln.xsq = Rot([k.sbuf(f"{name}_xsq{i}", [128, DC, TLN], BF16) for i in range(2)])
    ln.ones_col = f.ones_col
    ln.ps_s = Slot(k.psum(f"{name}_ps_s", [1, TLN], F32))
    ln.ps_q = Slot(k.psum(f"{name}_ps_q", [1, TLN], F32))
    ln.psA = Slot(k.psum(f"{name}_psA", [128, TLN], F32))
    ln.psB = Slot(k.psum(f"{name}_psB", [128, TLN], F32))
    for sl in (ln.ps_s, ln.ps_q, ln.psA, ln.psB):
        sl.buf = sl.buf
    ln.mu = k.sbuf(f"{name}_mu", [1, TLN], F32)
    ln.msq = k.sbuf(f"{name}_msq", [1, TLN], F32)
    ln.var = k.sbuf(f"{name}_var", [1, TLN], F32)
    ln.lnv = k.sbuf(f"{name}_lnv", [1, TLN], F32)
    ln.a_sb = k.sbuf(f"{name}_a", [1, TLN], F32)
    ln.b_sb = k.sbuf(f"{name}_b", [1, TLN], F32)
    ln.t1 = k.sbuf(f"{name}_t1", [128, TLN], F32)
    ln.prev_rstd = 0
    ln.prev_bc = 0
    return ln



def emit_qkv_seq(k, f, pj, layer, seq, xn_buf, xn_ready, dr, ln_tiles=None):
    OS = 512
    TPROJ = 512
    qscale = (1.0 / np.sqrt(128.0)) if layer == 0 else 0.125
    wlist = (("wq", "q"), ("wk", "k"), ("wv", "v"))
    for wname, role in wlist:
        wsrc = dr[wname + str(layer)]
        dst = dr["qT" if role == "q" else ("kT" if role == "k" else "vT")]
        for os_ in range(D // OS):
            wslot = pj.w.next()
            k.waits("gpsimd", [(wslot.sem, wslot.tgt)])
            wslot.ready_sem, wslot.ready_tgt = f.dW, k.dma(
                "gpsimd", wslot.buf[:],
                wsrc[:, os_ * OS:(os_ + 1) * OS].rearrange("(c p) o -> p c o", p=128), f.dW)
            for oc in range(OS // 128):
                orow = os_ * OS + oc * 128
                for tt in range(S // TPROJ):
                    c0 = tt * TPROJ
                    ps = pj.ps.next()
                    if ln_tiles is not None:
                        nper = TPROJ // LN_TILE
                        xr = (xn_ready[0], ln_tiles[(tt + 1) * nper - 1])
                    else:
                        xr = xn_ready
                    k.waits("tensor", [(wslot.ready_sem, wslot.ready_tgt), (ps.sem, ps.tgt),
                                       xr])
                    def mk(t, w=wslot.buf, oc=oc, c0=c0, p=ps.buf):
                        for c in range(DC):
                            inst = t.matmul(p[:], w[:, c, oc * 128:(oc + 1) * 128],
                                            xn_buf[:, c, c0:c0 + TPROJ],
                                            start=(c == 0), stop=(c == DC - 1))
                        return inst
                    t_mm = k.op("tensor", mk, f.cT)
                    if layer == 0 or role == "v":
                        ev = pj.ev.next()
                        if layer == 0:
                            sc = qscale if role == "q" else 1.0
                            k.waits("scalar", [(f.cT, t_mm), (ev.sem, ev.tgt)])
                            t_ev = k.op("scalar", lambda s, p=ps.buf, e=ev.buf, sc=sc:
                                        s.activation(e[:], p[:], AF.Copy, scale=sc), f.cS)
                        else:
                            bv = pj.bias["v"]
                            k.waits("scalar", [(f.cT, t_mm), (ev.sem, ev.tgt)])
                            t_ev = k.op("scalar", lambda s, p=ps.buf, e=ev.buf, oc2=os_ * (OS // 128) + oc:
                                        s.activation(e[:], p[:], AF.Sigmoid, bias=bv[:, oc2:oc2 + 1]), f.cS)
                        ps.sem, ps.tgt = f.cS, t_ev
                        k.wait("sync", f.cS, t_ev)
                        st = k.dma("sync", dst[seq, orow:orow + 128, c0:c0 + TPROJ], ev.buf[:], f.dS)
                        ev.sem, ev.tgt = f.dS, st
                    else:
                        bmap = pj.bias[role]
                        oc2 = os_ * (OS // 128) + oc
                        sc = qscale if role == "q" else 1.0
                        t_ev = None
                        for hh in range(2):
                            stg = pj.rp_in.next()
                            rot = pj.rp_rot.next()
                            evb = pj.rp_out.next()
                            prow = hh * 64
                            k.waits("vector", [(f.cT, t_mm), (stg.sem, stg.tgt), (rot.sem, rot.tgt),
                                               (evb.sem, evb.tgt)])
                            if role == "q":
                                k.op("vector", lambda v, p=ps.buf, sg=stg.buf, oc2=oc2, prow=prow, sc=sc, bmap=bmap:
                                     v.tensor_scalar(sg[:], p[prow:prow + 64, :],
                                                     bmap[prow:prow + 64, oc2:oc2 + 1], sc, ALU.add, ALU.mult))
                            else:
                                k.op("vector", lambda v, p=ps.buf, sg=stg.buf, oc2=oc2, prow=prow, bmap=bmap:
                                     v.tensor_scalar(sg[:], p[prow:prow + 64, :],
                                                     bmap[prow:prow + 64, oc2:oc2 + 1], None, ALU.add))
                            k.op("vector", lambda v, sg=stg.buf, r=rot.buf:
                                 v.tensor_scalar(r[0:32, :], sg[32:64, :], -1.0, None, ALU.mult))
                            k.op("vector", lambda v, sg=stg.buf, r=rot.buf:
                                 v.tensor_copy(r[32:64, :], sg[0:32, :]))
                            k.op("vector", lambda v, sg=stg.buf, c0=c0:
                                 v.tensor_tensor(sg[:], sg[:], f.cos_t[:, c0:c0 + TPROJ], ALU.mult))
                            k.op("vector", lambda v, r=rot.buf, c0=c0:
                                 v.tensor_tensor(r[:], r[:], f.sin_t[:, c0:c0 + TPROJ], ALU.mult))
                            t_ev = k.op("vector", lambda v, sg=stg.buf, r=rot.buf, e=evb.buf:
                                        v.tensor_tensor(e[:], sg[:], r[:], ALU.add), f.cV)
                            stg.sem, stg.tgt = f.cV, t_ev
                            rot.sem, rot.tgt = f.cV, t_ev
                            k.wait("sync", f.cV, t_ev)
                            st = k.dma("sync", dst[seq, orow + prow:orow + prow + 64, c0:c0 + TPROJ],
                                       evb.buf[:], f.dS)
                            evb.sem, evb.tgt = f.dS, st
                        ps.sem, ps.tgt = f.cV, t_ev
            wslot.sem, wslot.tgt = f.cT, t_mm


def make_proj(k, f, layer):
    pj = Frame()
    OS, TPROJ = 512, 512
    pj.w = Rot([k.sbuf(f"pj_w{i}", [128, DC, OS], BF16) for i in range(2)])
    pj.ps = Rot([Slot(None) for _ in range(0)])
    pj.ps = Rot([k.psum(f"pj_ps{i}", [128, TPROJ], F32) for i in range(3)])
    pj.ev = Rot([k.sbuf(f"pj_ev{i}", [128, TPROJ], BF16) for i in range(3)])
    if layer == 1:
        pj.rp_in = Rot([k.sbuf(f"pj_ri{i}", [64, TPROJ], F32) for i in range(2)])
        pj.rp_rot = Rot([k.sbuf(f"pj_rr{i}", [64, TPROJ], F32) for i in range(2)])
        pj.rp_out = Rot([k.sbuf(f"pj_ro{i}", [64, TPROJ], BF16) for i in range(3)])
        pj.bias = {}
    return pj


def _rotpsum(k, name, n, shape):
    return Rot([k.psum(f"{name}{i}", list(shape), F32) for i in range(n)])



def emit_attn_seq(k, f, at, layer, seq, o_seq, dr):
    dh = 128 if layer == 0 else 64
    nheads = D // dh
    QG = 512
    for h in range(nheads):
        qs, ks, vs = at.qh.next(), at.kh.next(), at.vh.next()
        rows = (h * dh, (h + 1) * dh)
        for sl, src in ((qs, dr["qT"]), (ks, dr["kT"]), (vs, dr["vT"])):
            k.waits("sync", [(sl.sem, sl.tgt)])
            sl.ready_sem, sl.ready_tgt = f.dL, k.dma(
                "sync", sl.buf[0:dh, :], src[seq, rows[0]:rows[1], :], f.dL)

        vtok = at.vtok
        t_ev = 0
        for kc in range(S // 128):
            ptr = at.ptr.next()
            k.waits("tensor", [(ptr.sem, ptr.tgt), (vs.ready_sem, vs.ready_tgt)])
            t_tr = k.op("tensor", lambda t, kc=kc, p=ptr.buf, v=vs.buf:
                        t.transpose(p[:, 0:dh], v[0:dh, kc * 128:(kc + 1) * 128],
                                    f.identb[0:dh, 0:dh]), f.cT)
            deps = [(f.cT, t_tr)] + ([(vtok.sem, vtok.tgt)] if kc == 0 else [])
            k.waits("scalar", deps)
            t_ev = k.op("scalar", lambda s, kc=kc, p=ptr.buf:
                        s.activation(vtok.buf[:, kc * dh:(kc + 1) * dh], p[:, 0:dh], AF.Copy), f.cS)
            ptr.sem, ptr.tgt = f.cS, t_ev
        vtok_ready = (f.cS, t_ev)
        vs.sem, vs.tgt = f.cT, f.cT.n

        for qg in range(S // QG):
            nk = (qg + 1) * (QG // 128)
            k.waits("vector", [(at.P.sem, at.P.tgt)])
            t_e = 0
            for qi in range(4):
                qc = qg * 4 + qi
                kcols = (qc + 1) * 128
                ntile = (kcols + 511) // 512
                for i in range(ntile):
                    psl = at.s_ps[i]
                    w = min(512, kcols - i * 512)
                    k.waits("tensor", [(psl.sem, psl.tgt), (qs.ready_sem, qs.ready_tgt),
                                       (ks.ready_sem, ks.ready_tgt)])
                    t_s = k.op("tensor", lambda t, i=i, w=w, qc=qc, p=psl.buf, q=qs.buf, kk=ks.buf:
                               t.matmul(p[:, 0:w], q[0:dh, qc * 128:(qc + 1) * 128],
                                        kk[0:dh, i * 512:i * 512 + w], start=True, stop=True), f.cT)
                    psl.sem, psl.tgt = f.cT, t_s
                dtile = at.s_ps[ntile - 1]
                dcol = (kcols - 128) - (ntile - 1) * 512
                k.waits("vector", [(f.cT, dtile.tgt)])
                k.op("vector", lambda v, p=dtile.buf, dcol=dcol:
                     v.tensor_tensor(p[:, dcol:dcol + 128], p[:, dcol:dcol + 128],
                                     f.tri[:], ALU.add))
                for i in range(ntile):
                    w = min(512, kcols - i * 512)
                    k.op("vector", lambda v, i=i, w=w, p=at.s_ps[i].buf:
                         v.reduce_max(at.mx[:, i:i + 1], p[:, 0:w], axis=AX.X))
                t_nm = k.chain("vector", lambda v, nt=ntile:
                               v.reduce_max(at.negm[:], at.mx[:, 0:nt], axis=AX.X, negate=True), f.cV)
                if kcols < nk * 128:
                    k.op("vector", lambda v, qi=qi, a=kcols, b=nk * 128:
                         v.memset(at.P.buf[:, qi, a:b], 0.0))
                k.wait("scalar", f.cV, t_nm)
                for i in range(ntile):
                    w = min(512, kcols - i * 512)
                    t_e = k.op("scalar", lambda s, i=i, w=w, p=at.s_ps[i].buf, qi=qi:
                               s.activation(at.P.buf[:, qi, i * 512:i * 512 + w], p[:, 0:w],
                                            AF.Exp, bias=at.negm[:]), f.cS)
                    at.s_ps[i].sem, at.s_ps[i].tgt = f.cS, t_e
            t_ptail = k.op("vector", lambda v: v.tensor_copy(at.mx[:, 0:1], at.mx[:, 0:1]), f.cV)
            p_ready = [(f.cS, t_e), (f.cV, t_ptail)]

            po, prb = at.po, at.prb
            t_pv = 0
            for kc in range(nk):
                ptr = at.ptr.next()
                k.waits("tensor", [(ptr.sem, ptr.tgt)] + (p_ready if kc == 0 else []))
                def mkT(t, kc=kc, p=ptr.buf):
                    for qi in range(4):
                        inst = t.transpose(p[:, qi * 128:(qi + 1) * 128],
                                           at.P.buf[:, qi, kc * 128:(kc + 1) * 128], f.identb[:])
                    return inst
                t_tr = k.op("tensor", mkT, f.cT)
                pts = at.pts.next()
                k.waits("scalar", [(f.cT, t_tr), (pts.sem, pts.tgt)])
                t_pe = k.op("scalar", lambda s, p=ptr.buf, o=pts.buf:
                            s.activation(o[:], p[:, 0:512], AF.Copy), f.cS)
                ptr.sem, ptr.tgt = f.cS, t_pe
                k.waits("tensor", [(f.cS, t_pe)] +
                        ([(po.sem, po.tgt), (prb.sem, prb.tgt), vtok_ready] if kc == 0 else []))
                def mkPV(t, kc=kc, pt=pts.buf, last=(kc == nk - 1)):
                    t.matmul(po.buf[0:dh, :], vtok.buf[:, kc * dh:(kc + 1) * dh], pt[:],
                             start=(kc == 0), stop=last)
                    return t.matmul(prb.buf[0:1, :], f.ones_col[:], pt[:],
                                    start=(kc == 0), stop=last)
                t_pv = k.op("tensor", mkPV, f.cT)
                pts.sem, pts.tgt = f.cT, t_pv
            at.P.sem, at.P.tgt = f.cT, t_pv

            k.wait("vector", f.cT, t_pv)
            t_rl = k.op("vector", lambda v: v.reciprocal(at.rl[:], prb.buf[0:1, :]), f.cV)
            k.waits("tensor", [(f.cV, t_rl)])
            t_bc = k.op("tensor", lambda t: t.matmul(prb.buf[:], f.oh[0:1, 0:128], at.rl[:],
                                                     start=True, stop=True), f.cT)
            k.waits("scalar", [(f.cT, t_bc)])
            t_rbc = k.op("scalar", lambda s: s.activation(at.rlb[:], prb.buf[:], AF.Copy), f.cS)
            prb.sem, prb.tgt = f.cS, t_rbc
            ch, r0 = (h, 0) if layer == 0 else (h // 2, (h % 2) * 64)
            k.waits("vector", [(f.cS, t_rbc), (f.cT, t_pv)])
            t_out = k.op("vector", lambda v, ch=ch, r0=r0, qg=qg:
                         v.tensor_tensor(o_seq.buf[r0:r0 + dh, ch, qg * QG:(qg + 1) * QG],
                                         po.buf[0:dh, :], at.rlb[0:dh, :], ALU.mult), f.cV)
            po.sem, po.tgt = f.cV, t_out
        qs.sem, qs.tgt = f.cT, f.cT.n
        ks.sem, ks.tgt = f.cT, f.cT.n
        vtok.sem, vtok.tgt = f.cT, f.cT.n
    o_seq.ready_sem, o_seq.ready_tgt = f.cV, f.cV.n
    return f.cV, f.cV.n


def make_attn(k, f, layer):
    at = Frame()
    dh = 128 if layer == 0 else 64
    at.qh = Rot([k.sbuf(f"at_q{i}", [128, S], BF16) for i in range(2)])
    at.kh = Rot([k.sbuf(f"at_k{i}", [128, S], BF16) for i in range(2)])
    at.vh = Rot([k.sbuf(f"at_v{i}", [128, S], BF16) for i in range(2)])
    at.vtok = Slot(k.sbuf("at_vtok", [128, (S // 128) * dh], BF16))
    at.s_ps = [Slot(k.psum(f"at_s{i}", [128, 512], F32)) for i in range(4)]
    at.ptr = Rot([k.psum(f"at_ptr{i}", [128, 1024], BF16) for i in range(2)])
    at.po = Slot(k.psum("at_po", [128, 512], F32))
    at.prb = Slot(k.psum("at_prb", [128, 512], F32))
    at.P = Slot(k.sbuf("at_P", [128, 4, S], BF16))
    at.mx = k.sbuf("at_mx", [128, 4], F32)
    at.negm = k.sbuf("at_negm", [128, 1], F32)
    at.pts = Rot([k.sbuf(f"at_pts{i}", [128, 512], BF16) for i in range(2)])
    at.rl = k.sbuf("at_rl", [1, 512], F32)
    at.rlb = k.sbuf("at_rlb", [128, 512], F32)
    return at



def emit_oproj_seq(k, f, pr, layer, seq, o_seq, o_ready, x_src, x_dst, dr, bias=None):
    OS, TPROJ = 512, 512
    wsrc = dr["wo" + str(layer)]
    for os_ in range(D // OS):
        wslot = pr.w.next()
        k.waits("gpsimd", [(wslot.sem, wslot.tgt)])
        wslot.ready_sem, wslot.ready_tgt = f.dW, k.dma(
            "gpsimd", wslot.buf[:],
            wsrc[:, os_ * OS:(os_ + 1) * OS].rearrange("(c p) o -> p c o", p=128), f.dW)
        t_mm = None
        for oc in range(OS // 128):
            orow = os_ * OS + oc * 128
            oc2 = os_ * (OS // 128) + oc
            for tt in range(S // TPROJ):
                c0 = tt * TPROJ
                ps = pr.ps.next()
                xin = pr.xin.next()
                k.waits("sync", [(xin.sem, xin.tgt)])
                xin.ready_sem, xin.ready_tgt = f.dL, k.dma(
                    "sync", xin.buf[:], x_src[seq, orow:orow + 128, c0:c0 + TPROJ], f.dL)
                k.waits("tensor", [(wslot.ready_sem, wslot.ready_tgt), (ps.sem, ps.tgt), o_ready])
                def mk(t, w=wslot.buf, oc=oc, c0=c0, p=ps.buf):
                    for c in range(DC):
                        inst = t.matmul(p[:], w[:, c, oc * 128:(oc + 1) * 128],
                                        o_seq[:, c, c0:c0 + TPROJ],
                                        start=(c == 0), stop=(c == DC - 1))
                    return inst
                t_mm = k.op("tensor", mk, f.cT)
                ev = pr.ev.next()
                k.waits("vector", [(f.cT, t_mm), (ev.sem, ev.tgt),
                                   (xin.ready_sem, xin.ready_tgt)])
                if bias is None:
                    t_ev = k.op("vector", lambda v, p=ps.buf, e=ev.buf, xi=xin.buf:
                                v.tensor_tensor(e[:], p[:], xi[:], ALU.add), f.cV)
                else:
                    t_ev = k.op("vector", lambda v, p=ps.buf, e=ev.buf, xi=xin.buf, oc2=oc2:
                                v.scalar_tensor_tensor(e[:], p[:], bias[:, oc2:oc2 + 1], xi[:],
                                                       ALU.add, ALU.add), f.cV)
                ps.sem, ps.tgt = f.cV, t_ev
                xin.sem, xin.tgt = f.cV, t_ev
                k.wait("sync", f.cV, t_ev)
                st = k.dma("sync", x_dst[seq, orow:orow + 128, c0:c0 + TPROJ], ev.buf[:], f.dS)
                ev.sem, ev.tgt = f.dS, st
        wslot.sem, wslot.tgt = f.cT, t_mm


def make_oproj(k, f):
    pr = Frame()
    OS, TPROJ = 512, 512
    pr.w = Rot([k.sbuf(f"pr_w{i}", [128, DC, OS], BF16) for i in range(2)])
    pr.ps = Rot([k.psum(f"pr_ps{i}", [128, TPROJ], F32) for i in range(2)])
    pr.xin = Rot([k.sbuf(f"pr_x{i}", [128, TPROJ], F32) for i in range(3)])
    pr.ev = Rot([k.sbuf(f"pr_e{i}", [128, TPROJ], F32) for i in range(3)])
    return pr



def declare_drams(nc, upto, debug_out=()):
    dr = {}
    dr["_dbg"] = {}
    def ext_in(name, shape, dtype=F32):
        dr[name] = nc.dram_tensor(name, list(shape), dtype, kind="ExternalInput")
    def scratch(name, shape, dtype):
        if name in debug_out and dtype == F32:
            dr[name] = nc.dram_tensor(name, list(shape), dtype, kind="ExternalOutput")
            return
        dr[name] = nc.dram_tensor(name, list(shape), dtype, kind="Internal")
        if name in debug_out:
            dr["_dbg"][name] = nc.dram_tensor("dbg_" + name, list(shape), F32,
                                              kind="ExternalOutput")

    ext_in("x_t", [BC, D, S])
    for nm in ("wq0", "wk0", "wv0", "wo0", "wq1", "wk1", "wv1", "wo1"):
        ext_in(nm, [D, D])
    ext_in("rw", [D, 8])
    ext_in("gw", [NE, D, F_MOE]); ext_in("uw", [NE, D, F_MOE]); ext_in("dw", [NE, F_MOE, D])
    for nm in ("bq1", "bk1", "bv1", "bo1"):
        ext_in(nm, [128, DC])
    ext_in("wg1", [D, I_FFN]); ext_in("wu1", [D, I_FFN]); ext_in("wd1", [I_FFN, D])
    ext_in("ident", [128, 128]); ext_in("tri", [128, 128]); ext_in("onehots", [8, 1024])
    ext_in("ones_col", [128, 1]); ext_in("cos_t", [64, S]); ext_in("sin_t", [64, S])

    scratch("qT", [BC, D, S], BF16)
    scratch("kT", [BC, D, S], BF16)
    scratch("vT", [BC, D, S], BF16)
    scratch("xa", [BC, D, S], F32)
    scratch("xb_", [BC, D, S], F32)
    scratch("xnT", [D, TT_], BF16)
    scratch("yacc", [D, TT_], F32)
    dr["out_t"] = nc.dram_tensor("out_t", [BC, D, S], F32, kind="ExternalOutput")
    return dr



from contextlib import contextmanager


@contextmanager
def scope(k):
    old = k.ctx
    with ExitStack() as es:
        k.ctx = es
        try:
            yield
        finally:
            k.ctx = old


def fence_all(k):
    tgts = k.fence_targets()
    for eng in ENGINES:
        k.waits(eng, tgts)



def emit_attn_block(k, f, dr, layer, x_src, x_dst):
    with scope(k):
        ln = make_ln(k, f)
        pj = make_proj(k, f, layer)
        xn = Slot(k.sbuf("xn_seq", [128, DC, S], BF16))
        if layer == 1:
            pj.bias = {}
            for role, nm in (("q", "bq1"), ("k", "bk1"), ("v", "bv1")):
                bb = k.sbuf(f"bias_{nm}", [128, DC], F32)
                k.dma("sync", bb[:], dr[nm][:], f.dL)
                pj.bias[role] = bb
            k.waits("scalar", [(f.dL, f.dL.n)])
            k.waits("vector", [(f.dL, f.dL.n)])
        for s_ in range(BC):
            k.waits("vector", [(xn.sem, xn.tgt)])
            sem, tgt, tiles = emit_ln_seq(k, f, ln, x_src, xn.buf, s_)
            emit_qkv_seq(k, f, pj, layer, s_, xn.buf, (sem, tgt), dr, ln_tiles=tiles)
            xn.sem, xn.tgt = f.cT, f.cT.n
    fence_all(k)
    with scope(k):
        o_seq = Slot(k.sbuf("o_seq", [128, DC, S], BF16))
        bo = None
        if layer == 1:
            bo = k.sbuf("bias_bo1", [128, DC], F32)
            k.dma("sync", bo[:], dr["bo1"][:], f.dL)
            k.waits("vector", [(f.dL, f.dL.n)])
        for s_ in range(BC):
            with scope(k):
                at = make_attn(k, f, layer)
                emit_attn_seq(k, f, at, layer, s_, o_seq, dr)
            fence_all(k)
            with scope(k):
                pr = make_oproj(k, f)
                emit_oproj_seq(k, f, pr, layer, s_, o_seq.buf,
                               (o_seq.ready_sem, o_seq.ready_tgt), x_src, x_dst, dr, bias=bo)
            fence_all(k)
            o_seq.sem, o_seq.tgt = f.cT, f.cT.n
    fence_all(k)



DEV_MOE_TAPS = False


def emit_moe_block(k, f, dr, x_src, x_dst):
    TLN = 512
    if DEV_MOE_TAPS:
        dr["dbg_rt"] = k.nc.dram_tensor("dbg_rt", [128, 48], F32, kind="ExternalOutput")
        dr["dbg_combT"] = k.nc.dram_tensor("dbg_combT", [8, TT_], F32, kind="ExternalOutput")
        dr["dbg_Bsb"] = k.nc.dram_tensor("dbg_Bsb", [128, 4, 512], F32, kind="ExternalOutput")
        dr["dbg_h"] = k.nc.dram_tensor("dbg_h", [F_MOE, S], F32, kind="ExternalOutput")
    with scope(k):
        combT = k.sbuf("combT", [8, TT_], BF16)
        ohb = k.sbuf("ohb", [8, 1024], BF16)
        k.dma("gpsimd", ohb[:], dr["onehots"][:], f.dW)
        emit_moe_inner(k, f, dr, x_src, x_dst, combT, ohb)
    fence_all(k)


def emit_moe_inner(k, f, dr, x_src, x_dst, combT, ohb):
    TLN = 512
    with scope(k):
        ln = make_ln(k, f, "mln")
        xn = Slot(k.sbuf("mxn", [128, DC, S], BF16))
        rwb = k.sbuf("rwb", [128, DC, 8], BF16)
        k.dma("gpsimd", rwb[:], dr["rw"][:].rearrange("(c p) e -> p c e", p=128), f.dW)
        rt = Frame()
        rt.pl = Slot(k.psum("rt_pl", [128, 512], F32))
        rt.pt = Slot(k.psum("rt_pt", [8, 512], F32))
        rt.L = k.sbuf("rt_L", [128, 8], F32)
        rt.m1 = k.sbuf("rt_m1", [128, 1], F32)
        rt.nm1 = k.sbuf("rt_nm1", [128, 1], F32)
        rt.m2 = k.sbuf("rt_m2", [128, 1], F32)
        rt.eq = k.sbuf("rt_eq", [128, 8], F32)
        rt.E = k.sbuf("rt_E", [128, 8], F32)
        rt.we = k.sbuf("rt_we", [128, 8], F32)
        rt.ssum = k.sbuf("rt_ss", [128, 1], F32)
        rt.rs = k.sbuf("rt_rs", [128, 1], F32)
        rt.comb = k.sbuf("rt_comb", [128, 8], F32)
        k.waits("tensor", [(f.dW, f.dW.n)])

        def router(seq):
            def extra(tt, c0, tln, dep):
                k.waits("sync", dep)
                k.dma("sync", dr["xnT"][:].rearrange("(c p) t -> p c t", p=128)
                      [:, :, seq * S + c0: seq * S + c0 + tln],
                      xn.buf[:, :, c0:c0 + tln], f.dS)
                for j in range(tln // 128):
                    t0 = c0 + j * 128
                    k.waits("tensor", dep + [(rt.pl.sem, rt.pl.tgt)])
                    def mkl(t, t0=t0):
                        for c in range(DC):
                            inst = t.matmul(rt.pl.buf[:, 0:8], xn.buf[:, c, t0:t0 + 128],
                                            rwb[:, c, :], start=(c == 0), stop=(c == DC - 1))
                        return inst
                    t_l = k.op("tensor", mkl, f.cT)
                    k.waits("vector", [(f.cT, t_l)])
                    k.chain("vector", lambda v: v.tensor_copy(rt.L[:], rt.pl.buf[:, 0:8]), f.cV)
                    t_lc = k.chain("vector", lambda v: v.reduce_max(rt.m1[:], rt.L[:], axis=AX.X), f.cV)
                    rt.pl.sem, rt.pl.tgt = f.cV, t_lc
                    k.chain("vector", lambda v: v.tensor_scalar(rt.nm1[:], rt.m1[:], -1.0, None, ALU.mult), f.cV)
                    k.chain("vector", lambda v: v.tensor_scalar(rt.eq[:], rt.L[:], rt.m1[:, 0:1], None, ALU.is_equal), f.cV)
                    k.chain("vector", lambda v: v.scalar_tensor_tensor(rt.eq[:], rt.eq[:], -1e30, rt.L[:], ALU.mult, ALU.add), f.cV)
                    t_m2 = k.chain("vector", lambda v: v.reduce_max(rt.m2[:], rt.eq[:], axis=AX.X), f.cV)
                    k.waits("scalar", [(f.cV, t_m2)])
                    t_E = k.op("scalar", lambda s: s.activation(rt.E[:], rt.L[:], AF.Exp, bias=rt.nm1[:, 0:1]), f.cS)
                    k.waits("vector", [(f.cS, t_E)])
                    k.chain("vector", lambda v: v.tensor_scalar(rt.we[:], rt.L[:], rt.m2[:, 0:1], None, ALU.is_ge), f.cV)
                    k.chain("vector", lambda v: v.tensor_tensor(rt.we[:], rt.we[:], rt.E[:], ALU.mult), f.cV)
                    k.chain("vector", lambda v: v.reduce_sum(rt.ssum[:], rt.we[:], axis=AX.X), f.cV)
                    k.chain("vector", lambda v: v.reciprocal(rt.rs[:], rt.ssum[:]), f.cV)
                    t_cb = k.chain("vector", lambda v: v.tensor_scalar(rt.comb[:], rt.we[:], rt.rs[:, 0:1], None, ALU.mult), f.cV)
                    if DEV_MOE_TAPS and seq == 0 and t0 == 0:
                        k.waits("gpsimd", [(f.cV, t_cb), (f.cS, f.cS.n)])
                        k.dma("gpsimd", dr["dbg_rt"][:, 0:8], rt.L[:], f.dW, allow_slow_non_contiguous=True)
                        k.dma("gpsimd", dr["dbg_rt"][:, 8:16], rt.E[:], f.dW, allow_slow_non_contiguous=True)
                        k.dma("gpsimd", dr["dbg_rt"][:, 16:24], rt.we[:], f.dW, allow_slow_non_contiguous=True)
                        k.dma("gpsimd", dr["dbg_rt"][:, 24:32], rt.comb[:], f.dW, allow_slow_non_contiguous=True)
                        k.dma("gpsimd", dr["dbg_rt"][:, 32:33], rt.m1[:], f.dW, allow_slow_non_contiguous=True)
                        k.dma("gpsimd", dr["dbg_rt"][:, 33:34], rt.m2[:], f.dW, allow_slow_non_contiguous=True)
                        k.dma("gpsimd", dr["dbg_rt"][:, 34:35], rt.rs[:], f.dW, allow_slow_non_contiguous=True)
                        k.dma("gpsimd", dr["dbg_rt"][:, 35:36], rt.nm1[:], f.dW, allow_slow_non_contiguous=True)
                    k.waits("tensor", [(f.cV, t_cb), (rt.pt.sem, rt.pt.tgt)])
                    t_tr = k.op("tensor", lambda t: t.transpose(rt.pt.buf[:, 0:128], rt.comb[:], f.identf[:]), f.cT)
                    k.waits("scalar", [(f.cT, t_tr)])
                    t_ce = k.op("scalar", lambda s, seq=seq, t0=t0: s.activation(
                        combT[:, seq * S + t0: seq * S + t0 + 128], rt.pt.buf[:, 0:128], AF.Copy), f.cS)
                    rt.pt.sem, rt.pt.tgt = f.cS, t_ce
            return extra

        for s_ in range(BC):
            k.waits("vector", [(xn.sem, xn.tgt)])
            emit_ln_seq(k, f, ln, x_src, xn.buf, s_, extra=router(s_))
            xn.sem, xn.tgt = f.cT, f.cT.n
        del ln
        comb_done = (f.cS, f.cS.n)
        xnT_done = (f.dS, f.dS.n)
        if DEV_MOE_TAPS:
            k.waits("gpsimd", [comb_done])
            k.dma("gpsimd", dr["dbg_combT"][:], combT[:], f.dW)
    fence_all(k)

    with scope(k):
        xnh = Slot(k.sbuf("xnh", [128, DC, S], BF16))
        Bsb = Slot(k.sbuf("Bsb", [128, 4, 512], F32))
        psB = Slot(k.psum("psB", [128, 512], F32))
        psgr = Rot([k.psum(f"psg{i}", [128, 512], F32) for i in range(2)])
        psur = Rot([k.psum(f"psu{i}", [128, 512], F32) for i in range(2)])
        psd = Rot([k.psum(f"psd{i}", [128, 512], F32) for i in range(2)])
        gwb = Rot([k.sbuf(f"gwb{i}", [128, DC, 128], BF16) for i in range(2)])
        uwb = Rot([k.sbuf(f"uwb{i}", [128, DC, 128], BF16) for i in range(2)])
        dwb = Rot([k.sbuf(f"dwb{i}", [128, FC_MOE, 128], BF16) for i in range(2)])
        gs = Rot([k.sbuf(f"gs{i}", [128, 512], BF16) for i in range(2)])
        ut = Rot([k.sbuf(f"ut{i}", [128, 512], BF16) for i in range(2)])
        hbuf = k.sbuf("hbuf", [128, FC_MOE, S], BF16)
        dsb = Rot([k.sbuf(f"dsb{i}", [128, S], F32) for i in range(2)])

        prev_acc = 0
        for half in range(2):
            th0 = half * S
            k.waits("sync", [(xnh.sem, xnh.tgt), xnT_done])
            xnh.ready_sem, xnh.ready_tgt = f.dL, k.dma(
                "sync", xnh.buf[:],
                dr["xnT"][:].rearrange("(c p) t -> p c t", p=128)[:, :, th0:th0 + S], f.dL)
            for e in range(NE):
                t_bev = 0
                for ts_ in range(4):
                    k.waits("tensor", [(psB.sem, psB.tgt), comb_done])
                    t_b = k.op("tensor", lambda t, e=e, ts_=ts_, th0=th0: t.matmul(
                        psB.buf[:], ohb[:, e * 128:(e + 1) * 128],
                        combT[:, th0 + ts_ * 512: th0 + (ts_ + 1) * 512],
                        start=True, stop=True), f.cT)
                    k.waits("scalar", [(f.cT, t_b)] + ([(Bsb.sem, Bsb.tgt)] if ts_ == 0 else []))
                    t_bev = k.op("scalar", lambda s, ts_=ts_: s.activation(
                        Bsb.buf[:, ts_, :], psB.buf[:], AF.Copy), f.cS)
                    psB.sem, psB.tgt = f.cS, t_bev
                Bsb.ready_sem, Bsb.ready_tgt = f.cS, t_bev

                t_h = 0
                hb_guard = getattr(emit_moe_inner, "_hb_guard", None)
                for fc in range(FC_MOE):
                    gsl, usl = gwb.next(), uwb.next()
                    k.waits("gpsimd", [(gsl.sem, gsl.tgt)])
                    gsl.ready_sem, gsl.ready_tgt = f.dW, k.dma(
                        "gpsimd", gsl.buf[:],
                        dr["gw"][e, :, fc * 128:(fc + 1) * 128].rearrange("(c p) o -> p c o", p=128), f.dW)
                    k.waits("gpsimd", [(usl.sem, usl.tgt)])
                    usl.ready_sem, usl.ready_tgt = f.dW, k.dma(
                        "gpsimd", usl.buf[:],
                        dr["uw"][e, :, fc * 128:(fc + 1) * 128].rearrange("(c p) o -> p c o", p=128), f.dW)
                    for ts_ in range(4):
                        psg, psu = psgr.next(), psur.next()
                        k.waits("tensor", [(gsl.ready_sem, gsl.ready_tgt), (usl.ready_sem, usl.ready_tgt),
                                           (psg.sem, psg.tgt), (psu.sem, psu.tgt),
                                           (xnh.ready_sem, xnh.ready_tgt)])
                        def mkgu(t, g=gsl.buf, u=usl.buf, ts_=ts_, psg=psg, psu=psu):
                            for c in range(DC):
                                t.matmul(psg.buf[:], g[:, c, :], xnh.buf[:, c, ts_ * 512:(ts_ + 1) * 512],
                                         start=(c == 0), stop=(c == DC - 1))
                            for c in range(DC):
                                inst = t.matmul(psu.buf[:], u[:, c, :], xnh.buf[:, c, ts_ * 512:(ts_ + 1) * 512],
                                                start=(c == 0), stop=(c == DC - 1))
                            return inst
                        t_mm = k.op("tensor", mkgu, f.cT)
                        g2, u2 = gs.next(), ut.next()
                        k.waits("scalar", [(f.cT, t_mm), (g2.sem, g2.tgt)])
                        t_si = k.op("scalar", lambda s, g2=g2, psg=psg: s.activation(g2.buf[:], psg.buf[:], AF.Silu), f.cS)
                        psg.sem, psg.tgt = f.cS, t_si
                        k.waits("vector", [(f.cT, t_mm), (u2.sem, u2.tgt),
                                           (Bsb.ready_sem, Bsb.ready_tgt)])
                        t_ut = k.op("vector", lambda v, u2=u2, ts_=ts_, psu=psu: v.tensor_tensor(
                            u2.buf[:], psu.buf[:], Bsb.buf[:, ts_, :], ALU.mult), f.cV)
                        psu.sem, psu.tgt = f.cV, t_ut
                        k.waits("vector", [(f.cS, t_si)] + ([hb_guard] if hb_guard else []))
                        hb_guard = None
                        t_h = k.op("vector", lambda v, g2=g2, u2=u2, fc=fc, ts_=ts_: v.tensor_tensor(
                            hbuf[:, fc, ts_ * 512:(ts_ + 1) * 512], g2.buf[:], u2.buf[:], ALU.mult), f.cV)
                        g2.sem, g2.tgt = f.cV, t_h
                        u2.sem, u2.tgt = f.cV, t_h
                    gsl.sem, gsl.tgt = f.cT, t_mm
                    usl.sem, usl.tgt = f.cT, t_mm
                h_ready = (f.cV, t_h)
                Bsb.sem, Bsb.tgt = f.cV, t_h
                if DEV_MOE_TAPS and half == 0 and e == 0:
                    k.waits("gpsimd", [(Bsb.ready_sem, Bsb.ready_tgt), h_ready])
                    tb = k.dma("gpsimd", dr["dbg_Bsb"][:], Bsb.buf[:], f.dW)
                    th2 = k.dma("gpsimd", dr["dbg_h"][:].rearrange("(c p) t -> p c t", p=128),
                                hbuf[:], f.dW)
                    Bsb.sem, Bsb.tgt = f.dW, tb
                    emit_moe_inner._hb_guard = (f.dW, th2)

                k.waits("gpsimd", [(f.dA, prev_acc)])
                for oc in range(DC):
                    dsl = dwb.next()
                    k.waits("gpsimd", [(dsl.sem, dsl.tgt)])
                    dsl.ready_sem, dsl.ready_tgt = f.dW, k.dma(
                        "gpsimd", dsl.buf[:],
                        dr["dw"][e, :, oc * 128:(oc + 1) * 128].rearrange("(c p) o -> p c o", p=128), f.dW)
                    ds = dsb.next()
                    t_de = 0
                    for ts_ in range(4):
                        pd = psd.next()
                        k.waits("tensor", [(dsl.ready_sem, dsl.ready_tgt), (pd.sem, pd.tgt),
                                           h_ready])
                        def mkd(t, d=dsl.buf, ts_=ts_, pd=pd):
                            for fc in range(FC_MOE):
                                inst = t.matmul(pd.buf[:], d[:, fc, :],
                                                hbuf[:, fc, ts_ * 512:(ts_ + 1) * 512],
                                                start=(fc == 0), stop=(fc == FC_MOE - 1))
                            return inst
                        t_dm = k.op("tensor", mkd, f.cT)
                        k.waits("scalar", [(f.cT, t_dm)] + ([(ds.sem, ds.tgt)] if ts_ == 0 else []))
                        t_de = k.op("scalar", lambda s, pd=pd, ds=ds, ts_=ts_: s.activation(
                            ds.buf[:, ts_ * 512:(ts_ + 1) * 512], pd.buf[:], AF.Copy), f.cS)
                        pd.sem, pd.tgt = f.cS, t_de
                    k.waits("gpsimd", [(f.cS, t_de)])
                    acc = {} if e == 0 else dict(accum_op=ALU.add)
                    st = k.dma("gpsimd", dr["yacc"][oc * 128:(oc + 1) * 128, th0:th0 + S],
                               ds.buf[:], f.dA, **acc)
                    ds.sem, ds.tgt = f.dA, st
                    dsl.sem, dsl.tgt = f.cT, f.cT.n
                prev_acc = f.dA.n
            xnh.sem, xnh.tgt = f.cT, f.cT.n
        acc_done = (f.dA, f.dA.n)
    fence_all(k)

    with scope(k):
        xi = Rot([k.sbuf(f"rxi{i}", [128, 512], F32) for i in range(3)])
        yi = Rot([k.sbuf(f"ryi{i}", [128, 512], F32) for i in range(3)])
        zo = Rot([k.sbuf(f"rzo{i}", [128, 512], F32) for i in range(3)])
        for oc in range(DC):
            for tt in range(TT_ // 512):
                s_ = (tt * 512) // S
                c0 = (tt * 512) % S
                xs_, ys_, zs_ = xi.next(), yi.next(), zo.next()
                k.waits("sync", [(xs_.sem, xs_.tgt)])
                xs_.ready_sem, xs_.ready_tgt = f.dL, k.dma(
                    "sync", xs_.buf[:], x_src[s_, oc * 128:(oc + 1) * 128, c0:c0 + 512], f.dL)
                k.waits("sync", [(ys_.sem, ys_.tgt)])
                ys_.ready_sem, ys_.ready_tgt = f.dL, k.dma(
                    "sync", ys_.buf[:], dr["yacc"][oc * 128:(oc + 1) * 128, tt * 512:tt * 512 + 512], f.dL)
                k.waits("vector", [(xs_.ready_sem, xs_.ready_tgt), (ys_.ready_sem, ys_.ready_tgt),
                                   (zs_.sem, zs_.tgt)])
                t_a = k.op("vector", lambda v, xs_=xs_, ys_=ys_, zs_=zs_: v.tensor_tensor(
                    zs_.buf[:], xs_.buf[:], ys_.buf[:], ALU.add), f.cV)
                xs_.sem, xs_.tgt = f.cV, t_a
                ys_.sem, ys_.tgt = f.cV, t_a
                k.waits("sync", [(f.cV, t_a)])
                st = k.dma("sync", x_dst[s_, oc * 128:(oc + 1) * 128, c0:c0 + 512], zs_.buf[:], f.dS)
                zs_.sem, zs_.tgt = f.dS, st
    fence_all(k)



def emit_ffn1_block(k, f, dr, x_src, out_dst):
    ISEC = 512
    NSEC = I_FFN // ISEC
    with scope(k):
        ln = make_ln(k, f, "fln")
        xn = Slot(k.sbuf("fxn", [128, DC, S], BF16))

        def dump(seq):
            def extra(tt, c0, tln, dep):
                k.waits("sync", dep)
                k.dma("sync", dr["xnT"][:].rearrange("(c p) t -> p c t", p=128)
                      [:, :, seq * S + c0: seq * S + c0 + tln],
                      xn.buf[:, :, c0:c0 + tln], f.dS)
            return extra

        for s_ in range(BC):
            k.waits("vector", [(xn.sem, xn.tgt)])
            emit_ln_seq(k, f, ln, x_src, xn.buf, s_, extra=dump(s_))
            xn.sem, xn.tgt = f.dS, f.dS.n
        xnT_done = (f.dS, f.dS.n)
    fence_all(k)

    with scope(k):
        xnh = Slot(k.sbuf("fxnh", [128, DC, S], BF16))
        wgb = Rot([k.sbuf(f"fwg{i}", [128, DC, ISEC], BF16) for i in range(2)])
        wub = Rot([k.sbuf(f"fwu{i}", [128, DC, ISEC], BF16) for i in range(2)])
        wdb = Rot([k.sbuf(f"fwd{i}", [128, ISEC // 128, D], BF16) for i in range(1)])
        hspT = k.sbuf("fhspT", [128, ISEC // 128, S], BF16)
        psgr = Rot([k.psum(f"fpsg{i}", [128, 512], F32) for i in range(2)])
        psur = Rot([k.psum(f"fpsu{i}", [128, 512], F32) for i in range(2)])
        pst = Rot([k.psum(f"fpst{i}", [128, 1024], BF16) for i in range(2)])
        psd = Rot([k.psum(f"fpsd{i}", [128, 512], F32) for i in range(2)])
        gs = Rot([k.sbuf(f"fgs{i}", [128, 512], BF16) for i in range(2)])
        hf = Rot([k.sbuf(f"fhf{i}", [128, 512], F32) for i in range(2)])
        ta = k.sbuf("fta", [128, 512], F32)
        tb = k.sbuf("ftb", [128, 512], F32)
        hm = Rot([k.sbuf(f"fhm{i}", [128, 512], BF16) for i in range(2)])
        dsb = Rot([k.sbuf(f"fdsb{i}", [128, S], F32) for i in range(1)])

        prev_acc = 0
        for half in range(2):
            th0 = half * S
            k.waits("sync", [(xnh.sem, xnh.tgt), xnT_done])
            xnh.ready_sem, xnh.ready_tgt = f.dL, k.dma(
                "sync", xnh.buf[:],
                dr["xnT"][:].rearrange("(c p) t -> p c t", p=128)[:, :, th0:th0 + S], f.dL)
            for isec in range(NSEC):
                g_sl, u_sl, d_sl = wgb.next(), wub.next(), wdb.next()
                k.waits("gpsimd", [(g_sl.sem, g_sl.tgt)])
                g_sl.ready_sem, g_sl.ready_tgt = f.dW, k.dma(
                    "gpsimd", g_sl.buf[:],
                    dr["wg1"][:, isec * ISEC:(isec + 1) * ISEC].rearrange("(c p) i -> p c i", p=128), f.dW)
                k.waits("gpsimd", [(u_sl.sem, u_sl.tgt)])
                u_sl.ready_sem, u_sl.ready_tgt = f.dW, k.dma(
                    "gpsimd", u_sl.buf[:],
                    dr["wu1"][:, isec * ISEC:(isec + 1) * ISEC].rearrange("(c p) i -> p c i", p=128), f.dW)
                k.waits("gpsimd", [(d_sl.sem, d_sl.tgt)])
                d_sl.ready_sem, d_sl.ready_tgt = f.dW, k.dma(
                    "gpsimd", d_sl.buf[:],
                    dr["wd1"][isec * ISEC:(isec + 1) * ISEC, :].rearrange("(c p) o -> p c o", p=128), f.dW)

                t_tre = 0
                for tc in range(S // 128):
                    psg, psu = psgr.next(), psur.next()
                    k.waits("tensor", [(g_sl.ready_sem, g_sl.ready_tgt), (u_sl.ready_sem, u_sl.ready_tgt),
                                       (psg.sem, psg.tgt), (psu.sem, psu.tgt),
                                       (xnh.ready_sem, xnh.ready_tgt)])
                    def mkgu(t, g=g_sl.buf, u=u_sl.buf, tc=tc, psg=psg, psu=psu):
                        for c in range(DC):
                            t.matmul(psg.buf[:], xnh.buf[:, c, tc * 128:(tc + 1) * 128], g[:, c, :],
                                     start=(c == 0), stop=(c == DC - 1))
                        for c in range(DC):
                            inst = t.matmul(psu.buf[:], xnh.buf[:, c, tc * 128:(tc + 1) * 128], u[:, c, :],
                                            start=(c == 0), stop=(c == DC - 1))
                        return inst
                    t_mm = k.op("tensor", mkgu, f.cT)
                    g2, h2, m2 = gs.next(), hf.next(), hm.next()
                    k.waits("scalar", [(f.cT, t_mm), (g2.sem, g2.tgt)])
                    t_si = k.op("scalar", lambda s, g2=g2, psg=psg: s.activation(g2.buf[:], psg.buf[:], AF.Silu), f.cS)
                    psg.sem, psg.tgt = f.cS, t_si
                    k.waits("vector", [(f.cS, t_si), (h2.sem, h2.tgt), (m2.sem, m2.tgt)])
                    t_h = k.chain("vector", lambda v, g2=g2, h2=h2, psu=psu: v.tensor_tensor(
                        h2.buf[:], g2.buf[:], psu.buf[:], ALU.mult), f.cV)
                    psu.sem, psu.tgt = f.cV, t_h
                    g2.sem, g2.tgt = f.cV, t_h
                    hb = h2.buf
                    A = [ta[:, i:512:4] for i in range(4)]
                    k.chain("vector", lambda v, hb=hb: v.tensor_scalar(ta[:], hb[:], -1.0, None, ALU.mult), f.cV)
                    k.chain("vector", lambda v, hb=hb: v.tensor_tensor(ta[:], hb[:], ta[:], ALU.max), f.cV)
                    k.chain("vector", lambda v: v.tensor_tensor(tb[:, 0:128], A[0], A[1], ALU.max), f.cV)
                    k.chain("vector", lambda v: v.tensor_tensor(tb[:, 128:256], A[2], A[3], ALU.max), f.cV)
                    k.chain("vector", lambda v: v.tensor_tensor(tb[:, 256:384], A[0], A[1], ALU.min), f.cV)
                    k.chain("vector", lambda v: v.tensor_tensor(tb[:, 384:512], A[2], A[3], ALU.min), f.cV)
                    k.chain("vector", lambda v: v.tensor_tensor(tb[:, 0:128], tb[:, 0:128], tb[:, 128:256], ALU.min), f.cV)
                    k.chain("vector", lambda v: v.tensor_tensor(tb[:, 256:384], tb[:, 256:384], tb[:, 384:512], ALU.max), f.cV)
                    k.chain("vector", lambda v: v.tensor_tensor(tb[:, 0:128], tb[:, 0:128], tb[:, 256:384], ALU.max), f.cV)
                    for i in range(4):
                        k.chain("vector", lambda v, i=i: v.tensor_tensor(A[i], A[i], tb[:, 0:128], ALU.is_ge), f.cV)
                    t_hm = k.chain("vector", lambda v, hb=hb, m2=m2: v.tensor_tensor(
                        m2.buf[:], hb[:], ta[:], ALU.mult), f.cV)
                    h2.sem, h2.tgt = f.cV, t_hm
                    ps2 = pst.next()
                    k.waits("tensor", [(f.cV, t_hm), (ps2.sem, ps2.tgt)])
                    def mkT(t, m2=m2, p=ps2.buf):
                        for ic in range(4):
                            inst = t.transpose(p[:, ic * 128:(ic + 1) * 128],
                                               m2.buf[:, ic * 128:(ic + 1) * 128], f.identb[:])
                        return inst
                    t_tr = k.op("tensor", mkT, f.cT)
                    m2.sem, m2.tgt = f.cT, t_tr
                    k.waits("scalar", [(f.cT, t_tr)])
                    t_tre = k.op("scalar", lambda s, p=ps2.buf, tc=tc: s.activation(
                        hspT[:, :, tc * 128:(tc + 1) * 128], p[:, 0:512], AF.Copy), f.cS)
                    ps2.sem, ps2.tgt = f.cS, t_tre
                hsp_ready = (f.cS, t_tre)

                k.waits("gpsimd", [(f.dA, prev_acc)])
                for oc in range(DC):
                    ds = dsb.next()
                    t_de = 0
                    for ts2 in range(4):
                        pd = psd.next()
                        k.waits("tensor", [(d_sl.ready_sem, d_sl.ready_tgt), (pd.sem, pd.tgt),
                                           hsp_ready])
                        def mkd(t, d=d_sl.buf, oc=oc, ts2=ts2, pd=pd):
                            for ic in range(4):
                                inst = t.matmul(pd.buf[:], d[:, ic, oc * 128:(oc + 1) * 128],
                                                hspT[:, ic, ts2 * 512:(ts2 + 1) * 512],
                                                start=(ic == 0), stop=(ic == 3))
                            return inst
                        t_dm = k.op("tensor", mkd, f.cT)
                        k.waits("scalar", [(f.cT, t_dm)] + ([(ds.sem, ds.tgt)] if ts2 == 0 else []))
                        t_de = k.op("scalar", lambda s, pd=pd, ds=ds, ts2=ts2: s.activation(
                            ds.buf[:, ts2 * 512:(ts2 + 1) * 512], pd.buf[:], AF.Copy), f.cS)
                        pd.sem, pd.tgt = f.cS, t_de
                    k.waits("gpsimd", [(f.cS, t_de)])
                    acc = {} if isec == 0 else dict(accum_op=ALU.add)
                    st = k.dma("gpsimd", dr["yacc"][oc * 128:(oc + 1) * 128, th0:th0 + S],
                               ds.buf[:], f.dA, **acc)
                    ds.sem, ds.tgt = f.dA, st
                prev_acc = f.dA.n
                d_sl.sem, d_sl.tgt = f.cT, f.cT.n
                g_sl.sem, g_sl.tgt = f.cT, f.cT.n
                u_sl.sem, u_sl.tgt = f.cT, f.cT.n
            xnh.sem, xnh.tgt = f.cT, f.cT.n
    fence_all(k)

    with scope(k):
        xi = Rot([k.sbuf(f"oxi{i}", [128, 512], F32) for i in range(3)])
        yi = Rot([k.sbuf(f"oyi{i}", [128, 512], F32) for i in range(3)])
        zo = Rot([k.sbuf(f"ozo{i}", [128, 512], F32) for i in range(3)])
        for oc in range(DC):
            for tt in range(TT_ // 512):
                s_ = (tt * 512) // S
                c0 = (tt * 512) % S
                xs_, ys_, zs_ = xi.next(), yi.next(), zo.next()
                k.waits("sync", [(xs_.sem, xs_.tgt)])
                xs_.ready_sem, xs_.ready_tgt = f.dL, k.dma(
                    "sync", xs_.buf[:], x_src[s_, oc * 128:(oc + 1) * 128, c0:c0 + 512], f.dL)
                k.waits("sync", [(ys_.sem, ys_.tgt)])
                ys_.ready_sem, ys_.ready_tgt = f.dL, k.dma(
                    "sync", ys_.buf[:], dr["yacc"][oc * 128:(oc + 1) * 128, tt * 512:tt * 512 + 512], f.dL)
                k.waits("vector", [(xs_.ready_sem, xs_.ready_tgt), (ys_.ready_sem, ys_.ready_tgt),
                                   (zs_.sem, zs_.tgt)])
                t_a = k.op("vector", lambda v, xs_=xs_, ys_=ys_, zs_=zs_: v.tensor_tensor(
                    zs_.buf[:], xs_.buf[:], ys_.buf[:], ALU.add), f.cV)
                xs_.sem, xs_.tgt = f.cV, t_a
                ys_.sem, ys_.tgt = f.cV, t_a
                k.waits("sync", [(f.cV, t_a)])
                st = k.dma("sync", out_dst[s_, oc * 128:(oc + 1) * 128, c0:c0 + 512], zs_.buf[:], f.dS)
                zs_.sem, zs_.tgt = f.dS, st
    fence_all(k)


def build_program(upto=8, debug_out=()):
    nc = bass.Bass("TRN2", target_bir_lowering=False)
    dr = declare_drams(nc, upto, debug_out)
    with ExitStack() as ctx:
        k = K(nc, ctx)
        f = setup_frame(k, dr)
        for eng in ENGINES:
            k.waits(eng, f.consts_ready)

        emit_attn_block(k, f, dr, 0, dr["x_t"], dr["xa"])
        if upto >= 4:
            emit_moe_block(k, f, dr, dr["xa"], dr["xb_"])
        if upto >= 5:
            emit_attn_block(k, f, dr, 1, dr["xb_"], dr["xa"])
        if upto >= 8:
            emit_ffn1_block(k, f, dr, dr["xa"], dr["out_t"])

        if dr["_dbg"]:
            fence_all(k)
            for name, mirror in dr["_dbg"].items():
                k.dma("gpsimd", mirror[:], dr[name][:], f.dW)

        if f.dS.n:
            k.wait("sync", f.dS, f.dS.n)
        if f.dA.n:
            k.wait("gpsimd", f.dA, f.dA.n)
        if f.dW.n:
            k.wait("gpsimd", f.dW, f.dW.n)
        if f.dL.n:
            k.wait("sync", f.dL, f.dL.n)
        block = ctx.enter_context(nc.Block())
        k.playback(block)
    return nc


def kernel(**inputs):
    per_core = host_layout(inputs)
    nc = build_program()
    res = run_bass_kernel_spmd(nc, per_core, core_ids=list(range(N_CORES)))
    outs = []
    for c in range(N_CORES):
        ot = np.asarray(res.results[c]["out_t"])
        outs.append(ot.transpose(0, 2, 1))
    return _c(np.concatenate(outs, axis=0)).astype(np.float32)


# version 03
# speedup vs baseline: 1.2811x; 1.2811x over previous
import numpy as np
import ml_dtypes
from contextlib import ExitStack

import concourse.bass as bass
import concourse.mybir as mybir
from concourse.bass_utils import run_bass_kernel_spmd

BF16 = mybir.dt.bfloat16
F32 = mybir.dt.float32
AF = mybir.ActivationFunctionType
ALU = mybir.AluOpType
AX = mybir.AxisListType

N_CORES = 8
B = 16
BC = B // N_CORES
S = 2048
D = 2048
DC = D // 128
TT_ = BC * S
NE = 8
F_MOE = 1408
FC_MOE = F_MOE // 128
I_FFN = 8192
LN_EPS = 1e-5


ENGINES = ("sync", "tensor", "vector", "scalar", "gpsimd")


class Sem:
    def __init__(self, k, name):
        self.h = k.ctx.enter_context(k.nc.semaphore(name))
        self.n = 0

    def bump(self, amount):
        self.n += amount
        return self.n


class K:

    def __init__(self, nc, ctx):
        self.nc = nc
        self.ctx = ctx
        self.q = {e: [] for e in ENGINES}
        self._sems = {}

    def sem(self, name):
        if name not in self._sems:
            self._sems[name] = Sem(self, name)
        return self._sems[name]

    def emit(self, eng, fn):
        self.q[eng].append(fn)

    def wait(self, eng, sem, val):
        if sem is None or val is None or val <= 0:
            return
        assert val <= sem.n, f"wait_ge({val}) but sem only reaches {sem.n}"
        self.emit(eng, lambda e, h=sem.h, v=val: e.wait_ge(h, v))

    def waits(self, eng, deps):
        for sem, val in deps:
            self.wait(eng, sem, val)

    def op(self, eng, fn, sem=None, amount=1):
        if sem is None:
            self.emit(eng, fn)
            return None
        tgt = sem.bump(amount)
        self.emit(eng, lambda e, fn=fn, h=sem.h, a=amount: fn(e).then_inc(h, a))
        return tgt

    def chain(self, eng, fn, sem):
        t = self.op(eng, fn, sem)
        self.wait(eng, sem, t)
        return t

    def dma(self, eng, out, in_, sem, **kw):
        return self.op(eng, lambda e, o=out, i=in_, kw=kw: e.dma_start(out=o, in_=i, **kw),
                       sem, 16)

    def _uniq(self, name):
        self._n = getattr(self, "_n", 0) + 1
        return f"{name}__{self._n}"

    def sbuf(self, name, shape, dtype):
        return self.ctx.enter_context(self.nc.sbuf_tensor(self._uniq(name), list(shape), dtype))

    def psum(self, name, shape, dtype=F32):
        return self.ctx.enter_context(self.nc.psum_tensor(self._uniq(name), list(shape), dtype))

    def fence_targets(self):
        return [(s, s.n) for s in self._sems.values() if s.n > 0]

    def playback(self, block):
        for eng in ENGINES:
            fns = self.q[eng]
            if fns:
                getattr(block, eng)(lambda e, fns=fns: [fn(e) for fn in fns])


class Slot:

    __slots__ = ("buf", "sem", "tgt", "ready_sem", "ready_tgt")

    def __init__(self, buf):
        self.buf = buf
        self.sem = None
        self.tgt = 0
        self.ready_sem = None
        self.ready_tgt = 0


class Rot:
    def __init__(self, bufs):
        self.slots = [Slot(b) for b in bufs]
        self.i = 0

    def next(self):
        s = self.slots[self.i % len(self.slots)]
        self.i += 1
        return s



def _c(a):
    return np.ascontiguousarray(a)


def host_constants():
    ident = np.eye(128, dtype=np.float32)
    tri = np.where(np.arange(128)[None, :] <= np.arange(128)[:, None], 0.0, -1e9).astype(np.float32)
    oh = np.zeros((8, 8 * 128), dtype=np.float32)
    for e in range(8):
        oh[e, e * 128:(e + 1) * 128] = 1.0
    ones_col = np.ones((128, 1), dtype=np.float32)
    half = 32
    inv_freq = 1.0 / 10000.0 ** (np.arange(0, half, dtype=np.float32) / half)
    t = np.arange(S, dtype=np.float32)
    freqs = np.einsum("i,j->ij", t, inv_freq)
    emb = np.concatenate([freqs, freqs], axis=-1)
    cos_t = _c(np.cos(emb).T.astype(np.float32))
    sin_t = _c(np.sin(emb).T.astype(np.float32))
    return dict(ident=ident, tri=tri, onehots=oh, ones_col=ones_col,
                cos_t=cos_t, sin_t=sin_t)


def host_layout(inputs):
    g = {}
    for nm, src in (("wq0", "L0_attn_Wq"), ("wk0", "L0_attn_Wk"),
                    ("wv0", "L0_attn_Wv"), ("wo0", "L0_attn_Wo"),
                    ("wq1", "L1_attn_Wq"), ("wk1", "L1_attn_Wk"),
                    ("wv1", "L1_attn_Wv"), ("wo1", "L1_attn_Wo")):
        g[nm] = _c(np.asarray(inputs[src]).T)
    g["rw"] = _c(np.asarray(inputs["L0_ffn_router_w"]).T)
    g["gw"] = _c(np.asarray(inputs["L0_ffn_gate_w"]).transpose(0, 2, 1))
    g["uw"] = _c(np.asarray(inputs["L0_ffn_up_w"]).transpose(0, 2, 1))
    g["dw"] = _c(np.asarray(inputs["L0_ffn_down_w"]).transpose(0, 2, 1))
    for nm, src in (("bq1", "L1_attn_bq"), ("bk1", "L1_attn_bk"),
                    ("bv1", "L1_attn_bv"), ("bo1", "L1_attn_bo")):
        g[nm] = _c(np.asarray(inputs[src]).reshape(DC, 128).T)
    g["wg1"] = _c(np.asarray(inputs["L1_ffn_w_gate"]))
    g["wu1"] = _c(np.asarray(inputs["L1_ffn_w_up"]))
    g["wd1"] = _c(np.asarray(inputs["L1_ffn_w_down"]))
    g.update(host_constants())

    x = np.asarray(inputs["L0_attn_x"])
    per_core = []
    for c in range(N_CORES):
        m = dict(g)
        m["x_t"] = _c(x[c * BC:(c + 1) * BC].transpose(0, 2, 1))
        per_core.append(m)
    return per_core



class Frame:
    pass


def setup_frame(k, dr):
    f = Frame()
    f.cT = k.sem("cT")
    f.cV = k.sem("cV")
    f.cS = k.sem("cS")
    f.cG = k.sem("cG")
    f.dL = k.sem("dL")
    f.dW = k.sem("dW")
    f.dS = k.sem("dS")
    f.dA = k.sem("dA")

    f.identb = k.sbuf("identb", [128, 128], BF16)
    f.identf = k.sbuf("identf", [128, 128], F32)
    f.oh = k.sbuf("oh_sb", [8, 1024], F32)
    f.ones_col = k.sbuf("ones_col_sb", [128, 1], BF16)
    f.tri = k.sbuf("tri_sb", [128, 128], F32)
    f.cos_t = k.sbuf("cos_sb", [64, S], F32)
    f.sin_t = k.sbuf("sin_sb", [64, S], F32)

    k.dma("gpsimd", f.identb[:], dr["ident"][:], f.dW)
    k.dma("gpsimd", f.ones_col[:], dr["ones_col"][:], f.dW)
    k.dma("sync", f.identf[:], dr["ident"][:], f.dL)
    k.dma("sync", f.oh[:], dr["onehots"][:], f.dL)
    k.dma("sync", f.tri[:], dr["tri"][:], f.dL)
    k.dma("sync", f.cos_t[:], dr["cos_t"][:], f.dL)
    k.dma("sync", f.sin_t[:], dr["sin_t"][:], f.dL)
    f.consts_ready = [(f.dW, f.dW.n), (f.dL, f.dL.n)]
    return f



def emit_ln_seq(k, f, ln, x_src, xn_buf, seq, extra=None):
    TLN = LN_TILE
    ntile = S // TLN
    last = None
    tile_ready = []
    for tt in range(ntile):
        c0 = tt * TLN
        xf = ln.xf.next()
        xb = ln.xb.next()
        src = x_src[seq].rearrange("(c p) t -> p c t", p=128)[:, :, c0:c0 + TLN]
        k.waits("sync", [(xf.sem, xf.tgt)])
        xf.ready_sem, xf.ready_tgt = f.dL, k.dma("sync", xf.buf[:], src, f.dL)
        k.waits("gpsimd", [(xb.sem, xb.tgt)])
        xb.ready_sem, xb.ready_tgt = f.dW, k.dma("gpsimd", xb.buf[:], src, f.dW)

        k.waits("scalar", [(xf.ready_sem, xf.ready_tgt), (ln.xsq.slots[0].sem, ln.xsq.slots[0].tgt)])
        xsq = ln.xsq.next()
        t_sq = k.op("scalar", lambda s, a=xsq.buf, b=xf.buf: s.activation(
            a[:], b[:].rearrange("p c t -> p (c t)"), AF.Square), f.cS)

        ps_s, ps_q = ln.ps_s, ln.ps_q
        k.waits("tensor", [(xb.ready_sem, xb.ready_tgt), (ps_s.sem, ps_s.tgt), (ps_q.sem, ps_q.tgt),
                           (f.cS, t_sq)])
        def mk_sums(t, xbb=xb.buf, xqq=xsq.buf, pss=ps_s.buf, psq=ps_q.buf):
            for c in range(DC):
                t.matmul(pss[:], ln.ones_col[:], xbb[:, c, :], start=(c == 0), stop=(c == DC - 1))
            for c in range(DC):
                inst = t.matmul(psq[:], ln.ones_col[:], xqq[:, c, :], start=(c == 0), stop=(c == DC - 1))
            return inst
        t_mm = k.op("tensor", mk_sums, f.cT)

        k.wait("vector", f.cT, t_mm)
        k.wait("vector", f.cS, ln.prev_rstd)
        k.chain("vector", lambda v: v.tensor_scalar(ln.mu[:], ps_s.buf[:], 1.0 / D, None, ALU.mult), f.cV)
        k.chain("vector", lambda v: v.tensor_scalar(ln.msq[:], ps_q.buf[:], 1.0 / D, None, ALU.mult), f.cV)
        k.chain("vector", lambda v: v.tensor_tensor(ln.var[:], ln.mu[:], ln.mu[:], ALU.mult), f.cV)
        t_var = k.chain("vector", lambda v: v.scalar_tensor_tensor(
            ln.var[:], ln.msq[:], float(LN_EPS), ln.var[:], ALU.add, ALU.subtract), f.cV)
        ps_s.sem, ps_s.tgt = f.cV, t_var
        ps_q.sem, ps_q.tgt = f.cV, t_var
        k.wait("scalar", f.cV, t_var)
        k.wait("scalar", f.cT, ln.prev_bc)
        k.chain("scalar", lambda s: s.activation(ln.lnv[:], ln.var[:], AF.Ln), f.cS)
        t_rstd = k.chain("scalar", lambda s: s.activation(ln.a_sb[:], ln.lnv[:], AF.Exp, scale=-0.5), f.cS)
        ln.prev_rstd = t_rstd
        k.wait("vector", f.cS, t_rstd)
        t_b = k.chain("vector", lambda v: v.scalar_tensor_tensor(
            ln.b_sb[:], ln.mu[:], -1.0, ln.a_sb[:], ALU.mult, ALU.mult), f.cV)

        psA, psB = ln.psA, ln.psB
        k.waits("tensor", [(f.cV, t_b), (psA.sem, psA.tgt), (psB.sem, psB.tgt)])
        def mk_bc(t):
            t.matmul(psA.buf[:], f.oh[0:1, 0:128], ln.a_sb[:], start=True, stop=True)
            return t.matmul(psB.buf[:], f.oh[0:1, 0:128], ln.b_sb[:], start=True, stop=True)
        t_bc = k.op("tensor", mk_bc, f.cT)
        ln.prev_bc = t_bc

        k.waits("vector", [(f.cT, t_bc)])
        for c in range(DC):
            k.op("vector", lambda v, c=c, xff=xf.buf: v.tensor_tensor(
                ln.t1[:], xff[:, c, :], psA.buf[:], ALU.mult))
            lastv = k.op("vector", lambda v, c=c, c0=c0: v.tensor_tensor(
                xn_buf[:, c, c0:c0 + TLN], ln.t1[:], psB.buf[:], ALU.add), f.cV)
        psA.sem, psA.tgt = f.cV, lastv
        psB.sem, psB.tgt = f.cV, lastv
        xf.sem, xf.tgt = f.cV, lastv
        xb.sem, xb.tgt = f.cT, t_mm
        xsq.sem, xsq.tgt = f.cT, t_mm
        last = lastv
        tile_ready.append(lastv)
        if extra is not None:
            extra(tt, c0, TLN, [(f.cV, lastv)])
    return f.cV, last, tile_ready


LN_TILE = 256


def make_ln(k, f, name="ln"):
    ln = Frame()
    TLN = LN_TILE
    ln.xf = Rot([k.sbuf(f"{name}_xf{i}", [128, DC, TLN], F32) for i in range(2)])
    ln.xb = Rot([k.sbuf(f"{name}_xb{i}", [128, DC, TLN], BF16) for i in range(2)])
    ln.xsq = Rot([k.sbuf(f"{name}_xsq{i}", [128, DC, TLN], BF16) for i in range(2)])
    ln.ones_col = f.ones_col
    ln.ps_s = Slot(k.psum(f"{name}_ps_s", [1, TLN], F32))
    ln.ps_q = Slot(k.psum(f"{name}_ps_q", [1, TLN], F32))
    ln.psA = Slot(k.psum(f"{name}_psA", [128, TLN], F32))
    ln.psB = Slot(k.psum(f"{name}_psB", [128, TLN], F32))
    for sl in (ln.ps_s, ln.ps_q, ln.psA, ln.psB):
        sl.buf = sl.buf
    ln.mu = k.sbuf(f"{name}_mu", [1, TLN], F32)
    ln.msq = k.sbuf(f"{name}_msq", [1, TLN], F32)
    ln.var = k.sbuf(f"{name}_var", [1, TLN], F32)
    ln.lnv = k.sbuf(f"{name}_lnv", [1, TLN], F32)
    ln.a_sb = k.sbuf(f"{name}_a", [1, TLN], F32)
    ln.b_sb = k.sbuf(f"{name}_b", [1, TLN], F32)
    ln.t1 = k.sbuf(f"{name}_t1", [128, TLN], F32)
    ln.prev_rstd = 0
    ln.prev_bc = 0
    return ln



def emit_qkv_seq(k, f, pj, layer, seq, xn_buf, xn_ready, dr, ln_tiles=None):
    OS = 512
    TPROJ = 512
    qscale = (1.0 / np.sqrt(128.0)) if layer == 0 else 0.125
    wlist = (("wq", "q"), ("wk", "k"), ("wv", "v"))
    for wname, role in wlist:
        wsrc = dr[wname + str(layer)]
        dst = dr["qT" if role == "q" else ("kT" if role == "k" else "vT")]
        for os_ in range(D // OS):
            wslot = pj.w.next()
            k.waits("gpsimd", [(wslot.sem, wslot.tgt)])
            wslot.ready_sem, wslot.ready_tgt = f.dW, k.dma(
                "gpsimd", wslot.buf[:],
                wsrc[:, os_ * OS:(os_ + 1) * OS].rearrange("(c p) o -> p c o", p=128), f.dW)
            for oc in range(OS // 128):
                orow = os_ * OS + oc * 128
                for tt in range(S // TPROJ):
                    c0 = tt * TPROJ
                    ps = pj.ps.next()
                    if ln_tiles is not None:
                        nper = TPROJ // LN_TILE
                        xr = (xn_ready[0], ln_tiles[(tt + 1) * nper - 1])
                    else:
                        xr = xn_ready
                    k.waits("tensor", [(wslot.ready_sem, wslot.ready_tgt), (ps.sem, ps.tgt),
                                       xr])
                    def mk(t, w=wslot.buf, oc=oc, c0=c0, p=ps.buf):
                        for c in range(DC):
                            inst = t.matmul(p[:], w[:, c, oc * 128:(oc + 1) * 128],
                                            xn_buf[:, c, c0:c0 + TPROJ],
                                            start=(c == 0), stop=(c == DC - 1))
                        return inst
                    t_mm = k.op("tensor", mk, f.cT)
                    if layer == 0 or role == "v":
                        ev = pj.ev.next()
                        if layer == 0:
                            sc = qscale if role == "q" else 1.0
                            k.waits("scalar", [(f.cT, t_mm), (ev.sem, ev.tgt)])
                            t_ev = k.op("scalar", lambda s, p=ps.buf, e=ev.buf, sc=sc:
                                        s.activation(e[:], p[:], AF.Copy, scale=sc), f.cS)
                        else:
                            bv = pj.bias["v"]
                            k.waits("scalar", [(f.cT, t_mm), (ev.sem, ev.tgt)])
                            t_ev = k.op("scalar", lambda s, p=ps.buf, e=ev.buf, oc2=os_ * (OS // 128) + oc:
                                        s.activation(e[:], p[:], AF.Sigmoid, bias=bv[:, oc2:oc2 + 1]), f.cS)
                        ps.sem, ps.tgt = f.cS, t_ev
                        k.wait("sync", f.cS, t_ev)
                        st = k.dma("sync", dst[seq, orow:orow + 128, c0:c0 + TPROJ], ev.buf[:], f.dS)
                        ev.sem, ev.tgt = f.dS, st
                    else:
                        bmap = pj.bias[role]
                        oc2 = os_ * (OS // 128) + oc
                        sc = qscale if role == "q" else 1.0
                        t_ev = None
                        for hh in range(2):
                            stg = pj.rp_in.next()
                            rot = pj.rp_rot.next()
                            evb = pj.rp_out.next()
                            prow = hh * 64
                            k.waits("vector", [(f.cT, t_mm), (stg.sem, stg.tgt), (rot.sem, rot.tgt),
                                               (evb.sem, evb.tgt)])
                            if role == "q":
                                k.op("vector", lambda v, p=ps.buf, sg=stg.buf, oc2=oc2, prow=prow, sc=sc, bmap=bmap:
                                     v.tensor_scalar(sg[:], p[prow:prow + 64, :],
                                                     bmap[prow:prow + 64, oc2:oc2 + 1], sc, ALU.add, ALU.mult))
                            else:
                                k.op("vector", lambda v, p=ps.buf, sg=stg.buf, oc2=oc2, prow=prow, bmap=bmap:
                                     v.tensor_scalar(sg[:], p[prow:prow + 64, :],
                                                     bmap[prow:prow + 64, oc2:oc2 + 1], None, ALU.add))
                            k.op("vector", lambda v, sg=stg.buf, r=rot.buf:
                                 v.tensor_scalar(r[0:32, :], sg[32:64, :], -1.0, None, ALU.mult))
                            k.op("vector", lambda v, sg=stg.buf, r=rot.buf:
                                 v.tensor_copy(r[32:64, :], sg[0:32, :]))
                            k.op("vector", lambda v, sg=stg.buf, c0=c0:
                                 v.tensor_tensor(sg[:], sg[:], f.cos_t[:, c0:c0 + TPROJ], ALU.mult))
                            k.op("vector", lambda v, r=rot.buf, c0=c0:
                                 v.tensor_tensor(r[:], r[:], f.sin_t[:, c0:c0 + TPROJ], ALU.mult))
                            t_ev = k.op("vector", lambda v, sg=stg.buf, r=rot.buf, e=evb.buf:
                                        v.tensor_tensor(e[:], sg[:], r[:], ALU.add), f.cV)
                            stg.sem, stg.tgt = f.cV, t_ev
                            rot.sem, rot.tgt = f.cV, t_ev
                            k.wait("sync", f.cV, t_ev)
                            st = k.dma("sync", dst[seq, orow + prow:orow + prow + 64, c0:c0 + TPROJ],
                                       evb.buf[:], f.dS)
                            evb.sem, evb.tgt = f.dS, st
                        ps.sem, ps.tgt = f.cV, t_ev
            wslot.sem, wslot.tgt = f.cT, t_mm


def make_proj(k, f, layer):
    pj = Frame()
    OS, TPROJ = 512, 512
    pj.w = Rot([k.sbuf(f"pj_w{i}", [128, DC, OS], BF16) for i in range(2)])
    pj.ps = Rot([Slot(None) for _ in range(0)])
    pj.ps = Rot([k.psum(f"pj_ps{i}", [128, TPROJ], F32) for i in range(3)])
    pj.ev = Rot([k.sbuf(f"pj_ev{i}", [128, TPROJ], BF16) for i in range(3)])
    if layer == 1:
        pj.rp_in = Rot([k.sbuf(f"pj_ri{i}", [64, TPROJ], F32) for i in range(2)])
        pj.rp_rot = Rot([k.sbuf(f"pj_rr{i}", [64, TPROJ], F32) for i in range(2)])
        pj.rp_out = Rot([k.sbuf(f"pj_ro{i}", [64, TPROJ], BF16) for i in range(3)])
        pj.bias = {}
    return pj


def _rotpsum(k, name, n, shape):
    return Rot([k.psum(f"{name}{i}", list(shape), F32) for i in range(n)])



def emit_attn_seq(k, f, at, layer, seq, o_seq, dr):
    dh = 128 if layer == 0 else 64
    nheads = D // dh
    QG = 512
    NG = S // QG

    def s_stage(qs, ks, qg, par):
        P = at.P[par]
        k.waits("vector", [(P.sem, P.tgt)])
        t_e = 0
        nk = (qg + 1) * (QG // 128)
        for qi in range(4):
            qc = qg * 4 + qi
            kcols = (qc + 1) * 128
            ntile = (kcols + 511) // 512
            for i in range(ntile):
                psl = at.s_ps[i]
                w = min(512, kcols - i * 512)
                k.waits("tensor", [(psl.sem, psl.tgt), (qs.ready_sem, qs.ready_tgt),
                                   (ks.ready_sem, ks.ready_tgt)])
                t_s = k.op("tensor", lambda t, i=i, w=w, qc=qc, p=psl.buf, q=qs.buf, kk=ks.buf:
                           t.matmul(p[:, 0:w], q[0:dh, qc * 128:(qc + 1) * 128],
                                    kk[0:dh, i * 512:i * 512 + w], start=True, stop=True), f.cT)
                psl.sem, psl.tgt = f.cT, t_s
            dtile = at.s_ps[ntile - 1]
            dcol = (kcols - 128) - (ntile - 1) * 512
            k.waits("vector", [(f.cT, dtile.tgt), (at.negm_g[qi][0], at.negm_g[qi][1])])
            k.op("vector", lambda v, p=dtile.buf, dcol=dcol:
                 v.tensor_tensor(p[:, dcol:dcol + 128], p[:, dcol:dcol + 128],
                                 f.tri[:], ALU.add))
            for i in range(ntile):
                w = min(512, kcols - i * 512)
                k.op("vector", lambda v, i=i, w=w, p=at.s_ps[i].buf:
                     v.reduce_max(at.mx[:, i:i + 1], p[:, 0:w], axis=AX.X))
            t_nm = k.chain("vector", lambda v, nt=ntile, qi=qi:
                           v.reduce_max(at.negm[:, qi:qi + 1], at.mx[:, 0:nt], axis=AX.X,
                                        negate=True), f.cV)
            if kcols < nk * 128:
                k.op("vector", lambda v, P=P, qi=qi, a=kcols, b=nk * 128:
                     v.memset(P.buf[:, qi, a:b], 0.0))
            k.wait("scalar", f.cV, t_nm)
            for i in range(ntile):
                w = min(512, kcols - i * 512)
                t_e = k.op("scalar", lambda s, i=i, w=w, p=at.s_ps[i].buf, qi=qi, P=P:
                           s.activation(P.buf[:, qi, i * 512:i * 512 + w], p[:, 0:w],
                                        AF.Exp, bias=at.negm[:, qi:qi + 1]), f.cS)
                at.s_ps[i].sem, at.s_ps[i].tgt = f.cS, t_e
            at.negm_g[qi] = (f.cS, t_e)
        t_pt = k.op("vector", lambda v: v.tensor_copy(at.mx[:, 0:1], at.mx[:, 0:1]), f.cV)
        return [(f.cS, t_e), (f.cV, t_pt)]

    def pv_stage(h, qg, par, p_ready, vtok, vtok_ready):
        P = at.P[par]
        nk = (qg + 1) * (QG // 128)
        po, prb = at.po, at.prb
        t_pv = 0
        for kc in range(nk):
            ptr = at.ptr.next()
            k.waits("tensor", [(ptr.sem, ptr.tgt)] + (p_ready if kc == 0 else []))
            def mkT(t, kc=kc, p=ptr.buf, P=P):
                for qi in range(4):
                    inst = t.transpose(p[:, qi * 128:(qi + 1) * 128],
                                       P.buf[:, qi, kc * 128:(kc + 1) * 128], f.identb[:])
                return inst
            t_tr = k.op("tensor", mkT, f.cT)
            pts = at.pts.next()
            k.waits("scalar", [(f.cT, t_tr), (pts.sem, pts.tgt)])
            t_pe = k.op("scalar", lambda s, p=ptr.buf, o=pts.buf:
                        s.activation(o[:], p[:, 0:512], AF.Copy), f.cS)
            ptr.sem, ptr.tgt = f.cS, t_pe
            k.waits("tensor", [(f.cS, t_pe)] +
                    ([(po.sem, po.tgt), (prb.sem, prb.tgt), vtok_ready] if kc == 0 else []))
            def mkPV(t, kc=kc, pt=pts.buf, last=(kc == nk - 1), vt=vtok.buf):
                t.matmul(po.buf[0:dh, :], vt[:, kc * dh:(kc + 1) * dh], pt[:],
                         start=(kc == 0), stop=last)
                return t.matmul(prb.buf[0:1, :], f.ones_col[:], pt[:],
                                start=(kc == 0), stop=last)
            t_pv = k.op("tensor", mkPV, f.cT)
            pts.sem, pts.tgt = f.cT, t_pv
        P.sem, P.tgt = f.cT, t_pv
        k.wait("vector", f.cT, t_pv)
        t_rl = k.op("vector", lambda v: v.reciprocal(at.rl[:], prb.buf[0:1, :]), f.cV)
        k.waits("tensor", [(f.cV, t_rl)])
        t_bc = k.op("tensor", lambda t: t.matmul(prb.buf[:], f.oh[0:1, 0:128], at.rl[:],
                                                 start=True, stop=True), f.cT)
        k.waits("scalar", [(f.cT, t_bc)])
        t_rbc = k.op("scalar", lambda s: s.activation(at.rlb[:], prb.buf[:], AF.Copy), f.cS)
        prb.sem, prb.tgt = f.cS, t_rbc
        ch, r0 = (h, 0) if layer == 0 else (h // 2, (h % 2) * 64)
        k.waits("vector", [(f.cS, t_rbc), (f.cT, t_pv)])
        t_out = k.op("vector", lambda v, ch=ch, r0=r0, qg=qg:
                     v.tensor_tensor(o_seq.buf[r0:r0 + dh, ch, qg * QG:(qg + 1) * QG],
                                     po.buf[0:dh, :], at.rlb[0:dh, :], ALU.mult), f.cV)
        po.sem, po.tgt = f.cV, t_out

    at.negm_g = [(None, 0)] * 4
    par = 0
    for h in range(nheads):
        qs, ks, vs = at.qh.next(), at.kh.next(), at.vh.next()
        rows = (h * dh, (h + 1) * dh)
        for sl, src in ((qs, dr["qT"]), (ks, dr["kT"]), (vs, dr["vT"])):
            k.waits("sync", [(sl.sem, sl.tgt)])
            sl.ready_sem, sl.ready_tgt = f.dL, k.dma(
                "sync", sl.buf[0:dh, :], src[seq, rows[0]:rows[1], :], f.dL)

        p_ready = s_stage(qs, ks, 0, par)
        vtok = at.vtok.next()
        t_ev = 0
        for kc in range(S // 128):
            ptr = at.ptr.next()
            k.waits("tensor", [(ptr.sem, ptr.tgt), (vs.ready_sem, vs.ready_tgt)])
            t_tr = k.op("tensor", lambda t, kc=kc, p=ptr.buf, v=vs.buf:
                        t.transpose(p[:, 0:dh], v[0:dh, kc * 128:(kc + 1) * 128],
                                    f.identb[0:dh, 0:dh]), f.cT)
            deps = [(f.cT, t_tr)] + ([(vtok.sem, vtok.tgt)] if kc == 0 else [])
            k.waits("scalar", deps)
            t_ev = k.op("scalar", lambda s, kc=kc, p=ptr.buf, vt=vtok.buf:
                        s.activation(vt[:, kc * dh:(kc + 1) * dh], p[:, 0:dh], AF.Copy), f.cS)
            ptr.sem, ptr.tgt = f.cS, t_ev
        vtok_ready = (f.cS, t_ev)
        vs.sem, vs.tgt = f.cT, f.cT.n

        for qg in range(NG):
            nxt = None
            if qg + 1 < NG:
                nxt = s_stage(qs, ks, qg + 1, 1 - par)
            pv_stage(h, qg, par, p_ready, vtok, vtok_ready)
            p_ready = nxt
            par = 1 - par
        qs.sem, qs.tgt = f.cT, f.cT.n
        ks.sem, ks.tgt = f.cT, f.cT.n
        vtok.sem, vtok.tgt = f.cT, f.cT.n
    o_seq.ready_sem, o_seq.ready_tgt = f.cV, f.cV.n
    return f.cV, f.cV.n


def make_attn(k, f, layer):
    at = Frame()
    dh = 128 if layer == 0 else 64
    at.qh = Rot([k.sbuf(f"at_q{i}", [128, S], BF16) for i in range(2)])
    at.kh = Rot([k.sbuf(f"at_k{i}", [128, S], BF16) for i in range(2)])
    at.vh = Rot([k.sbuf(f"at_v{i}", [128, S], BF16) for i in range(2)])
    at.vtok = Rot([k.sbuf(f"at_vtok{i}", [128, (S // 128) * dh], BF16) for i in range(2)])
    at.s_ps = [Slot(k.psum(f"at_s{i}", [128, 512], F32)) for i in range(4)]
    at.ptr = Rot([k.psum(f"at_ptr{i}", [128, 1024], BF16) for i in range(2)])
    at.po = Slot(k.psum("at_po", [128, 512], F32))
    at.prb = Slot(k.psum("at_prb", [128, 512], F32))
    at.P = [Slot(k.sbuf(f"at_P{i}", [128, 4, S], BF16)) for i in range(2)]
    at.mx = k.sbuf("at_mx", [128, 4], F32)
    at.negm = k.sbuf("at_negm", [128, 4], F32)
    at.pts = Rot([k.sbuf(f"at_pts{i}", [128, 512], BF16) for i in range(2)])
    at.rl = k.sbuf("at_rl", [1, 512], F32)
    at.rlb = k.sbuf("at_rlb", [128, 512], F32)
    return at



def emit_oproj_seq(k, f, pr, layer, seq, o_seq, o_ready, x_src, x_dst, dr, bias=None):
    OS, TPROJ = 512, 512
    wsrc = dr["wo" + str(layer)]
    for os_ in range(D // OS):
        wslot = pr.w.next()
        k.waits("gpsimd", [(wslot.sem, wslot.tgt)])
        wslot.ready_sem, wslot.ready_tgt = f.dW, k.dma(
            "gpsimd", wslot.buf[:],
            wsrc[:, os_ * OS:(os_ + 1) * OS].rearrange("(c p) o -> p c o", p=128), f.dW)
        t_mm = None
        for oc in range(OS // 128):
            orow = os_ * OS + oc * 128
            oc2 = os_ * (OS // 128) + oc
            for tt in range(S // TPROJ):
                c0 = tt * TPROJ
                ps = pr.ps.next()
                xin = pr.xin.next()
                k.waits("sync", [(xin.sem, xin.tgt)])
                xin.ready_sem, xin.ready_tgt = f.dL, k.dma(
                    "sync", xin.buf[:], x_src[seq, orow:orow + 128, c0:c0 + TPROJ], f.dL)
                k.waits("tensor", [(wslot.ready_sem, wslot.ready_tgt), (ps.sem, ps.tgt), o_ready])
                def mk(t, w=wslot.buf, oc=oc, c0=c0, p=ps.buf):
                    for c in range(DC):
                        inst = t.matmul(p[:], w[:, c, oc * 128:(oc + 1) * 128],
                                        o_seq[:, c, c0:c0 + TPROJ],
                                        start=(c == 0), stop=(c == DC - 1))
                    return inst
                t_mm = k.op("tensor", mk, f.cT)
                ev = pr.ev.next()
                k.waits("vector", [(f.cT, t_mm), (ev.sem, ev.tgt),
                                   (xin.ready_sem, xin.ready_tgt)])
                if bias is None:
                    t_ev = k.op("vector", lambda v, p=ps.buf, e=ev.buf, xi=xin.buf:
                                v.tensor_tensor(e[:], p[:], xi[:], ALU.add), f.cV)
                else:
                    t_ev = k.op("vector", lambda v, p=ps.buf, e=ev.buf, xi=xin.buf, oc2=oc2:
                                v.scalar_tensor_tensor(e[:], p[:], bias[:, oc2:oc2 + 1], xi[:],
                                                       ALU.add, ALU.add), f.cV)
                ps.sem, ps.tgt = f.cV, t_ev
                xin.sem, xin.tgt = f.cV, t_ev
                k.wait("sync", f.cV, t_ev)
                st = k.dma("sync", x_dst[seq, orow:orow + 128, c0:c0 + TPROJ], ev.buf[:], f.dS)
                ev.sem, ev.tgt = f.dS, st
        wslot.sem, wslot.tgt = f.cT, t_mm


def make_oproj(k, f):
    pr = Frame()
    OS, TPROJ = 512, 512
    pr.w = Rot([k.sbuf(f"pr_w{i}", [128, DC, OS], BF16) for i in range(2)])
    pr.ps = Rot([k.psum(f"pr_ps{i}", [128, TPROJ], F32) for i in range(2)])
    pr.xin = Rot([k.sbuf(f"pr_x{i}", [128, TPROJ], F32) for i in range(3)])
    pr.ev = Rot([k.sbuf(f"pr_e{i}", [128, TPROJ], F32) for i in range(3)])
    return pr



def declare_drams(nc, upto, debug_out=()):
    dr = {}
    dr["_dbg"] = {}
    def ext_in(name, shape, dtype=F32):
        dr[name] = nc.dram_tensor(name, list(shape), dtype, kind="ExternalInput")
    def scratch(name, shape, dtype):
        if name in debug_out and dtype == F32:
            dr[name] = nc.dram_tensor(name, list(shape), dtype, kind="ExternalOutput")
            return
        dr[name] = nc.dram_tensor(name, list(shape), dtype, kind="Internal")
        if name in debug_out:
            dr["_dbg"][name] = nc.dram_tensor("dbg_" + name, list(shape), F32,
                                              kind="ExternalOutput")

    ext_in("x_t", [BC, D, S])
    for nm in ("wq0", "wk0", "wv0", "wo0", "wq1", "wk1", "wv1", "wo1"):
        ext_in(nm, [D, D])
    ext_in("rw", [D, 8])
    ext_in("gw", [NE, D, F_MOE]); ext_in("uw", [NE, D, F_MOE]); ext_in("dw", [NE, F_MOE, D])
    for nm in ("bq1", "bk1", "bv1", "bo1"):
        ext_in(nm, [128, DC])
    ext_in("wg1", [D, I_FFN]); ext_in("wu1", [D, I_FFN]); ext_in("wd1", [I_FFN, D])
    ext_in("ident", [128, 128]); ext_in("tri", [128, 128]); ext_in("onehots", [8, 1024])
    ext_in("ones_col", [128, 1]); ext_in("cos_t", [64, S]); ext_in("sin_t", [64, S])

    scratch("qT", [BC, D, S], BF16)
    scratch("kT", [BC, D, S], BF16)
    scratch("vT", [BC, D, S], BF16)
    scratch("xa", [BC, D, S], F32)
    scratch("xb_", [BC, D, S], F32)
    scratch("xnT", [D, TT_], BF16)
    scratch("yacc", [D, TT_], F32)
    dr["out_t"] = nc.dram_tensor("out_t", [BC, D, S], F32, kind="ExternalOutput")
    return dr



from contextlib import contextmanager


@contextmanager
def scope(k):
    old = k.ctx
    with ExitStack() as es:
        k.ctx = es
        try:
            yield
        finally:
            k.ctx = old


def fence_all(k):
    tgts = k.fence_targets()
    for eng in ENGINES:
        k.waits(eng, tgts)



def emit_attn_block(k, f, dr, layer, x_src, x_dst):
    with scope(k):
        ln = make_ln(k, f)
        pj = make_proj(k, f, layer)
        xn = Slot(k.sbuf("xn_seq", [128, DC, S], BF16))
        if layer == 1:
            pj.bias = {}
            for role, nm in (("q", "bq1"), ("k", "bk1"), ("v", "bv1")):
                bb = k.sbuf(f"bias_{nm}", [128, DC], F32)
                k.dma("sync", bb[:], dr[nm][:], f.dL)
                pj.bias[role] = bb
            k.waits("scalar", [(f.dL, f.dL.n)])
            k.waits("vector", [(f.dL, f.dL.n)])
        for s_ in range(BC):
            k.waits("vector", [(xn.sem, xn.tgt)])
            sem, tgt, tiles = emit_ln_seq(k, f, ln, x_src, xn.buf, s_)
            emit_qkv_seq(k, f, pj, layer, s_, xn.buf, (sem, tgt), dr, ln_tiles=tiles)
            xn.sem, xn.tgt = f.cT, f.cT.n
    fence_all(k)
    with scope(k):
        o_seq = Slot(k.sbuf("o_seq", [128, DC, S], BF16))
        bo = None
        if layer == 1:
            bo = k.sbuf("bias_bo1", [128, DC], F32)
            k.dma("sync", bo[:], dr["bo1"][:], f.dL)
            k.waits("vector", [(f.dL, f.dL.n)])
        for s_ in range(BC):
            with scope(k):
                at = make_attn(k, f, layer)
                emit_attn_seq(k, f, at, layer, s_, o_seq, dr)
            fence_all(k)
            with scope(k):
                pr = make_oproj(k, f)
                emit_oproj_seq(k, f, pr, layer, s_, o_seq.buf,
                               (o_seq.ready_sem, o_seq.ready_tgt), x_src, x_dst, dr, bias=bo)
            fence_all(k)
            o_seq.sem, o_seq.tgt = f.cT, f.cT.n
    fence_all(k)



DEV_MOE_TAPS = False


def emit_moe_block(k, f, dr, x_src, x_dst):
    TLN = 512
    if DEV_MOE_TAPS:
        dr["dbg_rt"] = k.nc.dram_tensor("dbg_rt", [128, 48], F32, kind="ExternalOutput")
        dr["dbg_combT"] = k.nc.dram_tensor("dbg_combT", [8, TT_], F32, kind="ExternalOutput")
        dr["dbg_Bsb"] = k.nc.dram_tensor("dbg_Bsb", [128, 4, 512], F32, kind="ExternalOutput")
        dr["dbg_h"] = k.nc.dram_tensor("dbg_h", [F_MOE, S], F32, kind="ExternalOutput")
    with scope(k):
        combT = k.sbuf("combT", [8, TT_], BF16)
        ohb = k.sbuf("ohb", [8, 1024], BF16)
        k.dma("gpsimd", ohb[:], dr["onehots"][:], f.dW)
        emit_moe_inner(k, f, dr, x_src, x_dst, combT, ohb)
    fence_all(k)


def emit_moe_inner(k, f, dr, x_src, x_dst, combT, ohb):
    TLN = 512
    with scope(k):
        ln = make_ln(k, f, "mln")
        xn = Slot(k.sbuf("mxn", [128, DC, S], BF16))
        rwb = k.sbuf("rwb", [128, DC, 8], BF16)
        k.dma("gpsimd", rwb[:], dr["rw"][:].rearrange("(c p) e -> p c e", p=128), f.dW)
        rt = Frame()
        rt.pl = Slot(k.psum("rt_pl", [128, 512], F32))
        rt.pt = Slot(k.psum("rt_pt", [8, 512], F32))
        rt.L = k.sbuf("rt_L", [128, 8], F32)
        rt.m1 = k.sbuf("rt_m1", [128, 1], F32)
        rt.nm1 = k.sbuf("rt_nm1", [128, 1], F32)
        rt.m2 = k.sbuf("rt_m2", [128, 1], F32)
        rt.eq = k.sbuf("rt_eq", [128, 8], F32)
        rt.E = k.sbuf("rt_E", [128, 8], F32)
        rt.we = k.sbuf("rt_we", [128, 8], F32)
        rt.ssum = k.sbuf("rt_ss", [128, 1], F32)
        rt.rs = k.sbuf("rt_rs", [128, 1], F32)
        rt.comb = k.sbuf("rt_comb", [128, 8], F32)
        k.waits("tensor", [(f.dW, f.dW.n)])

        def router(seq):
            def extra(tt, c0, tln, dep):
                k.waits("sync", dep)
                k.dma("sync", dr["xnT"][:].rearrange("(c p) t -> p c t", p=128)
                      [:, :, seq * S + c0: seq * S + c0 + tln],
                      xn.buf[:, :, c0:c0 + tln], f.dS)
                for j in range(tln // 128):
                    t0 = c0 + j * 128
                    k.waits("tensor", dep + [(rt.pl.sem, rt.pl.tgt)])
                    def mkl(t, t0=t0):
                        for c in range(DC):
                            inst = t.matmul(rt.pl.buf[:, 0:8], xn.buf[:, c, t0:t0 + 128],
                                            rwb[:, c, :], start=(c == 0), stop=(c == DC - 1))
                        return inst
                    t_l = k.op("tensor", mkl, f.cT)
                    k.waits("vector", [(f.cT, t_l)])
                    k.chain("vector", lambda v: v.tensor_copy(rt.L[:], rt.pl.buf[:, 0:8]), f.cV)
                    t_lc = k.chain("vector", lambda v: v.reduce_max(rt.m1[:], rt.L[:], axis=AX.X), f.cV)
                    rt.pl.sem, rt.pl.tgt = f.cV, t_lc
                    k.chain("vector", lambda v: v.tensor_scalar(rt.nm1[:], rt.m1[:], -1.0, None, ALU.mult), f.cV)
                    k.chain("vector", lambda v: v.tensor_scalar(rt.eq[:], rt.L[:], rt.m1[:, 0:1], None, ALU.is_equal), f.cV)
                    k.chain("vector", lambda v: v.scalar_tensor_tensor(rt.eq[:], rt.eq[:], -1e30, rt.L[:], ALU.mult, ALU.add), f.cV)
                    t_m2 = k.chain("vector", lambda v: v.reduce_max(rt.m2[:], rt.eq[:], axis=AX.X), f.cV)
                    k.waits("scalar", [(f.cV, t_m2)])
                    t_E = k.op("scalar", lambda s: s.activation(rt.E[:], rt.L[:], AF.Exp, bias=rt.nm1[:, 0:1]), f.cS)
                    k.waits("vector", [(f.cS, t_E)])
                    k.chain("vector", lambda v: v.tensor_scalar(rt.we[:], rt.L[:], rt.m2[:, 0:1], None, ALU.is_ge), f.cV)
                    k.chain("vector", lambda v: v.tensor_tensor(rt.we[:], rt.we[:], rt.E[:], ALU.mult), f.cV)
                    k.chain("vector", lambda v: v.reduce_sum(rt.ssum[:], rt.we[:], axis=AX.X), f.cV)
                    k.chain("vector", lambda v: v.reciprocal(rt.rs[:], rt.ssum[:]), f.cV)
                    t_cb = k.chain("vector", lambda v: v.tensor_scalar(rt.comb[:], rt.we[:], rt.rs[:, 0:1], None, ALU.mult), f.cV)
                    if DEV_MOE_TAPS and seq == 0 and t0 == 0:
                        k.waits("gpsimd", [(f.cV, t_cb), (f.cS, f.cS.n)])
                        k.dma("gpsimd", dr["dbg_rt"][:, 0:8], rt.L[:], f.dW, allow_slow_non_contiguous=True)
                        k.dma("gpsimd", dr["dbg_rt"][:, 8:16], rt.E[:], f.dW, allow_slow_non_contiguous=True)
                        k.dma("gpsimd", dr["dbg_rt"][:, 16:24], rt.we[:], f.dW, allow_slow_non_contiguous=True)
                        k.dma("gpsimd", dr["dbg_rt"][:, 24:32], rt.comb[:], f.dW, allow_slow_non_contiguous=True)
                        k.dma("gpsimd", dr["dbg_rt"][:, 32:33], rt.m1[:], f.dW, allow_slow_non_contiguous=True)
                        k.dma("gpsimd", dr["dbg_rt"][:, 33:34], rt.m2[:], f.dW, allow_slow_non_contiguous=True)
                        k.dma("gpsimd", dr["dbg_rt"][:, 34:35], rt.rs[:], f.dW, allow_slow_non_contiguous=True)
                        k.dma("gpsimd", dr["dbg_rt"][:, 35:36], rt.nm1[:], f.dW, allow_slow_non_contiguous=True)
                    k.waits("tensor", [(f.cV, t_cb), (rt.pt.sem, rt.pt.tgt)])
                    t_tr = k.op("tensor", lambda t: t.transpose(rt.pt.buf[:, 0:128], rt.comb[:], f.identf[:]), f.cT)
                    k.waits("scalar", [(f.cT, t_tr)])
                    t_ce = k.op("scalar", lambda s, seq=seq, t0=t0: s.activation(
                        combT[:, seq * S + t0: seq * S + t0 + 128], rt.pt.buf[:, 0:128], AF.Copy), f.cS)
                    rt.pt.sem, rt.pt.tgt = f.cS, t_ce
            return extra

        for s_ in range(BC):
            k.waits("vector", [(xn.sem, xn.tgt)])
            emit_ln_seq(k, f, ln, x_src, xn.buf, s_, extra=router(s_))
            xn.sem, xn.tgt = f.cT, f.cT.n
        del ln
        comb_done = (f.cS, f.cS.n)
        xnT_done = (f.dS, f.dS.n)
        if DEV_MOE_TAPS:
            k.waits("gpsimd", [comb_done])
            k.dma("gpsimd", dr["dbg_combT"][:], combT[:], f.dW)
    fence_all(k)

    with scope(k):
        xnh = Slot(k.sbuf("xnh", [128, DC, S], BF16))
        Bsb = Slot(k.sbuf("Bsb", [128, 4, 512], F32))
        psB = Slot(k.psum("psB", [128, 512], F32))
        psgr = Rot([k.psum(f"psg{i}", [128, 512], F32) for i in range(2)])
        psur = Rot([k.psum(f"psu{i}", [128, 512], F32) for i in range(2)])
        psd = Rot([k.psum(f"psd{i}", [128, 512], F32) for i in range(2)])
        gwb = Rot([k.sbuf(f"gwb{i}", [128, DC, 128], BF16) for i in range(2)])
        uwb = Rot([k.sbuf(f"uwb{i}", [128, DC, 128], BF16) for i in range(2)])
        dwb = Rot([k.sbuf(f"dwb{i}", [128, FC_MOE, 128], BF16) for i in range(2)])
        gs = Rot([k.sbuf(f"gs{i}", [128, 512], BF16) for i in range(2)])
        ut = Rot([k.sbuf(f"ut{i}", [128, 512], BF16) for i in range(2)])
        hbuf = k.sbuf("hbuf", [128, FC_MOE, S], BF16)
        dsb = Rot([k.sbuf(f"dsb{i}", [128, S], F32) for i in range(2)])

        prev_acc = 0
        for half in range(2):
            th0 = half * S
            k.waits("sync", [(xnh.sem, xnh.tgt), xnT_done])
            xnh.ready_sem, xnh.ready_tgt = f.dL, k.dma(
                "sync", xnh.buf[:],
                dr["xnT"][:].rearrange("(c p) t -> p c t", p=128)[:, :, th0:th0 + S], f.dL)
            for e in range(NE):
                t_bev = 0
                for ts_ in range(4):
                    k.waits("tensor", [(psB.sem, psB.tgt), comb_done])
                    t_b = k.op("tensor", lambda t, e=e, ts_=ts_, th0=th0: t.matmul(
                        psB.buf[:], ohb[:, e * 128:(e + 1) * 128],
                        combT[:, th0 + ts_ * 512: th0 + (ts_ + 1) * 512],
                        start=True, stop=True), f.cT)
                    k.waits("scalar", [(f.cT, t_b)] + ([(Bsb.sem, Bsb.tgt)] if ts_ == 0 else []))
                    t_bev = k.op("scalar", lambda s, ts_=ts_: s.activation(
                        Bsb.buf[:, ts_, :], psB.buf[:], AF.Copy), f.cS)
                    psB.sem, psB.tgt = f.cS, t_bev
                Bsb.ready_sem, Bsb.ready_tgt = f.cS, t_bev

                t_h = 0
                hb_guard = getattr(emit_moe_inner, "_hb_guard", None)
                for fc in range(FC_MOE):
                    gsl, usl = gwb.next(), uwb.next()
                    k.waits("gpsimd", [(gsl.sem, gsl.tgt)])
                    gsl.ready_sem, gsl.ready_tgt = f.dW, k.dma(
                        "gpsimd", gsl.buf[:],
                        dr["gw"][e, :, fc * 128:(fc + 1) * 128].rearrange("(c p) o -> p c o", p=128), f.dW)
                    k.waits("gpsimd", [(usl.sem, usl.tgt)])
                    usl.ready_sem, usl.ready_tgt = f.dW, k.dma(
                        "gpsimd", usl.buf[:],
                        dr["uw"][e, :, fc * 128:(fc + 1) * 128].rearrange("(c p) o -> p c o", p=128), f.dW)
                    for ts_ in range(4):
                        psg, psu = psgr.next(), psur.next()
                        k.waits("tensor", [(gsl.ready_sem, gsl.ready_tgt), (usl.ready_sem, usl.ready_tgt),
                                           (psg.sem, psg.tgt), (psu.sem, psu.tgt),
                                           (xnh.ready_sem, xnh.ready_tgt)])
                        def mkgu(t, g=gsl.buf, u=usl.buf, ts_=ts_, psg=psg, psu=psu):
                            for c in range(DC):
                                t.matmul(psg.buf[:], g[:, c, :], xnh.buf[:, c, ts_ * 512:(ts_ + 1) * 512],
                                         start=(c == 0), stop=(c == DC - 1))
                            for c in range(DC):
                                inst = t.matmul(psu.buf[:], u[:, c, :], xnh.buf[:, c, ts_ * 512:(ts_ + 1) * 512],
                                                start=(c == 0), stop=(c == DC - 1))
                            return inst
                        t_mm = k.op("tensor", mkgu, f.cT)
                        g2, u2 = gs.next(), ut.next()
                        k.waits("scalar", [(f.cT, t_mm), (g2.sem, g2.tgt)])
                        t_si = k.op("scalar", lambda s, g2=g2, psg=psg: s.activation(g2.buf[:], psg.buf[:], AF.Silu), f.cS)
                        psg.sem, psg.tgt = f.cS, t_si
                        k.waits("vector", [(f.cT, t_mm), (u2.sem, u2.tgt),
                                           (Bsb.ready_sem, Bsb.ready_tgt)])
                        t_ut = k.op("vector", lambda v, u2=u2, ts_=ts_, psu=psu: v.tensor_tensor(
                            u2.buf[:], psu.buf[:], Bsb.buf[:, ts_, :], ALU.mult), f.cV)
                        psu.sem, psu.tgt = f.cV, t_ut
                        k.waits("vector", [(f.cS, t_si)] + ([hb_guard] if hb_guard else []))
                        hb_guard = None
                        t_h = k.op("vector", lambda v, g2=g2, u2=u2, fc=fc, ts_=ts_: v.tensor_tensor(
                            hbuf[:, fc, ts_ * 512:(ts_ + 1) * 512], g2.buf[:], u2.buf[:], ALU.mult), f.cV)
                        g2.sem, g2.tgt = f.cV, t_h
                        u2.sem, u2.tgt = f.cV, t_h
                    gsl.sem, gsl.tgt = f.cT, t_mm
                    usl.sem, usl.tgt = f.cT, t_mm
                h_ready = (f.cV, t_h)
                Bsb.sem, Bsb.tgt = f.cV, t_h
                if DEV_MOE_TAPS and half == 0 and e == 0:
                    k.waits("gpsimd", [(Bsb.ready_sem, Bsb.ready_tgt), h_ready])
                    tb = k.dma("gpsimd", dr["dbg_Bsb"][:], Bsb.buf[:], f.dW)
                    th2 = k.dma("gpsimd", dr["dbg_h"][:].rearrange("(c p) t -> p c t", p=128),
                                hbuf[:], f.dW)
                    Bsb.sem, Bsb.tgt = f.dW, tb
                    emit_moe_inner._hb_guard = (f.dW, th2)

                k.waits("gpsimd", [(f.dA, prev_acc)])
                for oc in range(DC):
                    dsl = dwb.next()
                    k.waits("gpsimd", [(dsl.sem, dsl.tgt)])
                    dsl.ready_sem, dsl.ready_tgt = f.dW, k.dma(
                        "gpsimd", dsl.buf[:],
                        dr["dw"][e, :, oc * 128:(oc + 1) * 128].rearrange("(c p) o -> p c o", p=128), f.dW)
                    ds = dsb.next()
                    t_de = 0
                    for ts_ in range(4):
                        pd = psd.next()
                        k.waits("tensor", [(dsl.ready_sem, dsl.ready_tgt), (pd.sem, pd.tgt),
                                           h_ready])
                        def mkd(t, d=dsl.buf, ts_=ts_, pd=pd):
                            for fc in range(FC_MOE):
                                inst = t.matmul(pd.buf[:], d[:, fc, :],
                                                hbuf[:, fc, ts_ * 512:(ts_ + 1) * 512],
                                                start=(fc == 0), stop=(fc == FC_MOE - 1))
                            return inst
                        t_dm = k.op("tensor", mkd, f.cT)
                        k.waits("scalar", [(f.cT, t_dm)] + ([(ds.sem, ds.tgt)] if ts_ == 0 else []))
                        t_de = k.op("scalar", lambda s, pd=pd, ds=ds, ts_=ts_: s.activation(
                            ds.buf[:, ts_ * 512:(ts_ + 1) * 512], pd.buf[:], AF.Copy), f.cS)
                        pd.sem, pd.tgt = f.cS, t_de
                    k.waits("gpsimd", [(f.cS, t_de)])
                    acc = {} if e == 0 else dict(accum_op=ALU.add)
                    st = k.dma("gpsimd", dr["yacc"][oc * 128:(oc + 1) * 128, th0:th0 + S],
                               ds.buf[:], f.dA, **acc)
                    ds.sem, ds.tgt = f.dA, st
                    dsl.sem, dsl.tgt = f.cT, f.cT.n
                prev_acc = f.dA.n
            xnh.sem, xnh.tgt = f.cT, f.cT.n
        acc_done = (f.dA, f.dA.n)
    fence_all(k)

    with scope(k):
        xi = Rot([k.sbuf(f"rxi{i}", [128, 512], F32) for i in range(3)])
        yi = Rot([k.sbuf(f"ryi{i}", [128, 512], F32) for i in range(3)])
        zo = Rot([k.sbuf(f"rzo{i}", [128, 512], F32) for i in range(3)])
        for oc in range(DC):
            for tt in range(TT_ // 512):
                s_ = (tt * 512) // S
                c0 = (tt * 512) % S
                xs_, ys_, zs_ = xi.next(), yi.next(), zo.next()
                k.waits("sync", [(xs_.sem, xs_.tgt)])
                xs_.ready_sem, xs_.ready_tgt = f.dL, k.dma(
                    "sync", xs_.buf[:], x_src[s_, oc * 128:(oc + 1) * 128, c0:c0 + 512], f.dL)
                k.waits("sync", [(ys_.sem, ys_.tgt)])
                ys_.ready_sem, ys_.ready_tgt = f.dL, k.dma(
                    "sync", ys_.buf[:], dr["yacc"][oc * 128:(oc + 1) * 128, tt * 512:tt * 512 + 512], f.dL)
                k.waits("vector", [(xs_.ready_sem, xs_.ready_tgt), (ys_.ready_sem, ys_.ready_tgt),
                                   (zs_.sem, zs_.tgt)])
                t_a = k.op("vector", lambda v, xs_=xs_, ys_=ys_, zs_=zs_: v.tensor_tensor(
                    zs_.buf[:], xs_.buf[:], ys_.buf[:], ALU.add), f.cV)
                xs_.sem, xs_.tgt = f.cV, t_a
                ys_.sem, ys_.tgt = f.cV, t_a
                k.waits("sync", [(f.cV, t_a)])
                st = k.dma("sync", x_dst[s_, oc * 128:(oc + 1) * 128, c0:c0 + 512], zs_.buf[:], f.dS)
                zs_.sem, zs_.tgt = f.dS, st
    fence_all(k)



def emit_ffn1_block(k, f, dr, x_src, out_dst):
    ISEC = 512
    NSEC = I_FFN // ISEC
    with scope(k):
        ln = make_ln(k, f, "fln")
        xn = Slot(k.sbuf("fxn", [128, DC, S], BF16))

        def dump(seq):
            def extra(tt, c0, tln, dep):
                k.waits("sync", dep)
                k.dma("sync", dr["xnT"][:].rearrange("(c p) t -> p c t", p=128)
                      [:, :, seq * S + c0: seq * S + c0 + tln],
                      xn.buf[:, :, c0:c0 + tln], f.dS)
            return extra

        for s_ in range(BC):
            k.waits("vector", [(xn.sem, xn.tgt)])
            emit_ln_seq(k, f, ln, x_src, xn.buf, s_, extra=dump(s_))
            xn.sem, xn.tgt = f.dS, f.dS.n
        xnT_done = (f.dS, f.dS.n)
    fence_all(k)

    with scope(k):
        xnh = Slot(k.sbuf("fxnh", [128, DC, S], BF16))
        wgb = Rot([k.sbuf(f"fwg{i}", [128, DC, ISEC], BF16) for i in range(2)])
        wub = Rot([k.sbuf(f"fwu{i}", [128, DC, ISEC], BF16) for i in range(2)])
        wdb = Rot([k.sbuf(f"fwd{i}", [128, ISEC // 128, D], BF16) for i in range(1)])
        hspT = k.sbuf("fhspT", [128, ISEC // 128, S], BF16)
        psgr = Rot([k.psum(f"fpsg{i}", [128, 512], F32) for i in range(2)])
        psur = Rot([k.psum(f"fpsu{i}", [128, 512], F32) for i in range(2)])
        pst = Rot([k.psum(f"fpst{i}", [128, 1024], BF16) for i in range(2)])
        psd = Rot([k.psum(f"fpsd{i}", [128, 512], F32) for i in range(2)])
        gs = Rot([k.sbuf(f"fgs{i}", [128, 512], BF16) for i in range(2)])
        hf = Rot([k.sbuf(f"fhf{i}", [128, 512], F32) for i in range(2)])
        ta = k.sbuf("fta", [128, 512], F32)
        tb = k.sbuf("ftb", [128, 512], F32)
        hm = Rot([k.sbuf(f"fhm{i}", [128, 512], BF16) for i in range(3)])
        dsb = Rot([k.sbuf(f"fdsb{i}", [128, S], F32) for i in range(1)])

        prev_acc = 0
        for half in range(2):
            th0 = half * S
            k.waits("sync", [(xnh.sem, xnh.tgt), xnT_done])
            xnh.ready_sem, xnh.ready_tgt = f.dL, k.dma(
                "sync", xnh.buf[:],
                dr["xnT"][:].rearrange("(c p) t -> p c t", p=128)[:, :, th0:th0 + S], f.dL)
            for isec in range(NSEC):
                g_sl, u_sl, d_sl = wgb.next(), wub.next(), wdb.next()
                k.waits("gpsimd", [(g_sl.sem, g_sl.tgt)])
                g_sl.ready_sem, g_sl.ready_tgt = f.dW, k.dma(
                    "gpsimd", g_sl.buf[:],
                    dr["wg1"][:, isec * ISEC:(isec + 1) * ISEC].rearrange("(c p) i -> p c i", p=128), f.dW)
                k.waits("gpsimd", [(u_sl.sem, u_sl.tgt)])
                u_sl.ready_sem, u_sl.ready_tgt = f.dW, k.dma(
                    "gpsimd", u_sl.buf[:],
                    dr["wu1"][:, isec * ISEC:(isec + 1) * ISEC].rearrange("(c p) i -> p c i", p=128), f.dW)
                k.waits("gpsimd", [(d_sl.sem, d_sl.tgt)])
                d_sl.ready_sem, d_sl.ready_tgt = f.dW, k.dma(
                    "gpsimd", d_sl.buf[:],
                    dr["wd1"][isec * ISEC:(isec + 1) * ISEC, :].rearrange("(c p) o -> p c o", p=128), f.dW)

                t_tre = 0
                pend = None
                def emit_tr(tc, m2, t_hm):
                    nonlocal t_tre
                    ps2 = pst.next()
                    k.waits("tensor", [(f.cV, t_hm), (ps2.sem, ps2.tgt)])
                    def mkT(t, m2=m2, p=ps2.buf):
                        for ic in range(4):
                            inst = t.transpose(p[:, ic * 128:(ic + 1) * 128],
                                               m2.buf[:, ic * 128:(ic + 1) * 128], f.identb[:])
                        return inst
                    t_tr = k.op("tensor", mkT, f.cT)
                    m2.sem, m2.tgt = f.cT, t_tr
                    k.waits("scalar", [(f.cT, t_tr)])
                    t_tre = k.op("scalar", lambda s, p=ps2.buf, tc=tc: s.activation(
                        hspT[:, :, tc * 128:(tc + 1) * 128], p[:, 0:512], AF.Copy), f.cS)
                    ps2.sem, ps2.tgt = f.cS, t_tre
                for tc in range(S // 128):
                    psg, psu = psgr.next(), psur.next()
                    k.waits("tensor", [(g_sl.ready_sem, g_sl.ready_tgt), (u_sl.ready_sem, u_sl.ready_tgt),
                                       (psg.sem, psg.tgt), (psu.sem, psu.tgt),
                                       (xnh.ready_sem, xnh.ready_tgt)])
                    def mkgu(t, g=g_sl.buf, u=u_sl.buf, tc=tc, psg=psg, psu=psu):
                        for c in range(DC):
                            t.matmul(psg.buf[:], xnh.buf[:, c, tc * 128:(tc + 1) * 128], g[:, c, :],
                                     start=(c == 0), stop=(c == DC - 1))
                        for c in range(DC):
                            inst = t.matmul(psu.buf[:], xnh.buf[:, c, tc * 128:(tc + 1) * 128], u[:, c, :],
                                            start=(c == 0), stop=(c == DC - 1))
                        return inst
                    t_mm = k.op("tensor", mkgu, f.cT)
                    g2, h2, m2 = gs.next(), hf.next(), hm.next()
                    k.waits("scalar", [(f.cT, t_mm), (g2.sem, g2.tgt)])
                    t_si = k.op("scalar", lambda s, g2=g2, psg=psg: s.activation(g2.buf[:], psg.buf[:], AF.Silu), f.cS)
                    psg.sem, psg.tgt = f.cS, t_si
                    k.waits("vector", [(f.cS, t_si), (h2.sem, h2.tgt), (m2.sem, m2.tgt)])
                    t_h = k.chain("vector", lambda v, g2=g2, h2=h2, psu=psu: v.tensor_tensor(
                        h2.buf[:], g2.buf[:], psu.buf[:], ALU.mult), f.cV)
                    psu.sem, psu.tgt = f.cV, t_h
                    g2.sem, g2.tgt = f.cV, t_h
                    hb = h2.buf
                    A = [ta[:, i:512:4] for i in range(4)]
                    k.chain("vector", lambda v, hb=hb: v.tensor_scalar(ta[:], hb[:], -1.0, None, ALU.mult), f.cV)
                    k.chain("vector", lambda v, hb=hb: v.tensor_tensor(ta[:], hb[:], ta[:], ALU.max), f.cV)
                    k.chain("vector", lambda v: v.tensor_tensor(tb[:, 0:128], A[0], A[1], ALU.max), f.cV)
                    k.chain("vector", lambda v: v.tensor_tensor(tb[:, 128:256], A[2], A[3], ALU.max), f.cV)
                    k.chain("vector", lambda v: v.tensor_tensor(tb[:, 256:384], A[0], A[1], ALU.min), f.cV)
                    k.chain("vector", lambda v: v.tensor_tensor(tb[:, 384:512], A[2], A[3], ALU.min), f.cV)
                    k.chain("vector", lambda v: v.tensor_tensor(tb[:, 0:128], tb[:, 0:128], tb[:, 128:256], ALU.min), f.cV)
                    k.chain("vector", lambda v: v.tensor_tensor(tb[:, 256:384], tb[:, 256:384], tb[:, 384:512], ALU.max), f.cV)
                    k.chain("vector", lambda v: v.tensor_tensor(tb[:, 0:128], tb[:, 0:128], tb[:, 256:384], ALU.max), f.cV)
                    for i in range(4):
                        k.chain("vector", lambda v, i=i: v.tensor_tensor(A[i], A[i], tb[:, 0:128], ALU.is_ge), f.cV)
                    t_hm = k.chain("vector", lambda v, hb=hb, m2=m2: v.tensor_tensor(
                        m2.buf[:], hb[:], ta[:], ALU.mult), f.cV)
                    h2.sem, h2.tgt = f.cV, t_hm
                    if pend is not None:
                        emit_tr(*pend)
                    pend = (tc, m2, t_hm)
                emit_tr(*pend)
                hsp_ready = (f.cS, t_tre)

                k.waits("gpsimd", [(f.dA, prev_acc)])
                for oc in range(DC):
                    ds = dsb.next()
                    t_de = 0
                    for ts2 in range(4):
                        pd = psd.next()
                        k.waits("tensor", [(d_sl.ready_sem, d_sl.ready_tgt), (pd.sem, pd.tgt),
                                           hsp_ready])
                        def mkd(t, d=d_sl.buf, oc=oc, ts2=ts2, pd=pd):
                            for ic in range(4):
                                inst = t.matmul(pd.buf[:], d[:, ic, oc * 128:(oc + 1) * 128],
                                                hspT[:, ic, ts2 * 512:(ts2 + 1) * 512],
                                                start=(ic == 0), stop=(ic == 3))
                            return inst
                        t_dm = k.op("tensor", mkd, f.cT)
                        k.waits("scalar", [(f.cT, t_dm)] + ([(ds.sem, ds.tgt)] if ts2 == 0 else []))
                        t_de = k.op("scalar", lambda s, pd=pd, ds=ds, ts2=ts2: s.activation(
                            ds.buf[:, ts2 * 512:(ts2 + 1) * 512], pd.buf[:], AF.Copy), f.cS)
                        pd.sem, pd.tgt = f.cS, t_de
                    k.waits("gpsimd", [(f.cS, t_de)])
                    acc = {} if isec == 0 else dict(accum_op=ALU.add)
                    st = k.dma("gpsimd", dr["yacc"][oc * 128:(oc + 1) * 128, th0:th0 + S],
                               ds.buf[:], f.dA, **acc)
                    ds.sem, ds.tgt = f.dA, st
                prev_acc = f.dA.n
                d_sl.sem, d_sl.tgt = f.cT, f.cT.n
                g_sl.sem, g_sl.tgt = f.cT, f.cT.n
                u_sl.sem, u_sl.tgt = f.cT, f.cT.n
            xnh.sem, xnh.tgt = f.cT, f.cT.n
    fence_all(k)

    with scope(k):
        xi = Rot([k.sbuf(f"oxi{i}", [128, 512], F32) for i in range(3)])
        yi = Rot([k.sbuf(f"oyi{i}", [128, 512], F32) for i in range(3)])
        zo = Rot([k.sbuf(f"ozo{i}", [128, 512], F32) for i in range(3)])
        for oc in range(DC):
            for tt in range(TT_ // 512):
                s_ = (tt * 512) // S
                c0 = (tt * 512) % S
                xs_, ys_, zs_ = xi.next(), yi.next(), zo.next()
                k.waits("sync", [(xs_.sem, xs_.tgt)])
                xs_.ready_sem, xs_.ready_tgt = f.dL, k.dma(
                    "sync", xs_.buf[:], x_src[s_, oc * 128:(oc + 1) * 128, c0:c0 + 512], f.dL)
                k.waits("sync", [(ys_.sem, ys_.tgt)])
                ys_.ready_sem, ys_.ready_tgt = f.dL, k.dma(
                    "sync", ys_.buf[:], dr["yacc"][oc * 128:(oc + 1) * 128, tt * 512:tt * 512 + 512], f.dL)
                k.waits("vector", [(xs_.ready_sem, xs_.ready_tgt), (ys_.ready_sem, ys_.ready_tgt),
                                   (zs_.sem, zs_.tgt)])
                t_a = k.op("vector", lambda v, xs_=xs_, ys_=ys_, zs_=zs_: v.tensor_tensor(
                    zs_.buf[:], xs_.buf[:], ys_.buf[:], ALU.add), f.cV)
                xs_.sem, xs_.tgt = f.cV, t_a
                ys_.sem, ys_.tgt = f.cV, t_a
                k.waits("sync", [(f.cV, t_a)])
                st = k.dma("sync", out_dst[s_, oc * 128:(oc + 1) * 128, c0:c0 + 512], zs_.buf[:], f.dS)
                zs_.sem, zs_.tgt = f.dS, st
    fence_all(k)


def build_program(upto=8, debug_out=()):
    nc = bass.Bass("TRN2", target_bir_lowering=False)
    dr = declare_drams(nc, upto, debug_out)
    with ExitStack() as ctx:
        k = K(nc, ctx)
        f = setup_frame(k, dr)
        for eng in ENGINES:
            k.waits(eng, f.consts_ready)

        emit_attn_block(k, f, dr, 0, dr["x_t"], dr["xa"])
        if upto >= 4:
            emit_moe_block(k, f, dr, dr["xa"], dr["xb_"])
        if upto >= 5:
            emit_attn_block(k, f, dr, 1, dr["xb_"], dr["xa"])
        if upto >= 8:
            emit_ffn1_block(k, f, dr, dr["xa"], dr["out_t"])

        if dr["_dbg"]:
            fence_all(k)
            for name, mirror in dr["_dbg"].items():
                k.dma("gpsimd", mirror[:], dr[name][:], f.dW)

        if f.dS.n:
            k.wait("sync", f.dS, f.dS.n)
        if f.dA.n:
            k.wait("gpsimd", f.dA, f.dA.n)
        if f.dW.n:
            k.wait("gpsimd", f.dW, f.dW.n)
        if f.dL.n:
            k.wait("sync", f.dL, f.dL.n)
        block = ctx.enter_context(nc.Block())
        k.playback(block)
    return nc


def kernel(**inputs):
    per_core = host_layout(inputs)
    nc = build_program()
    res = run_bass_kernel_spmd(nc, per_core, core_ids=list(range(N_CORES)))
    outs = []
    for c in range(N_CORES):
        ot = np.asarray(res.results[c]["out_t"])
        outs.append(ot.transpose(0, 2, 1))
    return _c(np.concatenate(outs, axis=0)).astype(np.float32)


# version 04 (final)
# speedup vs baseline: 1.4288x; 1.1152x over previous
import numpy as np
import ml_dtypes
from contextlib import ExitStack

import concourse.bass as bass
import concourse.mybir as mybir
from concourse.bass_utils import run_bass_kernel_spmd

BF16 = mybir.dt.bfloat16
F32 = mybir.dt.float32
AF = mybir.ActivationFunctionType
ALU = mybir.AluOpType
AX = mybir.AxisListType

N_CORES = 8
B = 16
BC = B // N_CORES
S = 2048
D = 2048
DC = D // 128
TT_ = BC * S
NE = 8
F_MOE = 1408
FC_MOE = F_MOE // 128
I_FFN = 8192
LN_EPS = 1e-5


ENGINES = ("sync", "tensor", "vector", "scalar", "gpsimd")


class Sem:
    def __init__(self, k, name):
        self.h = k.ctx.enter_context(k.nc.semaphore(name))
        self.n = 0

    def bump(self, amount):
        self.n += amount
        return self.n


class K:

    def __init__(self, nc, ctx):
        self.nc = nc
        self.ctx = ctx
        self.q = {e: [] for e in ENGINES}
        self._sems = {}

    def sem(self, name):
        if name not in self._sems:
            self._sems[name] = Sem(self, name)
        return self._sems[name]

    def emit(self, eng, fn):
        self.q[eng].append(fn)

    def wait(self, eng, sem, val):
        if sem is None or val is None or val <= 0:
            return
        assert val <= sem.n, f"wait_ge({val}) but sem only reaches {sem.n}"
        self.emit(eng, lambda e, h=sem.h, v=val: e.wait_ge(h, v))

    def waits(self, eng, deps):
        for sem, val in deps:
            self.wait(eng, sem, val)

    def op(self, eng, fn, sem=None, amount=1):
        if sem is None:
            self.emit(eng, fn)
            return None
        tgt = sem.bump(amount)
        self.emit(eng, lambda e, fn=fn, h=sem.h, a=amount: fn(e).then_inc(h, a))
        return tgt

    def chain(self, eng, fn, sem):
        t = self.op(eng, fn, sem)
        self.wait(eng, sem, t)
        return t

    def dma(self, eng, out, in_, sem, **kw):
        return self.op(eng, lambda e, o=out, i=in_, kw=kw: e.dma_start(out=o, in_=i, **kw),
                       sem, 16)

    def _uniq(self, name):
        self._n = getattr(self, "_n", 0) + 1
        return f"{name}__{self._n}"

    def sbuf(self, name, shape, dtype):
        return self.ctx.enter_context(self.nc.sbuf_tensor(self._uniq(name), list(shape), dtype))

    def psum(self, name, shape, dtype=F32):
        return self.ctx.enter_context(self.nc.psum_tensor(self._uniq(name), list(shape), dtype))

    def fence_targets(self):
        return [(s, s.n) for s in self._sems.values() if s.n > 0]

    def playback(self, block):
        for eng in ENGINES:
            fns = self.q[eng]
            if fns:
                getattr(block, eng)(lambda e, fns=fns: [fn(e) for fn in fns])


class Slot:

    __slots__ = ("buf", "sem", "tgt", "ready_sem", "ready_tgt", "sem2", "tgt2")

    def __init__(self, buf):
        self.buf = buf
        self.sem = None
        self.tgt = 0
        self.ready_sem = None
        self.ready_tgt = 0
        self.sem2 = None
        self.tgt2 = 0


class Rot:
    def __init__(self, bufs):
        self.slots = [Slot(b) for b in bufs]
        self.i = 0

    def next(self):
        s = self.slots[self.i % len(self.slots)]
        self.i += 1
        return s



def _c(a):
    return np.ascontiguousarray(a)


def host_constants():
    ident = np.eye(128, dtype=np.float32)
    tri = np.where(np.arange(128)[None, :] <= np.arange(128)[:, None], 0.0, -1e9).astype(np.float32)
    oh = np.zeros((8, 8 * 128), dtype=np.float32)
    for e in range(8):
        oh[e, e * 128:(e + 1) * 128] = 1.0
    ones_col = np.ones((128, 1), dtype=np.float32)
    half = 32
    inv_freq = 1.0 / 10000.0 ** (np.arange(0, half, dtype=np.float32) / half)
    t = np.arange(S, dtype=np.float32)
    freqs = np.einsum("i,j->ij", t, inv_freq)
    emb = np.concatenate([freqs, freqs], axis=-1)
    cos_t = _c(np.cos(emb).T.astype(np.float32))
    sin_t = _c(np.sin(emb).T.astype(np.float32))
    return dict(ident=ident, tri=tri, onehots=oh, ones_col=ones_col,
                cos_t=cos_t, sin_t=sin_t)


def host_layout(inputs):
    g = {}
    for nm, src in (("wq0", "L0_attn_Wq"), ("wk0", "L0_attn_Wk"),
                    ("wv0", "L0_attn_Wv"), ("wo0", "L0_attn_Wo"),
                    ("wq1", "L1_attn_Wq"), ("wk1", "L1_attn_Wk"),
                    ("wv1", "L1_attn_Wv"), ("wo1", "L1_attn_Wo")):
        g[nm] = _c(np.asarray(inputs[src]).T)
    g["rw"] = _c(np.asarray(inputs["L0_ffn_router_w"]).T)
    g["gw"] = _c(np.asarray(inputs["L0_ffn_gate_w"]).transpose(0, 2, 1))
    g["uw"] = _c(np.asarray(inputs["L0_ffn_up_w"]).transpose(0, 2, 1))
    g["dw"] = _c(np.asarray(inputs["L0_ffn_down_w"]).transpose(0, 2, 1))
    for nm, src in (("bq1", "L1_attn_bq"), ("bk1", "L1_attn_bk"),
                    ("bv1", "L1_attn_bv"), ("bo1", "L1_attn_bo")):
        g[nm] = _c(np.asarray(inputs[src]).reshape(DC, 128).T)
    g["wg1"] = _c(np.asarray(inputs["L1_ffn_w_gate"]))
    g["wu1"] = _c(np.asarray(inputs["L1_ffn_w_up"]))
    g["wd1"] = _c(np.asarray(inputs["L1_ffn_w_down"]))
    g.update(host_constants())

    x = np.asarray(inputs["L0_attn_x"])
    per_core = []
    for c in range(N_CORES):
        m = dict(g)
        m["x_t"] = _c(x[c * BC:(c + 1) * BC].transpose(0, 2, 1))
        per_core.append(m)
    return per_core



class Frame:
    pass


def setup_frame(k, dr):
    f = Frame()
    f.cT = k.sem("cT")
    f.cV = k.sem("cV")
    f.cS = k.sem("cS")
    f.cG = k.sem("cG")
    f.dL = k.sem("dL")
    f.dW = k.sem("dW")
    f.dS = k.sem("dS")
    f.dA = k.sem("dA")

    f.identb = k.sbuf("identb", [128, 128], BF16)
    f.identf = k.sbuf("identf", [128, 128], F32)
    f.oh = k.sbuf("oh_sb", [8, 1024], F32)
    f.ones_col = k.sbuf("ones_col_sb", [128, 1], BF16)
    f.tri = k.sbuf("tri_sb", [128, 128], F32)
    f.cos_t = k.sbuf("cos_sb", [64, S], F32)
    f.sin_t = k.sbuf("sin_sb", [64, S], F32)

    k.dma("gpsimd", f.identb[:], dr["ident"][:], f.dW)
    k.dma("gpsimd", f.ones_col[:], dr["ones_col"][:], f.dW)
    k.dma("sync", f.identf[:], dr["ident"][:], f.dL)
    k.dma("sync", f.oh[:], dr["onehots"][:], f.dL)
    k.dma("sync", f.tri[:], dr["tri"][:], f.dL)
    k.dma("sync", f.cos_t[:], dr["cos_t"][:], f.dL)
    k.dma("sync", f.sin_t[:], dr["sin_t"][:], f.dL)
    f.consts_ready = [(f.dW, f.dW.n), (f.dL, f.dL.n)]
    return f



def emit_ln_seq(k, f, ln, x_src, xn_buf, seq, extra=None):
    TLN = LN_TILE
    ntile = S // TLN
    last = None
    tile_ready = []
    for tt in range(ntile):
        c0 = tt * TLN
        xf = ln.xf.next()
        xb = ln.xb.next()
        src = x_src[seq].rearrange("(c p) t -> p c t", p=128)[:, :, c0:c0 + TLN]
        k.waits("sync", [(xf.sem, xf.tgt)])
        xf.ready_sem, xf.ready_tgt = f.dL, k.dma("sync", xf.buf[:], src, f.dL)
        k.waits("gpsimd", [(xb.sem, xb.tgt)])
        xb.ready_sem, xb.ready_tgt = f.dW, k.dma("gpsimd", xb.buf[:], src, f.dW)

        k.waits("scalar", [(xf.ready_sem, xf.ready_tgt), (ln.xsq.slots[0].sem, ln.xsq.slots[0].tgt)])
        xsq = ln.xsq.next()
        t_sq = k.op("scalar", lambda s, a=xsq.buf, b=xf.buf: s.activation(
            a[:], b[:].rearrange("p c t -> p (c t)"), AF.Square), f.cS)

        ps_s, ps_q = ln.ps_s, ln.ps_q
        k.waits("tensor", [(xb.ready_sem, xb.ready_tgt), (ps_s.sem, ps_s.tgt), (ps_q.sem, ps_q.tgt),
                           (f.cS, t_sq)])
        def mk_sums(t, xbb=xb.buf, xqq=xsq.buf, pss=ps_s.buf, psq=ps_q.buf):
            for c in range(DC):
                t.matmul(pss[:], ln.ones_col[:], xbb[:, c, :], start=(c == 0), stop=(c == DC - 1))
            for c in range(DC):
                inst = t.matmul(psq[:], ln.ones_col[:], xqq[:, c, :], start=(c == 0), stop=(c == DC - 1))
            return inst
        t_mm = k.op("tensor", mk_sums, f.cT)

        k.wait("vector", f.cT, t_mm)
        k.wait("vector", f.cS, ln.prev_rstd)
        k.chain("vector", lambda v: v.tensor_scalar(ln.mu[:], ps_s.buf[:], 1.0 / D, None, ALU.mult), f.cV)
        k.chain("vector", lambda v: v.tensor_scalar(ln.msq[:], ps_q.buf[:], 1.0 / D, None, ALU.mult), f.cV)
        k.chain("vector", lambda v: v.tensor_tensor(ln.var[:], ln.mu[:], ln.mu[:], ALU.mult), f.cV)
        t_var = k.chain("vector", lambda v: v.scalar_tensor_tensor(
            ln.var[:], ln.msq[:], float(LN_EPS), ln.var[:], ALU.add, ALU.subtract), f.cV)
        ps_s.sem, ps_s.tgt = f.cV, t_var
        ps_q.sem, ps_q.tgt = f.cV, t_var
        k.wait("scalar", f.cV, t_var)
        k.wait("scalar", f.cT, ln.prev_bc)
        k.chain("scalar", lambda s: s.activation(ln.lnv[:], ln.var[:], AF.Ln), f.cS)
        t_rstd = k.chain("scalar", lambda s: s.activation(ln.a_sb[:], ln.lnv[:], AF.Exp, scale=-0.5), f.cS)
        ln.prev_rstd = t_rstd
        k.wait("vector", f.cS, t_rstd)
        t_b = k.chain("vector", lambda v: v.scalar_tensor_tensor(
            ln.b_sb[:], ln.mu[:], -1.0, ln.a_sb[:], ALU.mult, ALU.mult), f.cV)

        psA, psB = ln.psA, ln.psB
        k.waits("tensor", [(f.cV, t_b), (psA.sem, psA.tgt), (psB.sem, psB.tgt)])
        def mk_bc(t):
            t.matmul(psA.buf[:], f.oh[0:1, 0:128], ln.a_sb[:], start=True, stop=True)
            return t.matmul(psB.buf[:], f.oh[0:1, 0:128], ln.b_sb[:], start=True, stop=True)
        t_bc = k.op("tensor", mk_bc, f.cT)
        ln.prev_bc = t_bc

        k.waits("vector", [(f.cT, t_bc)])
        for c in range(DC):
            k.op("vector", lambda v, c=c, xff=xf.buf: v.tensor_tensor(
                ln.t1[:], xff[:, c, :], psA.buf[:], ALU.mult))
            lastv = k.op("vector", lambda v, c=c, c0=c0: v.tensor_tensor(
                xn_buf[:, c, c0:c0 + TLN], ln.t1[:], psB.buf[:], ALU.add), f.cV)
        psA.sem, psA.tgt = f.cV, lastv
        psB.sem, psB.tgt = f.cV, lastv
        xf.sem, xf.tgt = f.cV, lastv
        xb.sem, xb.tgt = f.cT, t_mm
        xsq.sem, xsq.tgt = f.cT, t_mm
        last = lastv
        tile_ready.append(lastv)
        if extra is not None:
            extra(tt, c0, TLN, [(f.cV, lastv)])
    return f.cV, last, tile_ready


LN_TILE = 256


def make_ln(k, f, name="ln"):
    ln = Frame()
    TLN = LN_TILE
    ln.xf = Rot([k.sbuf(f"{name}_xf{i}", [128, DC, TLN], F32) for i in range(2)])
    ln.xb = Rot([k.sbuf(f"{name}_xb{i}", [128, DC, TLN], BF16) for i in range(2)])
    ln.xsq = Rot([k.sbuf(f"{name}_xsq{i}", [128, DC, TLN], BF16) for i in range(2)])
    ln.ones_col = f.ones_col
    ln.ps_s = Slot(k.psum(f"{name}_ps_s", [1, TLN], F32))
    ln.ps_q = Slot(k.psum(f"{name}_ps_q", [1, TLN], F32))
    ln.psA = Slot(k.psum(f"{name}_psA", [128, TLN], F32))
    ln.psB = Slot(k.psum(f"{name}_psB", [128, TLN], F32))
    for sl in (ln.ps_s, ln.ps_q, ln.psA, ln.psB):
        sl.buf = sl.buf
    ln.mu = k.sbuf(f"{name}_mu", [1, TLN], F32)
    ln.msq = k.sbuf(f"{name}_msq", [1, TLN], F32)
    ln.var = k.sbuf(f"{name}_var", [1, TLN], F32)
    ln.lnv = k.sbuf(f"{name}_lnv", [1, TLN], F32)
    ln.a_sb = k.sbuf(f"{name}_a", [1, TLN], F32)
    ln.b_sb = k.sbuf(f"{name}_b", [1, TLN], F32)
    ln.t1 = k.sbuf(f"{name}_t1", [128, TLN], F32)
    ln.prev_rstd = 0
    ln.prev_bc = 0
    return ln



def emit_qkv_seq(k, f, pj, layer, seq, xn_buf, xn_ready, dr, ln_tiles=None):
    OS = 512
    TPROJ = 512
    qscale = (1.0 / np.sqrt(128.0)) if layer == 0 else 0.125
    wlist = (("wq", "q"), ("wk", "k"), ("wv", "v"))
    for wname, role in wlist:
        wsrc = dr[wname + str(layer)]
        dst = dr["qT" if role == "q" else ("kT" if role == "k" else "vT")]
        for os_ in range(D // OS):
            wslot = pj.w.next()
            k.waits("gpsimd", [(wslot.sem, wslot.tgt)])
            wslot.ready_sem, wslot.ready_tgt = f.dW, k.dma(
                "gpsimd", wslot.buf[:],
                wsrc[:, os_ * OS:(os_ + 1) * OS].rearrange("(c p) o -> p c o", p=128), f.dW)
            for oc in range(OS // 128):
                orow = os_ * OS + oc * 128
                for tt in range(S // TPROJ):
                    c0 = tt * TPROJ
                    ps = pj.ps.next()
                    if ln_tiles is not None:
                        nper = TPROJ // LN_TILE
                        xr = (xn_ready[0], ln_tiles[(tt + 1) * nper - 1])
                    else:
                        xr = xn_ready
                    k.waits("tensor", [(wslot.ready_sem, wslot.ready_tgt), (ps.sem, ps.tgt),
                                       xr])
                    def mk(t, w=wslot.buf, oc=oc, c0=c0, p=ps.buf):
                        for c in range(DC):
                            inst = t.matmul(p[:], w[:, c, oc * 128:(oc + 1) * 128],
                                            xn_buf[:, c, c0:c0 + TPROJ],
                                            start=(c == 0), stop=(c == DC - 1))
                        return inst
                    t_mm = k.op("tensor", mk, f.cT)
                    if layer == 0 or role == "v":
                        ev = pj.ev.next()
                        if layer == 0:
                            sc = qscale if role == "q" else 1.0
                            k.waits("scalar", [(f.cT, t_mm), (ev.sem, ev.tgt)])
                            t_ev = k.op("scalar", lambda s, p=ps.buf, e=ev.buf, sc=sc:
                                        s.activation(e[:], p[:], AF.Copy, scale=sc), f.cS)
                        else:
                            bv = pj.bias["v"]
                            k.waits("scalar", [(f.cT, t_mm), (ev.sem, ev.tgt)])
                            t_ev = k.op("scalar", lambda s, p=ps.buf, e=ev.buf, oc2=os_ * (OS // 128) + oc:
                                        s.activation(e[:], p[:], AF.Sigmoid, bias=bv[:, oc2:oc2 + 1]), f.cS)
                        ps.sem, ps.tgt = f.cS, t_ev
                        k.wait("sync", f.cS, t_ev)
                        st = k.dma("sync", dst[seq, orow:orow + 128, c0:c0 + TPROJ], ev.buf[:], f.dS)
                        ev.sem, ev.tgt = f.dS, st
                    else:
                        bmap = pj.bias[role]
                        oc2 = os_ * (OS // 128) + oc
                        sc = qscale if role == "q" else 1.0
                        t_ev = None
                        for hh in range(2):
                            stg = pj.rp_in.next()
                            rot = pj.rp_rot.next()
                            evb = pj.rp_out.next()
                            prow = hh * 64
                            k.waits("vector", [(f.cT, t_mm), (stg.sem, stg.tgt), (rot.sem, rot.tgt),
                                               (evb.sem, evb.tgt)])
                            if role == "q":
                                k.op("vector", lambda v, p=ps.buf, sg=stg.buf, oc2=oc2, prow=prow, sc=sc, bmap=bmap:
                                     v.tensor_scalar(sg[:], p[prow:prow + 64, :],
                                                     bmap[prow:prow + 64, oc2:oc2 + 1], sc, ALU.add, ALU.mult))
                            else:
                                k.op("vector", lambda v, p=ps.buf, sg=stg.buf, oc2=oc2, prow=prow, bmap=bmap:
                                     v.tensor_scalar(sg[:], p[prow:prow + 64, :],
                                                     bmap[prow:prow + 64, oc2:oc2 + 1], None, ALU.add))
                            k.op("vector", lambda v, sg=stg.buf, r=rot.buf:
                                 v.tensor_scalar(r[0:32, :], sg[32:64, :], -1.0, None, ALU.mult))
                            k.op("vector", lambda v, sg=stg.buf, r=rot.buf:
                                 v.tensor_copy(r[32:64, :], sg[0:32, :]))
                            k.op("vector", lambda v, sg=stg.buf, c0=c0:
                                 v.tensor_tensor(sg[:], sg[:], f.cos_t[:, c0:c0 + TPROJ], ALU.mult))
                            k.op("vector", lambda v, r=rot.buf, c0=c0:
                                 v.tensor_tensor(r[:], r[:], f.sin_t[:, c0:c0 + TPROJ], ALU.mult))
                            t_ev = k.op("vector", lambda v, sg=stg.buf, r=rot.buf, e=evb.buf:
                                        v.tensor_tensor(e[:], sg[:], r[:], ALU.add), f.cV)
                            stg.sem, stg.tgt = f.cV, t_ev
                            rot.sem, rot.tgt = f.cV, t_ev
                            k.wait("sync", f.cV, t_ev)
                            st = k.dma("sync", dst[seq, orow + prow:orow + prow + 64, c0:c0 + TPROJ],
                                       evb.buf[:], f.dS)
                            evb.sem, evb.tgt = f.dS, st
                        ps.sem, ps.tgt = f.cV, t_ev
            wslot.sem, wslot.tgt = f.cT, t_mm


def make_proj(k, f, layer):
    pj = Frame()
    OS, TPROJ = 512, 512
    pj.w = Rot([k.sbuf(f"pj_w{i}", [128, DC, OS], BF16) for i in range(2)])
    pj.ps = Rot([Slot(None) for _ in range(0)])
    pj.ps = Rot([k.psum(f"pj_ps{i}", [128, TPROJ], F32) for i in range(3)])
    pj.ev = Rot([k.sbuf(f"pj_ev{i}", [128, TPROJ], BF16) for i in range(3)])
    if layer == 1:
        pj.rp_in = Rot([k.sbuf(f"pj_ri{i}", [64, TPROJ], F32) for i in range(2)])
        pj.rp_rot = Rot([k.sbuf(f"pj_rr{i}", [64, TPROJ], F32) for i in range(2)])
        pj.rp_out = Rot([k.sbuf(f"pj_ro{i}", [64, TPROJ], BF16) for i in range(3)])
        pj.bias = {}
    return pj


def _rotpsum(k, name, n, shape):
    return Rot([k.psum(f"{name}{i}", list(shape), F32) for i in range(n)])



def emit_attn_seq(k, f, at, layer, seq, o_seq, dr):
    dh = 128 if layer == 0 else 64
    nheads = D // dh
    QG = 512
    NG = S // QG

    def s_stage(qs, ks, qg, par):
        P = at.P[par]
        k.waits("vector", [(P.sem, P.tgt)])
        t_e = 0
        for qi in range(4):
            qc = qg * 4 + qi
            kcols = (qc + 1) * 128
            ntile = (kcols + 511) // 512
            for i in range(ntile):
                psl = at.s_ps[i]
                w = min(512, kcols - i * 512)
                k.waits("tensor", [(psl.sem, psl.tgt), (qs.ready_sem, qs.ready_tgt),
                                   (ks.ready_sem, ks.ready_tgt)])
                t_s = k.op("tensor", lambda t, i=i, w=w, qc=qc, p=psl.buf, q=qs.buf, kk=ks.buf:
                           t.matmul(p[:, 0:w], q[0:dh, qc * 128:(qc + 1) * 128],
                                    kk[0:dh, i * 512:i * 512 + w], start=True, stop=True), f.cT)
                psl.sem, psl.tgt = f.cT, t_s
            dtile = at.s_ps[ntile - 1]
            dcol = (kcols - 128) - (ntile - 1) * 512
            k.waits("vector", [(f.cT, dtile.tgt), (at.negm_g[qi][0], at.negm_g[qi][1])])
            k.op("vector", lambda v, p=dtile.buf, dcol=dcol:
                 v.tensor_tensor(p[:, dcol:dcol + 128], p[:, dcol:dcol + 128],
                                 f.tri[:], ALU.add))
            for i in range(ntile):
                w = min(512, kcols - i * 512)
                k.op("vector", lambda v, i=i, w=w, p=at.s_ps[i].buf:
                     v.reduce_max(at.mx[:, i:i + 1], p[:, 0:w], axis=AX.X))
            t_nm = k.chain("vector", lambda v, nt=ntile, qi=qi:
                           v.reduce_max(at.negm[:, qi:qi + 1], at.mx[:, 0:nt], axis=AX.X,
                                        negate=True), f.cV)
            k.wait("scalar", f.cV, t_nm)
            for i in range(ntile):
                w = min(512, kcols - i * 512)
                t_e = k.op("scalar", lambda s, i=i, w=w, p=at.s_ps[i].buf, qi=qi, P=P:
                           s.activation(P.buf[:, qi, i * 512:i * 512 + w], p[:, 0:w],
                                        AF.Exp, bias=at.negm[:, qi:qi + 1]), f.cS)
                at.s_ps[i].sem, at.s_ps[i].tgt = f.cS, t_e
            at.negm_g[qi] = (f.cS, t_e)
        return [(f.cS, t_e)]

    def emit_loads(h):
        qs, ks, vs = at.qh.next(), at.kh.next(), at.vh.next()
        rows = (h * dh, (h + 1) * dh)
        for sl, src in ((qs, dr["qT"]), (ks, dr["kT"]), (vs, dr["vT"])):
            k.waits("sync", [(sl.sem, sl.tgt)])
            sl.ready_sem, sl.ready_tgt = f.dL, k.dma(
                "sync", sl.buf[0:dh, :], src[seq, rows[0]:rows[1], :], f.dL)
        return qs, ks, vs

    def emit_vtrans(vs):
        vtok = at.vtok.next()
        t_ev = 0
        for kc in range(S // 128):
            ptr = at.ptr.next()
            k.waits("tensor", [(ptr.sem, ptr.tgt), (vs.ready_sem, vs.ready_tgt)])
            t_tr = k.op("tensor", lambda t, kc=kc, p=ptr.buf, v=vs.buf:
                        t.transpose(p[:, 0:dh], v[0:dh, kc * 128:(kc + 1) * 128],
                                    f.identb[0:dh, 0:dh]), f.cT)
            deps = [(f.cT, t_tr)] + ([(vtok.sem, vtok.tgt)] if kc == 0 else [])
            k.waits("scalar", deps)
            t_ev = k.op("scalar", lambda s, kc=kc, p=ptr.buf, vt=vtok.buf:
                        s.activation(vt[:, kc * dh:(kc + 1) * dh], p[:, 0:dh], AF.Copy), f.cS)
            ptr.sem, ptr.tgt = f.cS, t_ev
        vs.sem, vs.tgt = f.cT, f.cT.n
        return vtok, (f.cS, t_ev)

    def pv_stage(h, qg, par, p_ready, vtok, vtok_ready):
        P = at.P[par]
        nk = (qg + 1) * (QG // 128)
        po, prb = at.po, at.prb
        t_pv = 0
        pend = None

        def emit_pv(kc, pts, qlo, last):
            nonlocal t_pv
            k.waits("tensor", [(pts.sem2, pts.tgt2)] +
                    ([(po.sem, po.tgt), (prb.sem, prb.tgt), vtok_ready] if kc == 0 else []))
            def mkPV(t, kc=kc, pt=pts.buf, qlo=qlo, last=last, vt=vtok.buf):
                q0 = qlo * 128
                t.matmul(po.buf[0:dh, q0:512], vt[:, kc * dh:(kc + 1) * dh], pt[:, q0:512],
                         start=(kc == 0), stop=last)
                return t.matmul(prb.buf[0:1, q0:512], f.ones_col[:], pt[:, q0:512],
                                start=(kc == 0), stop=last)
            t_pv = k.op("tensor", mkPV, f.cT)
            pts.sem, pts.tgt = f.cT, t_pv

        for kc in range(nk):
            qlo = max(0, kc - qg * 4)
            ptr = at.ptr.next()
            k.waits("tensor", [(ptr.sem, ptr.tgt)] + (p_ready if kc == 0 else []))
            def mkT(t, kc=kc, p=ptr.buf, P=P, qlo=qlo):
                for qi in range(qlo, 4):
                    inst = t.transpose(p[:, qi * 128:(qi + 1) * 128],
                                       P.buf[:, qi, kc * 128:(kc + 1) * 128], f.identb[:])
                return inst
            t_tr = k.op("tensor", mkT, f.cT)
            pts = at.pts.next()
            k.waits("scalar", [(f.cT, t_tr), (pts.sem, pts.tgt)])
            t_pe = k.op("scalar", lambda s, p=ptr.buf, o=pts.buf, qlo=qlo:
                        s.activation(o[:, qlo * 128:512], p[:, qlo * 128:512], AF.Copy), f.cS)
            ptr.sem, ptr.tgt = f.cS, t_pe
            pts.sem2, pts.tgt2 = f.cS, t_pe
            if pend is not None:
                emit_pv(*pend, last=False)
            pend = (kc, pts, qlo)
        emit_pv(*pend, last=True)
        P.sem, P.tgt = f.cT, t_pv
        k.wait("vector", f.cT, t_pv)
        t_rl = k.op("vector", lambda v: v.reciprocal(at.rl[:], prb.buf[0:1, :]), f.cV)
        k.waits("tensor", [(f.cV, t_rl)])
        t_bc = k.op("tensor", lambda t: t.matmul(prb.buf[:], f.oh[0:1, 0:128], at.rl[:],
                                                 start=True, stop=True), f.cT)
        k.waits("scalar", [(f.cT, t_bc)])
        t_rbc = k.op("scalar", lambda s: s.activation(at.rlb[:], prb.buf[:], AF.Copy), f.cS)
        prb.sem, prb.tgt = f.cS, t_rbc
        ch, r0 = (h, 0) if layer == 0 else (h // 2, (h % 2) * 64)
        k.waits("vector", [(f.cS, t_rbc), (f.cT, t_pv)])
        t_out = k.op("vector", lambda v, ch=ch, r0=r0, qg=qg:
                     v.tensor_tensor(o_seq.buf[r0:r0 + dh, ch, qg * QG:(qg + 1) * QG],
                                     po.buf[0:dh, :], at.rlb[0:dh, :], ALU.mult), f.cV)
        po.sem, po.tgt = f.cV, t_out

    at.negm_g = [(None, 0)] * 4
    par = 0
    qs, ks, vs = emit_loads(0)
    p_ready = s_stage(qs, ks, 0, par)
    vtok, vtok_ready = emit_vtrans(vs)
    nxt_slots = None
    for h in range(nheads):
        for qg in range(NG):
            if qg == 0 and h + 1 < nheads:
                nxt_slots = emit_loads(h + 1)
            nxt = None
            if qg + 1 < NG:
                nxt = (s_stage(qs, ks, qg + 1, 1 - par), 1 - par, qs, ks, vtok, vtok_ready)
            elif h + 1 < nheads:
                q2, k2, v2 = nxt_slots
                pr2 = s_stage(q2, k2, 0, 1 - par)
                vt2, vr2 = emit_vtrans(v2)
                nxt = (pr2, 1 - par, q2, k2, vt2, vr2)
            pv_stage(h, qg, par, p_ready, vtok, vtok_ready)
            if nxt is None:
                break
            p_ready, par2, qs2, ks2, vtok2, vtr2 = nxt
            if qg + 1 == NG:
                qs.sem, qs.tgt = f.cT, f.cT.n
                ks.sem, ks.tgt = f.cT, f.cT.n
                vtok.sem, vtok.tgt = f.cT, f.cT.n
                qs, ks, vtok, vtok_ready = qs2, ks2, vtok2, vtr2
            par = par2
    qs.sem, qs.tgt = f.cT, f.cT.n
    ks.sem, ks.tgt = f.cT, f.cT.n
    vtok.sem, vtok.tgt = f.cT, f.cT.n
    o_seq.ready_sem, o_seq.ready_tgt = f.cV, f.cV.n
    return f.cV, f.cV.n


def make_attn(k, f, layer):
    at = Frame()
    dh = 128 if layer == 0 else 64
    at.qh = Rot([k.sbuf(f"at_q{i}", [128, S], BF16) for i in range(2)])
    at.kh = Rot([k.sbuf(f"at_k{i}", [128, S], BF16) for i in range(2)])
    at.vh = Rot([k.sbuf(f"at_v{i}", [128, S], BF16) for i in range(2)])
    at.vtok = Rot([k.sbuf(f"at_vtok{i}", [128, (S // 128) * dh], BF16) for i in range(2)])
    at.s_ps = [Slot(k.psum(f"at_s{i}", [128, 512], F32)) for i in range(4)]
    at.ptr = Rot([k.psum(f"at_ptr{i}", [128, 1024], BF16) for i in range(2)])
    at.po = Slot(k.psum("at_po", [128, 512], F32))
    at.prb = Slot(k.psum("at_prb", [128, 512], F32))
    at.P = [Slot(k.sbuf(f"at_P{i}", [128, 4, S], BF16)) for i in range(2)]
    at.mx = k.sbuf("at_mx", [128, 4], F32)
    at.negm = k.sbuf("at_negm", [128, 4], F32)
    at.pts = Rot([k.sbuf(f"at_pts{i}", [128, 512], BF16) for i in range(2)])
    at.rl = k.sbuf("at_rl", [1, 512], F32)
    at.rlb = k.sbuf("at_rlb", [128, 512], F32)
    return at



def emit_oproj_seq(k, f, pr, layer, seq, o_seq, o_ready, x_src, x_dst, dr, bias=None):
    OS, TPROJ = 512, 512
    wsrc = dr["wo" + str(layer)]
    for os_ in range(D // OS):
        wslot = pr.w.next()
        k.waits("gpsimd", [(wslot.sem, wslot.tgt)])
        wslot.ready_sem, wslot.ready_tgt = f.dW, k.dma(
            "gpsimd", wslot.buf[:],
            wsrc[:, os_ * OS:(os_ + 1) * OS].rearrange("(c p) o -> p c o", p=128), f.dW)
        t_mm = None
        for oc in range(OS // 128):
            orow = os_ * OS + oc * 128
            oc2 = os_ * (OS // 128) + oc
            for tt in range(S // TPROJ):
                c0 = tt * TPROJ
                ps = pr.ps.next()
                xin = pr.xin.next()
                k.waits("sync", [(xin.sem, xin.tgt)])
                xin.ready_sem, xin.ready_tgt = f.dL, k.dma(
                    "sync", xin.buf[:], x_src[seq, orow:orow + 128, c0:c0 + TPROJ], f.dL)
                k.waits("tensor", [(wslot.ready_sem, wslot.ready_tgt), (ps.sem, ps.tgt), o_ready])
                def mk(t, w=wslot.buf, oc=oc, c0=c0, p=ps.buf):
                    for c in range(DC):
                        inst = t.matmul(p[:], w[:, c, oc * 128:(oc + 1) * 128],
                                        o_seq[:, c, c0:c0 + TPROJ],
                                        start=(c == 0), stop=(c == DC - 1))
                    return inst
                t_mm = k.op("tensor", mk, f.cT)
                ev = pr.ev.next()
                k.waits("vector", [(f.cT, t_mm), (ev.sem, ev.tgt),
                                   (xin.ready_sem, xin.ready_tgt)])
                if bias is None:
                    t_ev = k.op("vector", lambda v, p=ps.buf, e=ev.buf, xi=xin.buf:
                                v.tensor_tensor(e[:], p[:], xi[:], ALU.add), f.cV)
                else:
                    t_ev = k.op("vector", lambda v, p=ps.buf, e=ev.buf, xi=xin.buf, oc2=oc2:
                                v.scalar_tensor_tensor(e[:], p[:], bias[:, oc2:oc2 + 1], xi[:],
                                                       ALU.add, ALU.add), f.cV)
                ps.sem, ps.tgt = f.cV, t_ev
                xin.sem, xin.tgt = f.cV, t_ev
                k.wait("sync", f.cV, t_ev)
                st = k.dma("sync", x_dst[seq, orow:orow + 128, c0:c0 + TPROJ], ev.buf[:], f.dS)
                ev.sem, ev.tgt = f.dS, st
        wslot.sem, wslot.tgt = f.cT, t_mm


def make_oproj(k, f):
    pr = Frame()
    OS, TPROJ = 512, 512
    pr.w = Rot([k.sbuf(f"pr_w{i}", [128, DC, OS], BF16) for i in range(2)])
    pr.ps = Rot([k.psum(f"pr_ps{i}", [128, TPROJ], F32) for i in range(2)])
    pr.xin = Rot([k.sbuf(f"pr_x{i}", [128, TPROJ], F32) for i in range(3)])
    pr.ev = Rot([k.sbuf(f"pr_e{i}", [128, TPROJ], F32) for i in range(3)])
    return pr



def declare_drams(nc, upto, debug_out=()):
    dr = {}
    dr["_dbg"] = {}
    def ext_in(name, shape, dtype=F32):
        dr[name] = nc.dram_tensor(name, list(shape), dtype, kind="ExternalInput")
    def scratch(name, shape, dtype):
        if name in debug_out and dtype == F32:
            dr[name] = nc.dram_tensor(name, list(shape), dtype, kind="ExternalOutput")
            return
        dr[name] = nc.dram_tensor(name, list(shape), dtype, kind="Internal")
        if name in debug_out:
            dr["_dbg"][name] = nc.dram_tensor("dbg_" + name, list(shape), F32,
                                              kind="ExternalOutput")

    ext_in("x_t", [BC, D, S])
    for nm in ("wq0", "wk0", "wv0", "wo0", "wq1", "wk1", "wv1", "wo1"):
        ext_in(nm, [D, D])
    ext_in("rw", [D, 8])
    ext_in("gw", [NE, D, F_MOE]); ext_in("uw", [NE, D, F_MOE]); ext_in("dw", [NE, F_MOE, D])
    for nm in ("bq1", "bk1", "bv1", "bo1"):
        ext_in(nm, [128, DC])
    ext_in("wg1", [D, I_FFN]); ext_in("wu1", [D, I_FFN]); ext_in("wd1", [I_FFN, D])
    ext_in("ident", [128, 128]); ext_in("tri", [128, 128]); ext_in("onehots", [8, 1024])
    ext_in("ones_col", [128, 1]); ext_in("cos_t", [64, S]); ext_in("sin_t", [64, S])

    scratch("qT", [BC, D, S], BF16)
    scratch("kT", [BC, D, S], BF16)
    scratch("vT", [BC, D, S], BF16)
    scratch("xa", [BC, D, S], F32)
    scratch("xb_", [BC, D, S], F32)
    scratch("xnT", [D, TT_], BF16)
    scratch("yacc", [D, TT_], F32)
    dr["out_t"] = nc.dram_tensor("out_t", [BC, D, S], F32, kind="ExternalOutput")
    return dr



from contextlib import contextmanager


@contextmanager
def scope(k):
    old = k.ctx
    with ExitStack() as es:
        k.ctx = es
        try:
            yield
        finally:
            k.ctx = old


def fence_all(k):
    tgts = k.fence_targets()
    for eng in ENGINES:
        k.waits(eng, tgts)



def emit_attn_block(k, f, dr, layer, x_src, x_dst):
    with scope(k):
        ln = make_ln(k, f)
        pj = make_proj(k, f, layer)
        xn = Slot(k.sbuf("xn_seq", [128, DC, S], BF16))
        if layer == 1:
            pj.bias = {}
            for role, nm in (("q", "bq1"), ("k", "bk1"), ("v", "bv1")):
                bb = k.sbuf(f"bias_{nm}", [128, DC], F32)
                k.dma("sync", bb[:], dr[nm][:], f.dL)
                pj.bias[role] = bb
            k.waits("scalar", [(f.dL, f.dL.n)])
            k.waits("vector", [(f.dL, f.dL.n)])
        for s_ in range(BC):
            k.waits("vector", [(xn.sem, xn.tgt)])
            sem, tgt, tiles = emit_ln_seq(k, f, ln, x_src, xn.buf, s_)
            emit_qkv_seq(k, f, pj, layer, s_, xn.buf, (sem, tgt), dr, ln_tiles=tiles)
            xn.sem, xn.tgt = f.cT, f.cT.n
    fence_all(k)
    with scope(k):
        o_seq = Slot(k.sbuf("o_seq", [128, DC, S], BF16))
        bo = None
        if layer == 1:
            bo = k.sbuf("bias_bo1", [128, DC], F32)
            k.dma("sync", bo[:], dr["bo1"][:], f.dL)
            k.waits("vector", [(f.dL, f.dL.n)])
        for s_ in range(BC):
            with scope(k):
                at = make_attn(k, f, layer)
                emit_attn_seq(k, f, at, layer, s_, o_seq, dr)
            fence_all(k)
            with scope(k):
                pr = make_oproj(k, f)
                emit_oproj_seq(k, f, pr, layer, s_, o_seq.buf,
                               (o_seq.ready_sem, o_seq.ready_tgt), x_src, x_dst, dr, bias=bo)
            fence_all(k)
            o_seq.sem, o_seq.tgt = f.cT, f.cT.n
    fence_all(k)



DEV_MOE_TAPS = False


def emit_moe_block(k, f, dr, x_src, x_dst):
    TLN = 512
    if DEV_MOE_TAPS:
        dr["dbg_rt"] = k.nc.dram_tensor("dbg_rt", [128, 48], F32, kind="ExternalOutput")
        dr["dbg_combT"] = k.nc.dram_tensor("dbg_combT", [8, TT_], F32, kind="ExternalOutput")
        dr["dbg_Bsb"] = k.nc.dram_tensor("dbg_Bsb", [128, 4, 512], F32, kind="ExternalOutput")
        dr["dbg_h"] = k.nc.dram_tensor("dbg_h", [F_MOE, S], F32, kind="ExternalOutput")
    with scope(k):
        combT = k.sbuf("combT", [8, TT_], BF16)
        ohb = k.sbuf("ohb", [8, 1024], BF16)
        k.dma("gpsimd", ohb[:], dr["onehots"][:], f.dW)
        emit_moe_inner(k, f, dr, x_src, x_dst, combT, ohb)
    fence_all(k)


def emit_moe_inner(k, f, dr, x_src, x_dst, combT, ohb):
    TLN = 512
    with scope(k):
        ln = make_ln(k, f, "mln")
        xn = Slot(k.sbuf("mxn", [128, DC, S], BF16))
        rwb = k.sbuf("rwb", [128, DC, 8], BF16)
        k.dma("gpsimd", rwb[:], dr["rw"][:].rearrange("(c p) e -> p c e", p=128), f.dW)
        rt = Frame()
        rt.pl = Slot(k.psum("rt_pl", [128, 512], F32))
        rt.pt = Slot(k.psum("rt_pt", [8, 512], F32))
        rt.L = k.sbuf("rt_L", [128, 8], F32)
        rt.m1 = k.sbuf("rt_m1", [128, 1], F32)
        rt.nm1 = k.sbuf("rt_nm1", [128, 1], F32)
        rt.m2 = k.sbuf("rt_m2", [128, 1], F32)
        rt.eq = k.sbuf("rt_eq", [128, 8], F32)
        rt.E = k.sbuf("rt_E", [128, 8], F32)
        rt.we = k.sbuf("rt_we", [128, 8], F32)
        rt.ssum = k.sbuf("rt_ss", [128, 1], F32)
        rt.rs = k.sbuf("rt_rs", [128, 1], F32)
        rt.comb = k.sbuf("rt_comb", [128, 8], F32)
        k.waits("tensor", [(f.dW, f.dW.n)])

        def router(seq):
            def extra(tt, c0, tln, dep):
                k.waits("sync", dep)
                k.dma("sync", dr["xnT"][:].rearrange("(c p) t -> p c t", p=128)
                      [:, :, seq * S + c0: seq * S + c0 + tln],
                      xn.buf[:, :, c0:c0 + tln], f.dS)
                for j in range(tln // 128):
                    t0 = c0 + j * 128
                    k.waits("tensor", dep + [(rt.pl.sem, rt.pl.tgt)])
                    def mkl(t, t0=t0):
                        for c in range(DC):
                            inst = t.matmul(rt.pl.buf[:, 0:8], xn.buf[:, c, t0:t0 + 128],
                                            rwb[:, c, :], start=(c == 0), stop=(c == DC - 1))
                        return inst
                    t_l = k.op("tensor", mkl, f.cT)
                    k.waits("vector", [(f.cT, t_l)])
                    k.chain("vector", lambda v: v.tensor_copy(rt.L[:], rt.pl.buf[:, 0:8]), f.cV)
                    t_lc = k.chain("vector", lambda v: v.reduce_max(rt.m1[:], rt.L[:], axis=AX.X), f.cV)
                    rt.pl.sem, rt.pl.tgt = f.cV, t_lc
                    k.chain("vector", lambda v: v.tensor_scalar(rt.nm1[:], rt.m1[:], -1.0, None, ALU.mult), f.cV)
                    k.chain("vector", lambda v: v.tensor_scalar(rt.eq[:], rt.L[:], rt.m1[:, 0:1], None, ALU.is_equal), f.cV)
                    k.chain("vector", lambda v: v.scalar_tensor_tensor(rt.eq[:], rt.eq[:], -1e30, rt.L[:], ALU.mult, ALU.add), f.cV)
                    t_m2 = k.chain("vector", lambda v: v.reduce_max(rt.m2[:], rt.eq[:], axis=AX.X), f.cV)
                    k.waits("scalar", [(f.cV, t_m2)])
                    t_E = k.op("scalar", lambda s: s.activation(rt.E[:], rt.L[:], AF.Exp, bias=rt.nm1[:, 0:1]), f.cS)
                    k.waits("vector", [(f.cS, t_E)])
                    k.chain("vector", lambda v: v.tensor_scalar(rt.we[:], rt.L[:], rt.m2[:, 0:1], None, ALU.is_ge), f.cV)
                    k.chain("vector", lambda v: v.tensor_tensor(rt.we[:], rt.we[:], rt.E[:], ALU.mult), f.cV)
                    k.chain("vector", lambda v: v.reduce_sum(rt.ssum[:], rt.we[:], axis=AX.X), f.cV)
                    k.chain("vector", lambda v: v.reciprocal(rt.rs[:], rt.ssum[:]), f.cV)
                    t_cb = k.chain("vector", lambda v: v.tensor_scalar(rt.comb[:], rt.we[:], rt.rs[:, 0:1], None, ALU.mult), f.cV)
                    if DEV_MOE_TAPS and seq == 0 and t0 == 0:
                        k.waits("gpsimd", [(f.cV, t_cb), (f.cS, f.cS.n)])
                        k.dma("gpsimd", dr["dbg_rt"][:, 0:8], rt.L[:], f.dW, allow_slow_non_contiguous=True)
                        k.dma("gpsimd", dr["dbg_rt"][:, 8:16], rt.E[:], f.dW, allow_slow_non_contiguous=True)
                        k.dma("gpsimd", dr["dbg_rt"][:, 16:24], rt.we[:], f.dW, allow_slow_non_contiguous=True)
                        k.dma("gpsimd", dr["dbg_rt"][:, 24:32], rt.comb[:], f.dW, allow_slow_non_contiguous=True)
                        k.dma("gpsimd", dr["dbg_rt"][:, 32:33], rt.m1[:], f.dW, allow_slow_non_contiguous=True)
                        k.dma("gpsimd", dr["dbg_rt"][:, 33:34], rt.m2[:], f.dW, allow_slow_non_contiguous=True)
                        k.dma("gpsimd", dr["dbg_rt"][:, 34:35], rt.rs[:], f.dW, allow_slow_non_contiguous=True)
                        k.dma("gpsimd", dr["dbg_rt"][:, 35:36], rt.nm1[:], f.dW, allow_slow_non_contiguous=True)
                    k.waits("tensor", [(f.cV, t_cb), (rt.pt.sem, rt.pt.tgt)])
                    t_tr = k.op("tensor", lambda t: t.transpose(rt.pt.buf[:, 0:128], rt.comb[:], f.identf[:]), f.cT)
                    k.waits("scalar", [(f.cT, t_tr)])
                    t_ce = k.op("scalar", lambda s, seq=seq, t0=t0: s.activation(
                        combT[:, seq * S + t0: seq * S + t0 + 128], rt.pt.buf[:, 0:128], AF.Copy), f.cS)
                    rt.pt.sem, rt.pt.tgt = f.cS, t_ce
            return extra

        for s_ in range(BC):
            k.waits("vector", [(xn.sem, xn.tgt)])
            emit_ln_seq(k, f, ln, x_src, xn.buf, s_, extra=router(s_))
            xn.sem, xn.tgt = f.cT, f.cT.n
        del ln
        comb_done = (f.cS, f.cS.n)
        xnT_done = (f.dS, f.dS.n)
        if DEV_MOE_TAPS:
            k.waits("gpsimd", [comb_done])
            k.dma("gpsimd", dr["dbg_combT"][:], combT[:], f.dW)
    fence_all(k)

    with scope(k):
        xnh = Slot(k.sbuf("xnh", [128, DC, S], BF16))
        Bsb = Slot(k.sbuf("Bsb", [128, 4, 512], F32))
        psB = Slot(k.psum("psB", [128, 512], F32))
        psgr = Rot([k.psum(f"psg{i}", [128, 512], F32) for i in range(2)])
        psur = Rot([k.psum(f"psu{i}", [128, 512], F32) for i in range(2)])
        psd = Rot([k.psum(f"psd{i}", [128, 512], F32) for i in range(2)])
        gwb = Rot([k.sbuf(f"gwb{i}", [128, DC, 128], BF16) for i in range(2)])
        uwb = Rot([k.sbuf(f"uwb{i}", [128, DC, 128], BF16) for i in range(2)])
        dwb = Rot([k.sbuf(f"dwb{i}", [128, FC_MOE, 128], BF16) for i in range(2)])
        gs = Rot([k.sbuf(f"gs{i}", [128, 512], BF16) for i in range(2)])
        ut = Rot([k.sbuf(f"ut{i}", [128, 512], BF16) for i in range(2)])
        hbuf = k.sbuf("hbuf", [128, FC_MOE, S], BF16)
        dsb = Rot([k.sbuf(f"dsb{i}", [128, S], F32) for i in range(2)])

        prev_acc = 0
        for half in range(2):
            th0 = half * S
            k.waits("sync", [(xnh.sem, xnh.tgt), xnT_done])
            xnh.ready_sem, xnh.ready_tgt = f.dL, k.dma(
                "sync", xnh.buf[:],
                dr["xnT"][:].rearrange("(c p) t -> p c t", p=128)[:, :, th0:th0 + S], f.dL)
            for e in range(NE):
                t_bev = 0
                for ts_ in range(4):
                    k.waits("tensor", [(psB.sem, psB.tgt), comb_done])
                    t_b = k.op("tensor", lambda t, e=e, ts_=ts_, th0=th0: t.matmul(
                        psB.buf[:], ohb[:, e * 128:(e + 1) * 128],
                        combT[:, th0 + ts_ * 512: th0 + (ts_ + 1) * 512],
                        start=True, stop=True), f.cT)
                    k.waits("scalar", [(f.cT, t_b)] + ([(Bsb.sem, Bsb.tgt)] if ts_ == 0 else []))
                    t_bev = k.op("scalar", lambda s, ts_=ts_: s.activation(
                        Bsb.buf[:, ts_, :], psB.buf[:], AF.Copy), f.cS)
                    psB.sem, psB.tgt = f.cS, t_bev
                Bsb.ready_sem, Bsb.ready_tgt = f.cS, t_bev

                t_h = 0
                hb_guard = getattr(emit_moe_inner, "_hb_guard", None)
                for fc in range(FC_MOE):
                    gsl, usl = gwb.next(), uwb.next()
                    k.waits("gpsimd", [(gsl.sem, gsl.tgt)])
                    gsl.ready_sem, gsl.ready_tgt = f.dW, k.dma(
                        "gpsimd", gsl.buf[:],
                        dr["gw"][e, :, fc * 128:(fc + 1) * 128].rearrange("(c p) o -> p c o", p=128), f.dW)
                    k.waits("gpsimd", [(usl.sem, usl.tgt)])
                    usl.ready_sem, usl.ready_tgt = f.dW, k.dma(
                        "gpsimd", usl.buf[:],
                        dr["uw"][e, :, fc * 128:(fc + 1) * 128].rearrange("(c p) o -> p c o", p=128), f.dW)
                    for ts_ in range(4):
                        psg, psu = psgr.next(), psur.next()
                        k.waits("tensor", [(gsl.ready_sem, gsl.ready_tgt), (usl.ready_sem, usl.ready_tgt),
                                           (psg.sem, psg.tgt), (psu.sem, psu.tgt),
                                           (xnh.ready_sem, xnh.ready_tgt)])
                        def mkgu(t, g=gsl.buf, u=usl.buf, ts_=ts_, psg=psg, psu=psu):
                            for c in range(DC):
                                t.matmul(psg.buf[:], g[:, c, :], xnh.buf[:, c, ts_ * 512:(ts_ + 1) * 512],
                                         start=(c == 0), stop=(c == DC - 1))
                            for c in range(DC):
                                inst = t.matmul(psu.buf[:], u[:, c, :], xnh.buf[:, c, ts_ * 512:(ts_ + 1) * 512],
                                                start=(c == 0), stop=(c == DC - 1))
                            return inst
                        t_mm = k.op("tensor", mkgu, f.cT)
                        g2, u2 = gs.next(), ut.next()
                        k.waits("scalar", [(f.cT, t_mm), (g2.sem, g2.tgt)])
                        t_si = k.op("scalar", lambda s, g2=g2, psg=psg: s.activation(g2.buf[:], psg.buf[:], AF.Silu), f.cS)
                        psg.sem, psg.tgt = f.cS, t_si
                        k.waits("vector", [(f.cT, t_mm), (u2.sem, u2.tgt),
                                           (Bsb.ready_sem, Bsb.ready_tgt)])
                        t_ut = k.op("vector", lambda v, u2=u2, ts_=ts_, psu=psu: v.tensor_tensor(
                            u2.buf[:], psu.buf[:], Bsb.buf[:, ts_, :], ALU.mult), f.cV)
                        psu.sem, psu.tgt = f.cV, t_ut
                        k.waits("vector", [(f.cS, t_si)] + ([hb_guard] if hb_guard else []))
                        hb_guard = None
                        t_h = k.op("vector", lambda v, g2=g2, u2=u2, fc=fc, ts_=ts_: v.tensor_tensor(
                            hbuf[:, fc, ts_ * 512:(ts_ + 1) * 512], g2.buf[:], u2.buf[:], ALU.mult), f.cV)
                        g2.sem, g2.tgt = f.cV, t_h
                        u2.sem, u2.tgt = f.cV, t_h
                    gsl.sem, gsl.tgt = f.cT, t_mm
                    usl.sem, usl.tgt = f.cT, t_mm
                h_ready = (f.cV, t_h)
                Bsb.sem, Bsb.tgt = f.cV, t_h
                if DEV_MOE_TAPS and half == 0 and e == 0:
                    k.waits("gpsimd", [(Bsb.ready_sem, Bsb.ready_tgt), h_ready])
                    tb = k.dma("gpsimd", dr["dbg_Bsb"][:], Bsb.buf[:], f.dW)
                    th2 = k.dma("gpsimd", dr["dbg_h"][:].rearrange("(c p) t -> p c t", p=128),
                                hbuf[:], f.dW)
                    Bsb.sem, Bsb.tgt = f.dW, tb
                    emit_moe_inner._hb_guard = (f.dW, th2)

                k.waits("gpsimd", [(f.dA, prev_acc)])
                for oc in range(DC):
                    dsl = dwb.next()
                    k.waits("gpsimd", [(dsl.sem, dsl.tgt)])
                    dsl.ready_sem, dsl.ready_tgt = f.dW, k.dma(
                        "gpsimd", dsl.buf[:],
                        dr["dw"][e, :, oc * 128:(oc + 1) * 128].rearrange("(c p) o -> p c o", p=128), f.dW)
                    ds = dsb.next()
                    t_de = 0
                    for ts_ in range(4):
                        pd = psd.next()
                        k.waits("tensor", [(dsl.ready_sem, dsl.ready_tgt), (pd.sem, pd.tgt),
                                           h_ready])
                        def mkd(t, d=dsl.buf, ts_=ts_, pd=pd):
                            for fc in range(FC_MOE):
                                inst = t.matmul(pd.buf[:], d[:, fc, :],
                                                hbuf[:, fc, ts_ * 512:(ts_ + 1) * 512],
                                                start=(fc == 0), stop=(fc == FC_MOE - 1))
                            return inst
                        t_dm = k.op("tensor", mkd, f.cT)
                        k.waits("scalar", [(f.cT, t_dm)] + ([(ds.sem, ds.tgt)] if ts_ == 0 else []))
                        t_de = k.op("scalar", lambda s, pd=pd, ds=ds, ts_=ts_: s.activation(
                            ds.buf[:, ts_ * 512:(ts_ + 1) * 512], pd.buf[:], AF.Copy), f.cS)
                        pd.sem, pd.tgt = f.cS, t_de
                    k.waits("gpsimd", [(f.cS, t_de)])
                    acc = {} if e == 0 else dict(accum_op=ALU.add)
                    st = k.dma("gpsimd", dr["yacc"][oc * 128:(oc + 1) * 128, th0:th0 + S],
                               ds.buf[:], f.dA, **acc)
                    ds.sem, ds.tgt = f.dA, st
                    dsl.sem, dsl.tgt = f.cT, f.cT.n
                prev_acc = f.dA.n
            xnh.sem, xnh.tgt = f.cT, f.cT.n
        acc_done = (f.dA, f.dA.n)
    fence_all(k)

    with scope(k):
        xi = Rot([k.sbuf(f"rxi{i}", [128, 512], F32) for i in range(3)])
        yi = Rot([k.sbuf(f"ryi{i}", [128, 512], F32) for i in range(3)])
        zo = Rot([k.sbuf(f"rzo{i}", [128, 512], F32) for i in range(3)])
        for oc in range(DC):
            for tt in range(TT_ // 512):
                s_ = (tt * 512) // S
                c0 = (tt * 512) % S
                xs_, ys_, zs_ = xi.next(), yi.next(), zo.next()
                k.waits("sync", [(xs_.sem, xs_.tgt)])
                xs_.ready_sem, xs_.ready_tgt = f.dL, k.dma(
                    "sync", xs_.buf[:], x_src[s_, oc * 128:(oc + 1) * 128, c0:c0 + 512], f.dL)
                k.waits("sync", [(ys_.sem, ys_.tgt)])
                ys_.ready_sem, ys_.ready_tgt = f.dL, k.dma(
                    "sync", ys_.buf[:], dr["yacc"][oc * 128:(oc + 1) * 128, tt * 512:tt * 512 + 512], f.dL)
                k.waits("vector", [(xs_.ready_sem, xs_.ready_tgt), (ys_.ready_sem, ys_.ready_tgt),
                                   (zs_.sem, zs_.tgt)])
                t_a = k.op("vector", lambda v, xs_=xs_, ys_=ys_, zs_=zs_: v.tensor_tensor(
                    zs_.buf[:], xs_.buf[:], ys_.buf[:], ALU.add), f.cV)
                xs_.sem, xs_.tgt = f.cV, t_a
                ys_.sem, ys_.tgt = f.cV, t_a
                k.waits("sync", [(f.cV, t_a)])
                st = k.dma("sync", x_dst[s_, oc * 128:(oc + 1) * 128, c0:c0 + 512], zs_.buf[:], f.dS)
                zs_.sem, zs_.tgt = f.dS, st
    fence_all(k)



def emit_ffn1_block(k, f, dr, x_src, out_dst):
    ISEC = 512
    NSEC = I_FFN // ISEC
    with scope(k):
        ln = make_ln(k, f, "fln")
        xn = Slot(k.sbuf("fxn", [128, DC, S], BF16))

        def dump(seq):
            def extra(tt, c0, tln, dep):
                k.waits("sync", dep)
                k.dma("sync", dr["xnT"][:].rearrange("(c p) t -> p c t", p=128)
                      [:, :, seq * S + c0: seq * S + c0 + tln],
                      xn.buf[:, :, c0:c0 + tln], f.dS)
            return extra

        for s_ in range(BC):
            k.waits("vector", [(xn.sem, xn.tgt)])
            emit_ln_seq(k, f, ln, x_src, xn.buf, s_, extra=dump(s_))
            xn.sem, xn.tgt = f.dS, f.dS.n
        xnT_done = (f.dS, f.dS.n)
    fence_all(k)

    with scope(k):
        xnh = Slot(k.sbuf("fxnh", [128, DC, S], BF16))
        wgb = Rot([k.sbuf(f"fwg{i}", [128, DC, ISEC], BF16) for i in range(2)])
        wub = Rot([k.sbuf(f"fwu{i}", [128, DC, ISEC], BF16) for i in range(2)])
        wdb = Rot([k.sbuf(f"fwd{i}", [128, ISEC // 128, D], BF16) for i in range(1)])
        hspT = k.sbuf("fhspT", [128, ISEC // 128, S], BF16)
        psgr = Rot([k.psum(f"fpsg{i}", [128, 512], F32) for i in range(2)])
        psur = Rot([k.psum(f"fpsu{i}", [128, 512], F32) for i in range(2)])
        pst = Rot([k.psum(f"fpst{i}", [128, 1024], BF16) for i in range(2)])
        psd = Rot([k.psum(f"fpsd{i}", [128, 512], F32) for i in range(2)])
        gs = Rot([k.sbuf(f"fgs{i}", [128, 512], BF16) for i in range(2)])
        hf = Rot([k.sbuf(f"fhf{i}", [128, 512], F32) for i in range(2)])
        ta = k.sbuf("fta", [128, 512], F32)
        tb = k.sbuf("ftb", [128, 512], F32)
        hm = Rot([k.sbuf(f"fhm{i}", [128, 512], BF16) for i in range(3)])
        dsb = Rot([k.sbuf(f"fdsb{i}", [128, S], F32) for i in range(1)])

        prev_acc = 0
        for half in range(2):
            th0 = half * S
            k.waits("sync", [(xnh.sem, xnh.tgt), xnT_done])
            xnh.ready_sem, xnh.ready_tgt = f.dL, k.dma(
                "sync", xnh.buf[:],
                dr["xnT"][:].rearrange("(c p) t -> p c t", p=128)[:, :, th0:th0 + S], f.dL)
            for isec in range(NSEC):
                g_sl, u_sl, d_sl = wgb.next(), wub.next(), wdb.next()
                k.waits("gpsimd", [(g_sl.sem, g_sl.tgt)])
                g_sl.ready_sem, g_sl.ready_tgt = f.dW, k.dma(
                    "gpsimd", g_sl.buf[:],
                    dr["wg1"][:, isec * ISEC:(isec + 1) * ISEC].rearrange("(c p) i -> p c i", p=128), f.dW)
                k.waits("gpsimd", [(u_sl.sem, u_sl.tgt)])
                u_sl.ready_sem, u_sl.ready_tgt = f.dW, k.dma(
                    "gpsimd", u_sl.buf[:],
                    dr["wu1"][:, isec * ISEC:(isec + 1) * ISEC].rearrange("(c p) i -> p c i", p=128), f.dW)
                k.waits("gpsimd", [(d_sl.sem, d_sl.tgt)])
                d_sl.ready_sem, d_sl.ready_tgt = f.dW, k.dma(
                    "gpsimd", d_sl.buf[:],
                    dr["wd1"][isec * ISEC:(isec + 1) * ISEC, :].rearrange("(c p) o -> p c o", p=128), f.dW)

                t_tre = 0
                pend = None
                def emit_tr(tc, m2, t_hm):
                    nonlocal t_tre
                    ps2 = pst.next()
                    k.waits("tensor", [(f.cV, t_hm), (ps2.sem, ps2.tgt)])
                    def mkT(t, m2=m2, p=ps2.buf):
                        for ic in range(4):
                            inst = t.transpose(p[:, ic * 128:(ic + 1) * 128],
                                               m2.buf[:, ic * 128:(ic + 1) * 128], f.identb[:])
                        return inst
                    t_tr = k.op("tensor", mkT, f.cT)
                    m2.sem, m2.tgt = f.cT, t_tr
                    k.waits("scalar", [(f.cT, t_tr)])
                    t_tre = k.op("scalar", lambda s, p=ps2.buf, tc=tc: s.activation(
                        hspT[:, :, tc * 128:(tc + 1) * 128], p[:, 0:512], AF.Copy), f.cS)
                    ps2.sem, ps2.tgt = f.cS, t_tre
                for tc in range(S // 128):
                    psg, psu = psgr.next(), psur.next()
                    k.waits("tensor", [(g_sl.ready_sem, g_sl.ready_tgt), (u_sl.ready_sem, u_sl.ready_tgt),
                                       (psg.sem, psg.tgt), (psu.sem, psu.tgt),
                                       (xnh.ready_sem, xnh.ready_tgt)])
                    def mkgu(t, g=g_sl.buf, u=u_sl.buf, tc=tc, psg=psg, psu=psu):
                        for c in range(DC):
                            t.matmul(psg.buf[:], xnh.buf[:, c, tc * 128:(tc + 1) * 128], g[:, c, :],
                                     start=(c == 0), stop=(c == DC - 1))
                        for c in range(DC):
                            inst = t.matmul(psu.buf[:], xnh.buf[:, c, tc * 128:(tc + 1) * 128], u[:, c, :],
                                            start=(c == 0), stop=(c == DC - 1))
                        return inst
                    t_mm = k.op("tensor", mkgu, f.cT)
                    g2, h2, m2 = gs.next(), hf.next(), hm.next()
                    k.waits("scalar", [(f.cT, t_mm), (g2.sem, g2.tgt)])
                    t_si = k.op("scalar", lambda s, g2=g2, psg=psg: s.activation(g2.buf[:], psg.buf[:], AF.Silu), f.cS)
                    psg.sem, psg.tgt = f.cS, t_si
                    k.waits("vector", [(f.cS, t_si), (h2.sem, h2.tgt), (m2.sem, m2.tgt)])
                    t_h = k.chain("vector", lambda v, g2=g2, h2=h2, psu=psu: v.tensor_tensor(
                        h2.buf[:], g2.buf[:], psu.buf[:], ALU.mult), f.cV)
                    psu.sem, psu.tgt = f.cV, t_h
                    g2.sem, g2.tgt = f.cV, t_h
                    hb = h2.buf
                    A = [ta[:, i:512:4] for i in range(4)]
                    k.chain("vector", lambda v, hb=hb: v.tensor_scalar(ta[:], hb[:], -1.0, None, ALU.mult), f.cV)
                    k.chain("vector", lambda v, hb=hb: v.tensor_tensor(ta[:], hb[:], ta[:], ALU.max), f.cV)
                    k.chain("vector", lambda v: v.tensor_tensor(tb[:, 0:128], A[0], A[1], ALU.max), f.cV)
                    k.chain("vector", lambda v: v.tensor_tensor(tb[:, 128:256], A[2], A[3], ALU.max), f.cV)
                    k.chain("vector", lambda v: v.tensor_tensor(tb[:, 256:384], A[0], A[1], ALU.min), f.cV)
                    k.chain("vector", lambda v: v.tensor_tensor(tb[:, 384:512], A[2], A[3], ALU.min), f.cV)
                    k.chain("vector", lambda v: v.tensor_tensor(tb[:, 0:128], tb[:, 0:128], tb[:, 128:256], ALU.min), f.cV)
                    k.chain("vector", lambda v: v.tensor_tensor(tb[:, 256:384], tb[:, 256:384], tb[:, 384:512], ALU.max), f.cV)
                    k.chain("vector", lambda v: v.tensor_tensor(tb[:, 0:128], tb[:, 0:128], tb[:, 256:384], ALU.max), f.cV)
                    for i in range(4):
                        k.chain("vector", lambda v, i=i: v.tensor_tensor(A[i], A[i], tb[:, 0:128], ALU.is_ge), f.cV)
                    t_hm = k.chain("vector", lambda v, hb=hb, m2=m2: v.tensor_tensor(
                        m2.buf[:], hb[:], ta[:], ALU.mult), f.cV)
                    h2.sem, h2.tgt = f.cV, t_hm
                    if pend is not None:
                        emit_tr(*pend)
                    pend = (tc, m2, t_hm)
                emit_tr(*pend)
                hsp_ready = (f.cS, t_tre)

                k.waits("gpsimd", [(f.dA, prev_acc)])
                for oc in range(DC):
                    ds = dsb.next()
                    t_de = 0
                    for ts2 in range(4):
                        pd = psd.next()
                        k.waits("tensor", [(d_sl.ready_sem, d_sl.ready_tgt), (pd.sem, pd.tgt),
                                           hsp_ready])
                        def mkd(t, d=d_sl.buf, oc=oc, ts2=ts2, pd=pd):
                            for ic in range(4):
                                inst = t.matmul(pd.buf[:], d[:, ic, oc * 128:(oc + 1) * 128],
                                                hspT[:, ic, ts2 * 512:(ts2 + 1) * 512],
                                                start=(ic == 0), stop=(ic == 3))
                            return inst
                        t_dm = k.op("tensor", mkd, f.cT)
                        k.waits("scalar", [(f.cT, t_dm)] + ([(ds.sem, ds.tgt)] if ts2 == 0 else []))
                        t_de = k.op("scalar", lambda s, pd=pd, ds=ds, ts2=ts2: s.activation(
                            ds.buf[:, ts2 * 512:(ts2 + 1) * 512], pd.buf[:], AF.Copy), f.cS)
                        pd.sem, pd.tgt = f.cS, t_de
                    k.waits("gpsimd", [(f.cS, t_de)])
                    acc = {} if isec == 0 else dict(accum_op=ALU.add)
                    st = k.dma("gpsimd", dr["yacc"][oc * 128:(oc + 1) * 128, th0:th0 + S],
                               ds.buf[:], f.dA, **acc)
                    ds.sem, ds.tgt = f.dA, st
                prev_acc = f.dA.n
                d_sl.sem, d_sl.tgt = f.cT, f.cT.n
                g_sl.sem, g_sl.tgt = f.cT, f.cT.n
                u_sl.sem, u_sl.tgt = f.cT, f.cT.n
            xnh.sem, xnh.tgt = f.cT, f.cT.n
    fence_all(k)

    with scope(k):
        xi = Rot([k.sbuf(f"oxi{i}", [128, 512], F32) for i in range(3)])
        yi = Rot([k.sbuf(f"oyi{i}", [128, 512], F32) for i in range(3)])
        zo = Rot([k.sbuf(f"ozo{i}", [128, 512], F32) for i in range(3)])
        for oc in range(DC):
            for tt in range(TT_ // 512):
                s_ = (tt * 512) // S
                c0 = (tt * 512) % S
                xs_, ys_, zs_ = xi.next(), yi.next(), zo.next()
                k.waits("sync", [(xs_.sem, xs_.tgt)])
                xs_.ready_sem, xs_.ready_tgt = f.dL, k.dma(
                    "sync", xs_.buf[:], x_src[s_, oc * 128:(oc + 1) * 128, c0:c0 + 512], f.dL)
                k.waits("sync", [(ys_.sem, ys_.tgt)])
                ys_.ready_sem, ys_.ready_tgt = f.dL, k.dma(
                    "sync", ys_.buf[:], dr["yacc"][oc * 128:(oc + 1) * 128, tt * 512:tt * 512 + 512], f.dL)
                k.waits("vector", [(xs_.ready_sem, xs_.ready_tgt), (ys_.ready_sem, ys_.ready_tgt),
                                   (zs_.sem, zs_.tgt)])
                t_a = k.op("vector", lambda v, xs_=xs_, ys_=ys_, zs_=zs_: v.tensor_tensor(
                    zs_.buf[:], xs_.buf[:], ys_.buf[:], ALU.add), f.cV)
                xs_.sem, xs_.tgt = f.cV, t_a
                ys_.sem, ys_.tgt = f.cV, t_a
                k.waits("sync", [(f.cV, t_a)])
                st = k.dma("sync", out_dst[s_, oc * 128:(oc + 1) * 128, c0:c0 + 512], zs_.buf[:], f.dS)
                zs_.sem, zs_.tgt = f.dS, st
    fence_all(k)


def build_program(upto=8, debug_out=()):
    nc = bass.Bass("TRN2", target_bir_lowering=False)
    dr = declare_drams(nc, upto, debug_out)
    with ExitStack() as ctx:
        k = K(nc, ctx)
        f = setup_frame(k, dr)
        for eng in ENGINES:
            k.waits(eng, f.consts_ready)

        emit_attn_block(k, f, dr, 0, dr["x_t"], dr["xa"])
        if upto >= 4:
            emit_moe_block(k, f, dr, dr["xa"], dr["xb_"])
        if upto >= 5:
            emit_attn_block(k, f, dr, 1, dr["xb_"], dr["xa"])
        if upto >= 8:
            emit_ffn1_block(k, f, dr, dr["xa"], dr["out_t"])

        if dr["_dbg"]:
            fence_all(k)
            for name, mirror in dr["_dbg"].items():
                k.dma("gpsimd", mirror[:], dr[name][:], f.dW)

        if f.dS.n:
            k.wait("sync", f.dS, f.dS.n)
        if f.dA.n:
            k.wait("gpsimd", f.dA, f.dA.n)
        if f.dW.n:
            k.wait("gpsimd", f.dW, f.dW.n)
        if f.dL.n:
            k.wait("sync", f.dL, f.dL.n)
        block = ctx.enter_context(nc.Block())
        k.playback(block)
    return nc


def kernel(**inputs):
    per_core = host_layout(inputs)
    nc = build_program()
    res = run_bass_kernel_spmd(nc, per_core, core_ids=list(range(N_CORES)))
    outs = []
    for c in range(N_CORES):
        ot = np.asarray(res.results[c]["out_t"])
        outs.append(ot.transpose(0, 2, 1))
    return _c(np.concatenate(outs, axis=0)).astype(np.float32)
